# version 09
# baseline (speedup 1.0000x reference)
import os
import numpy as np
import ml_dtypes
from contextlib import ExitStack

import concourse.bass as bass
import concourse.mybir as mybir
from concourse.bass_utils import run_bass_kernel_spmd

F32 = mybir.dt.float32
BF16 = mybir.dt.bfloat16
AX = mybir.AluOpType
AF = mybir.ActivationFunctionType
AXX = mybir.AxisListType.X

N_CORES = 8
D = 1024
SEQ = 16384
T = 2048
HALO = 256
SUB = 512
NSUB = T // SUB
HTOK = SUB + HALO
NH, HD = 16, 64
LHN = 8
NLAT, CHK = 64, 256
EE = 8
NQT = T // 128
QPS = SUB // 128

ENGS = ("tensor", "vector", "scalar", "gpsimd", "sync")
PSEM = {"tensor": "t", "vector": "v", "scalar": "s", "gpsimd": "g", "sync": "y"}


class Plan:
    def __init__(self):
        self.ops = {e: [] for e in ENGS}
        self.cnt = {}
        self.lastwait = {e: {} for e in ENGS}

    def sem(self, name):
        self.cnt[name] = 0

    def bump(self, name, n=1):
        self.cnt[name] += n
        return self.cnt[name]

    def val(self, name):
        return self.cnt[name]


def build_nc():
    nc = bass.Bass("TRN2", target_bir_lowering=False)
    plan = Plan()
    for s in ("t", "v", "s", "g", "y", "cc"):
        plan.sem(s)

    di = {}
    def inp(name, shape, dt=F32):
        di[name] = nc.dram_tensor(name, list(shape), dt, kind="ExternalInput")
        return di[name]

    inp("xfm", [D, HALO + T])
    inp("xtm", [T, D])
    inp("wq", [D, D]); inp("wk", [D, D]); inp("wv", [D, D])
    inp("wgq", [D, 512]); inp("wgk", [D, 512]); inp("wgv", [D, 512])
    inp("rout", [512, D]); inp("wout", [D, D])
    inp("ewT", [EE, D, D]); inp("eb", [EE, D])
    inp("gwT", [D, EE]); inp("gbrep", [128, EE])
    inp("gstat", [128, EE, 4])
    inp("masknm", [128, 1024], BF16)
    inp("maskb0", [128, 1024], BF16)
    inp("lmask", [64, T], BF16)
    inp("ident", [128, 128], BF16)
    inp("identf", [128, 128], F32)
    inp("sel16", [16, EE * 128], BF16)
    inp("sel8", [8, 4 * 128], BF16)

    y_out = nc.dram_tensor("y", [T, D], F32, kind="ExternalOutput")
    x2scr = nc.dram_tensor("x2scr", [T, D], F32)
    agin = nc.dram_tensor("agin", [D, 8], F32)
    agout = nc.dram_tensor("agout", [N_CORES, D, 8], F32)
    scrA = nc.dram_tensor("scrA", [65, NH, 2, 256], BF16)
    scrB = nc.dram_tensor("scrB", [65, LHN, SUB], BF16)
    scrC = nc.dram_tensor("scrC", [2, 8, 2, SUB], BF16)

    KDBG = bool(os.environ.get("KDBG"))
    dbg = {}
    if KDBG:
        for nm, shape, dt in (
            ("d_q", [128, 8, SUB], BF16), ("d_k", [128, 8, HTOK], BF16),
            ("d_v", [128, 6, NH * 65], BF16), ("d_gq", [128, 4, SUB], BF16),
            ("d_pool", [128, 8, NLAT], F32), ("d_gk", [128, 4, NLAT], BF16),
            ("d_gvp", [64, LHN * 65], BF16),
            ("d_pvfm", [128, 8, SUB], BF16), ("d_denl", [16, SUB], BF16),
            ("d_remstk", [128, 4, SUB], BF16), ("d_denr", [8, SUB], BF16),
            ("d_gll", [16, SUB], F32), ("d_glr", [16, SUB], F32),
            ("d_mixed", [128, 8, SUB], BF16),
            ("d_x2", [T, D], F32),
            ("d_x2pre", [128, D], F32),
            ("d_stats", [8, 128, 1], F32),
            ("d_w", [128, NQT, EE], F32),
            ("d_x2fm", [128, 8, T], BF16),
            ("d_acc", [128, NQT, D], F32),
        ):
            dbg[nm] = nc.dram_tensor(nm, list(shape), dt, kind="ExternalOutput")

    stack = ExitStack()
    sems = {}
    for s in ("t", "v", "s", "g", "y", "cc"):
        sems[s] = stack.enter_context(nc.semaphore("sem_" + s))

    def wt(eng, sem, v=None):
        if v is None:
            v = plan.val(sem)
        if v <= 0:
            return
        lw = plan.lastwait[eng]
        if lw.get(sem, -1) >= v:
            return
        lw[sem] = v
        plan.ops[eng].append(lambda e, S, s_=sem, vv=v: e.wait_ge(S[s_], vv))

    def op(eng, fn, inc=None, n=1):
        if inc is None:
            plan.ops[eng].append(lambda e, S, f=fn: f(e, S))
        else:
            plan.bump(inc, n)
            plan.ops[eng].append(lambda e, S, f=fn, s_=inc, k=n: f(e, S).then_inc(S[s_], k))

    def dma(queue, out_ap, in_ap):
        op(queue, lambda e, S, o=out_ap, i=in_ap: e.dma_start(out=o, in_=i),
           inc=PSEM[queue], n=16)

    pbh = [stack.enter_context(nc.psum_tensor(f"pb{i}", [128, 512], F32)) for i in range(8)]

    class Banks:
        def __init__(self):
            self.freemark = [None] * 8
            self.rr = 0

        def acquire(self):
            b = self.rr % 8
            self.rr += 1
            fm = self.freemark[b]
            if fm is not None:
                wt("tensor", fm[0], fm[1])
            return b

        def release(self, b, eng):
            self.freemark[b] = (PSEM[eng], plan.val(PSEM[eng]))

    banks = Banks()

    sb = {}
    def sbuf_on(st, name, shape, dt):
        sb[name] = st.enter_context(nc.sbuf_tensor("s_" + name, list(shape), dt))
        return sb[name]

    for name, shape, dt in (
        ("masknm", [128, 1024], BF16), ("maskb0", [128, 1024], BF16),
        ("lmask", [64, T], BF16), ("ident", [128, 128], BF16),
        ("identf", [128, 128], F32),
        ("sel16", [16, EE * 128], BF16), ("sel8", [8, 4 * 128], BF16),
        ("gstat", [128, EE * 4], BF16), ("gwT", [128, 8, EE], BF16),
        ("gbrep", [128, EE], F32), ("ebsb", [8, D], BF16),
        ("pooled", [128, 8, NLAT], F32), ("pooledbf", [128, 8, NLAT], BF16),
        ("gksb", [128, 4, NLAT], BF16), ("gvp", [64, LHN * 65], BF16),
        ("junk", [128, D], BF16), ("ssum", [128, 1], F32), ("ssq", [128, 1], F32),
        ("mu", [128, 1], F32), ("msq", [128, 1], F32), ("mu2", [128, 1], F32),
        ("var", [128, 1], F32), ("lnv", [128, 1], F32), ("aa", [128, 1], F32),
        ("bb", [128, 1], F32), ("nmu", [128, 1], F32),
        ("spc0", [128, 1], F32), ("spc1", [128, 1], F32),
    ):
        sbuf_on(stack, name, shape, dt)

    A = ExitStack()
    for name, shape, dt in (
        ("xfs", [128, 2, 8, HTOK], BF16),
        ("wring", [128, 2, 8, 1024], BF16),
        ("qsb", [128, 8, SUB], BF16),
        ("ksb", [128, 8, HTOK], BF16),
        ("vsb", [128, 6, NH * 65], BF16),
        ("gqsb", [128, 4, SUB], BF16),
        ("esb", [128, 2, 2, 4, 256], BF16),
        ("ersb", [64, 2, SUB], BF16),
        ("pvs", [65, NH, 2, 256], BF16),
        ("pvfm", [128, 8, SUB], BF16),
        ("denl", [16, SUB], BF16),
        ("rems", [65, LHN, SUB], BF16),
        ("remstk", [128, 4, SUB], BF16),
        ("denr", [8, SUB], BF16),
        ("glr_l", [16, SUB], F32),
        ("glr_r", [16, SUB], F32),
        ("recl", [16, SUB], F32),
        ("clbf", [16, SUB], BF16),
        ("crbf", [16, SUB], BF16),
        ("recrbf", [8, SUB], BF16),
        ("remnorm", [128, 4, SUB], BF16),
        ("crsb", [128, SUB], BF16),
        ("tmpv", [128, SUB], BF16),
        ("mixed", [128, 8, SUB], BF16),
        ("xtmb", [128, 2, D], F32),
        ("x2pre", [128, D], F32),
        ("x2tm", [128, 1, D], F32),
        ("gvtmp", [128, 4, NLAT], BF16),
    ):
        sbuf_on(A, name, shape, dt)

    dma("sync", sb["masknm"][:, :], di["masknm"].ap()[:, :])
    dma("sync", sb["maskb0"][:, :], di["maskb0"].ap()[:, :])
    dma("sync", sb["lmask"][:, :], di["lmask"].ap()[:, :])
    dma("sync", sb["ident"][:, :], di["ident"].ap()[:, :])
    dma("sync", sb["identf"][:, :], di["identf"].ap()[:, :])
    dma("sync", sb["sel16"][:, :], di["sel16"].ap()[:, :])
    dma("sync", sb["sel8"][:, :], di["sel8"].ap()[:, :])
    dma("sync", sb["gbrep"][:, :], di["gbrep"].ap()[:, :])
    dma("gpsimd", sb["gstat"][:, :], di["gstat"].ap().rearrange("p e f -> p (e f)"))
    dma("gpsimd", sb["gwT"][:, :, :], di["gwT"].ap().rearrange("(c p) e -> p c e", p=128))
    dma("gpsimd", sb["ebsb"][:, :], di["eb"].ap()[:, :])

    op("vector", lambda e, S: e.memset(sb["spc0"][:, :], 0.0), inc="v")
    op("vector", lambda e, S: e.memset(sb["spc1"][:, :], 0.0), inc="v")

    def vspace():
        op("vector", lambda e, S: e.tensor_copy(sb["spc1"][:, :], sb["spc0"][:, :]), inc="v")

    for j in range(NSUB):
        if j >= 1:
            wt("gpsimd", "v")
        dma("gpsimd", sb["xfs"][:, 1, :, 0:SUB],
            di["xfm"].ap().rearrange("(c p) t -> p c t", p=128)[:, :, HALO + j * SUB: HALO + (j + 1) * SUB])
        gval = plan.val("g")
        wt("vector", "g", gval)
        for c in range(8):
            op("vector",
               lambda e, S, cc=c, j_=j: e.reduce_sum(
                   sb["pooled"][:, cc, j_ * 2:(j_ + 1) * 2],
                   sb["xfs"][:, 1, cc, 0:SUB].rearrange("p (s q) -> p s q", q=256),
                   AXX),
               inc="v")
    vspace()
    op("vector", lambda e, S: e.tensor_scalar(
        sb["pooled"][:, :, 0:8],
        sb["pooled"][:, :, 0:8],
        1.0 / 256.0, None, AX.mult), inc="v")
    pool_reads_done = plan.val("v")
    wt("sync", "v")
    dma("sync", agin.ap().rearrange("(c p) n -> p c n", p=128), sb["pooled"][:, :, 0:8])
    wt("gpsimd", "y")
    op("gpsimd", lambda e, S: e.collective_compute(
        "AllGather", AX.bypass,
        replica_groups=[list(range(N_CORES))],
        ins=[agin.ap().opt()],
        outs=[agout.ap().opt()]), inc="cc")
    wt("sync", "cc")
    for r_ in range(N_CORES):
        dma("sync", sb["pooled"][:, :, 8 * r_: 8 * r_ + 8],
            agout.ap()[r_, :, :].rearrange("(c p) n -> p c n", p=128))
    agload_done = plan.val("y")

    wring_lastuse = [0, 0]
    wring_next = 0

    def load_wring(name, chunks, width):
        nonlocal wring_next
        slot = wring_next % 2
        wring_next += 1
        if wring_lastuse[slot] > 0:
            wt("gpsimd", "t", wring_lastuse[slot])
        dma("gpsimd", sb["wring"][:, slot, 0:chunks, 0:width],
            di[name].ap().rearrange("(c p) o -> p c o", p=128))
        return slot, plan.val("g")

    xfs_lastuse = [None, None]
    xtmb_lastuse = [None, None]
    x2tm_lastuse = [None, None]
    marks = {}
    dump_marks = {}
    load_marks = {}
    spill_marks = []
    wslots = {}

    def load_xfs(s):
        slot = s % 2
        if s == 0:
            pass
        if slot == 1:
            wt("gpsimd", "v", pool_reads_done)
        if xfs_lastuse[slot] is not None:
            wt("gpsimd", "t", xfs_lastuse[slot])
        dma("gpsimd", sb["xfs"][:, slot, :, :],
            di["xfm"].ap().rearrange("(c p) t -> p c t", p=128)[:, :, s * SUB: s * SUB + HTOK])
        return plan.val("g")

    evac_rr = [0]

    def evac_copy(b, out_ap_fn):
        tv = plan.val("t")
        eng = "vector" if (evac_rr[0] % 2 == 0) else "scalar"
        evac_rr[0] += 1
        wt(eng, "t", tv)
        if eng == "vector":
            op(eng, lambda e, S, b_=b, f=out_ap_fn: e.tensor_copy(*f(b_)), inc="v")
        else:
            op(eng, lambda e, S, b_=b, f=out_ap_fn: e.copy(*f(b_)), inc="s")
        banks.release(b, eng)

    def emit_proj(s, xready):
        slot = s % 2

        def proj_fm(dstn, wslot, ocs, own_only, nlist):
            for oc in ocs:
                for (a, c) in nlist:
                    b = banks.acquire()
                    for kc in range(8):
                        op("tensor", lambda e, S, b_=b, kc_=kc, oc_=oc, ws=wslot, sl=slot,
                           a_=a, c_=c, oo=own_only: e.matmul(
                               pbh[b_][:, 0:c_ - a_],
                               sb["wring"][:, ws, kc_, oc_ * 128: oc_ * 128 + 128],
                               (sb["xfs"][:, sl, kc_, HALO:HTOK] if oo
                                else sb["xfs"][:, sl, kc_, :])[:, a_:c_],
                               start=(kc_ == 0), stop=(kc_ == 7)),
                           inc=("t" if kc == 7 else None))
                    evac_copy(b, lambda b_, oc_=oc, a_=a, c_=c, d=dstn:
                              (sb[d][:, oc_, a_:c_], pbh[b_][:, 0:c_ - a_]))

        wq_slot, wq_rdy = load_wring("wq", 8, 1024)
        wk_slot, wk_rdy = load_wring("wk", 8, 1024)
        wt("tensor", "g", max(xready, wq_rdy))
        proj_fm("qsb", wq_slot, range(8), True, [(0, 512)])
        wring_lastuse[wq_slot] = plan.val("t")
        wv_slot, wv_rdy = load_wring("wv", 8, 1024)
        wt("tensor", "g", wk_rdy)
        proj_fm("ksb", wk_slot, range(8), False, [(0, 512), (512, 768)])
        wring_lastuse[wk_slot] = plan.val("t")
        wgq_slot, wgq_rdy = load_wring("wgq", 8, 512)
        wt("tensor", "g", wv_rdy)
        for tc in range(6):
            for half in range(2):
                b = banks.acquire()
                for kc in range(8):
                    op("tensor", lambda e, S, b_=b, kc_=kc, tc_=tc, h_=half, sl=slot,
                       ws=wv_slot: e.matmul(
                        pbh[b_][:, 0:512],
                        sb["xfs"][:, sl, kc_, tc_ * 128: tc_ * 128 + 128],
                        sb["wring"][:, ws, kc_, h_ * 512: h_ * 512 + 512],
                        start=(kc_ == 0), stop=(kc_ == 7)),
                       inc=("t" if kc == 7 else None))
                evac_copy(b, lambda b_, tc_=tc, h_=half:
                          (sb["vsb"].ap().rearrange("p t (h c) -> p t h c", c=65)[:, tc_, h_ * 8:h_ * 8 + 8, 0:64],
                           pbh[b_][:, 0:512].rearrange("p (h d) -> p h d", d=64)))
        wring_lastuse[wv_slot] = plan.val("t")
        if s == 0:
            op("vector", lambda e, S: e.memset(
                sb["vsb"].ap().rearrange("p t (h c) -> p t h c", c=65)[:, :, :, 64:65], 1.0), inc="v")
        wt("tensor", "g", wgq_rdy)
        proj_fm("gqsb", wgq_slot, range(4), True, [(0, 512)])
        wring_lastuse[wgq_slot] = plan.val("t")
        xfs_lastuse[slot] = plan.val("t")
        marks[("pev", s)] = (plan.val("v"), plan.val("s"))
        if s == 0:
            wslots["wgk"] = load_wring("wgk", 8, 512)
            wslots["wgv"] = load_wring("wgv", 8, 512)

        if KDBG and s == 0:
            wt("sync", "v")
            dma("sync", dbg["d_q"].ap()[:, :, :], sb["qsb"][:, :, :])
            dma("sync", dbg["d_k"].ap()[:, :, :], sb["ksb"][:, :, :])
            dma("sync", dbg["d_v"].ap()[:, :, :], sb["vsb"][:, :, :])
            dma("sync", dbg["d_gq"].ap()[:, :, :], sb["gqsb"][:, :, :])

    def emit_gkgv():
        ws_gk = wslots["wgk"]
        ws_gv = wslots["wgv"]
        wt("vector", "y", agload_done)
        op("vector", lambda e, S: e.tensor_copy(
            sb["pooledbf"].ap().rearrange("p c n -> p (c n)"),
            sb["pooled"].ap().rearrange("p c n -> p (c n)")), inc="v")
        wt("tensor", "g", ws_gv[1])
        wt("tensor", "v")
        for dstn, wslot in (("gksb", ws_gk[0]), ("gvtmp", ws_gv[0])):
            for oc in range(4):
                b = banks.acquire()
                for kc in range(8):
                    op("tensor", lambda e, S, b_=b, kc_=kc, oc_=oc, ws=wslot: e.matmul(
                        pbh[b_][:, 0:NLAT],
                        sb["wring"][:, ws, kc_, oc_ * 128: oc_ * 128 + 128],
                        sb["pooledbf"][:, kc_, :],
                        start=(kc_ == 0), stop=(kc_ == 7)),
                       inc=("t" if kc == 7 else None))
                tv = plan.val("t")
                wt("vector", "t", tv)
                op("vector", lambda e, S, b_=b, oc_=oc, d=dstn: e.tensor_copy(
                    sb[d][:, oc_, :], pbh[b_][:, 0:NLAT]), inc="v")
                banks.release(b, "vector")
        wring_lastuse[wslots["wgk"][0]] = plan.val("t")
        wring_lastuse[wslots["wgv"][0]] = plan.val("t")
        op("vector", lambda e, S: e.memset(
            sb["gvp"].ap().rearrange("n (h c) -> n h c", c=65)[:, :, 64:65], 1.0), inc="v")
        for pt in range(4):
            b = banks.acquire()
            wt("tensor", "v")
            op("tensor", lambda e, S, b_=b, pt_=pt: e.transpose(
                pbh[b_].bitcast(BF16)[0:64, 0:128], sb["gvtmp"][:, pt_, :], sb["ident"][:, :]), inc="t")
            tv = plan.val("t")
            wt("vector", "t", tv)
            op("vector", lambda e, S, b_=b, pt_=pt: e.tensor_copy(
                sb["gvp"].ap().rearrange("n (h c) -> n h c", c=65)[0:64, 2 * pt_:2 * pt_ + 2, 0:64],
                pbh[b_].bitcast(BF16)[0:64, 0:128].rearrange("n (h d) -> n h d", d=64)), inc="v")
            banks.release(b, "vector")
        if KDBG:
            wt("sync", "v")
            dma("sync", dbg["d_pool"].ap()[:, :, :], sb["pooled"][:, :, :])
            dma("sync", dbg["d_gk"].ap()[:, :, :], sb["gksb"][:, :, :])
            dma("sync", dbg["d_gvp"].ap()[:, :], sb["gvp"][:, :])

    def emit_remote(s):
        if (s - 1, "C") in dump_marks:
            wt("vector", "y", dump_marks[(s - 1, "C")])
        wt("tensor", "v", marks[("pev", s)][0])
        wt("tensor", "s", marks[("pev", s)][1])
        for lh in range(LHN):
            es = lh % 2
            base = 64 * (lh % 2)
            b = banks.acquire()
            op("tensor", lambda e, S, b_=b, lh_=lh, ba=base: e.matmul(
                pbh[b_][0:64, 0:512],
                sb["gksb"][ba:ba + 64, lh_ // 2, :],
                sb["gqsb"][ba:ba + 64, lh_ // 2, :],
                start=True, stop=True), inc="t")
            tv = plan.val("t")
            wt("scalar", "t", tv)
            op("scalar", lambda e, S, b_=b, es_=es: e.activation(
                sb["ersb"][0:64, es_, :], pbh[b_][0:64, 0:512], AF.Exp, scale=0.125), inc="s")
            banks.release(b, "scalar")
            sv = plan.val("s")
            wt("gpsimd", "s", sv)
            op("gpsimd", lambda e, S, es_=es, si=s: e.tensor_tensor(
                sb["ersb"][0:64, es_, :], sb["ersb"][0:64, es_, :],
                sb["lmask"][0:64, si * SUB:(si + 1) * SUB], AX.mult), inc="g", n=1)
            gv_ = plan.val("g")
            wt("tensor", "g", gv_)
            b2 = banks.acquire()
            op("tensor", lambda e, S, b_=b2, lh_=lh, es_=es: e.matmul(
                pbh[b_][0:65, 0:512],
                sb["gvp"][0:64, lh_ * 65: lh_ * 65 + 65],
                sb["ersb"][0:64, es_, :],
                start=True, stop=True), inc="t")
            tv = plan.val("t")
            wt("vector", "t", tv)
            op("vector", lambda e, S, b_=b2, lh_=lh: e.tensor_copy(
                sb["rems"][:, lh_, :], pbh[b_][0:65, 0:512]), inc="v")
            banks.release(b2, "vector")
        rem_ev = plan.val("v")
        if (s - 1,) in load_marks:
            wt("sync", "y", load_marks[(s - 1,)])
        wt("sync", "v", rem_ev)
        dma("sync", scrB.ap()[:, :, :], sb["rems"][:, :, :])
        dump_marks[(s, "B")] = plan.val("y")

    def emit_local(s):
        if (s - 1, "A") in dump_marks:
            wt("vector", "y", dump_marks[(s - 1, "A")])
        wt("tensor", "v", marks[("pev", s)][0])
        wt("tensor", "s", marks[("pev", s)][1])
        for hp in range(8):
            for blk in range(2):
                unit = hp * 2 + blk
                es = unit % 2
                sc_banks = []
                for hi in range(2):
                    h = hp * 2 + hi
                    base = 64 * (h % 2)
                    for kc in range(4):
                        b = banks.acquire()
                        sc_banks.append(b)
                        kt = 2 * blk + kc
                        op("tensor", lambda e, S, b_=b, h_=h, ba=base, kt_=kt, bl=blk: e.matmul(
                            pbh[b_][:, 0:256],
                            sb["ksb"][ba:ba + 64, h_ // 2, kt_ * 128: kt_ * 128 + 128],
                            sb["qsb"][ba:ba + 64, h_ // 2, bl * 256: bl * 256 + 256],
                            start=True, stop=True),
                           inc=("t" if (hi == 1 and kc == 3) else None))
                tv = plan.val("t")
                wt("scalar", "t", tv)
                for i8, b in enumerate(sc_banks):
                    hi, kc = divmod(i8, 4)
                    op("scalar", lambda e, S, b_=b, es_=es, hi_=hi, kc_=kc: e.activation(
                        sb["esb"][:, es_, hi_, kc_, :], pbh[b_][:, 0:256], AF.Exp, scale=0.125),
                       inc="s")
                    banks.release(b, "scalar")
                sv = plan.val("s")
                wt("vector", "s", sv)
                mk = "maskb0" if (s == 0 and blk == 0) else "masknm"
                for hi in range(2):
                    op("vector", lambda e, S, es_=es, hi_=hi, m=mk: e.tensor_tensor(
                        sb["esb"].ap().rearrange("p r h k q -> p r h (k q)")[:, es_, hi_, :],
                        sb["esb"].ap().rearrange("p r h k q -> p r h (k q)")[:, es_, hi_, :],
                        sb[m][:, :], AX.mult), inc="v")
                vv = plan.val("v")
                wt("tensor", "v", vv)
                for hi in range(2):
                    h = hp * 2 + hi
                    b = banks.acquire()
                    for kc in range(4):
                        kt = 2 * blk + kc
                        op("tensor", lambda e, S, b_=b, h_=h, kt_=kt, es_=es, hi_=hi, kc_=kc: e.matmul(
                            pbh[b_][0:65, 0:256],
                            sb["vsb"][:, kt_, h_ * 65: h_ * 65 + 65],
                            sb["esb"][:, es_, hi_, kc_, :],
                            start=(kc_ == 0), stop=(kc_ == 3)),
                           inc=("t" if kc == 3 else None))
                    tv = plan.val("t")
                    wt("vector", "t", tv)
                    op("vector", lambda e, S, b_=b, h_=h, bl=blk: e.tensor_copy(
                        sb["pvs"][:, h_, bl, :], pbh[b_][0:65, 0:256]), inc="v")
                    banks.release(b, "vector")
        wt("sync", "v", plan.val("v"))
        dma("sync", scrA.ap()[:, :, :, :], sb["pvs"][:, :, :, :])
        dump_marks[(s, "A")] = plan.val("y")

    def emit_glog(s):
        wt("vector", "y", dump_marks[(s, "B")])
        for pt in range(8):
            b = banks.acquire()
            op("tensor", lambda e, S, b_=b, pt_=pt: e.matmul(
                pbh[b_][0:4, 0:512],
                sb["gstat"][:, pt_ * 4: pt_ * 4 + 4],
                sb["qsb"][:, pt_, :],
                start=True, stop=True), inc="t")
            tv = plan.val("t")
            wt("vector", "t", tv)
            op("vector", lambda e, S, b_=b, pt_=pt: e.tensor_copy(
                sb["rems"][0:4, pt_, :], pbh[b_][0:4, 0:512]), inc="v")
            banks.release(b, "vector")
        wt("sync", "v", plan.val("v"))
        for k in range(2):
            for hi in range(2):
                dma("sync",
                    scrC.ap()[k:k + 1, :, hi, :],
                    sb["rems"][2 * k + hi: 2 * k + hi + 1, 0:8, :])
        dump_marks[(s, "C")] = plan.val("y")

    def emit_gather_loads(s):
        wt("sync", "y", dump_marks[(s, "C")])
        if ("cmb", s - 1) in marks:
            wt("sync", "v", marks[("cmb", s - 1)])
        for hi in range(2):
            dma("sync",
                sb["pvfm"][64 * hi: 64 * hi + 64, :, :],
                scrA.ap().rearrange("d (pt hi) b q -> d pt hi (b q)", hi=2)[0:64, :, hi, :])
        dma("sync",
            sb["denl"][:, :],
            scrA.ap().rearrange("d h b q -> d h (b q)")[64, :, :])
        for hi in range(2):
            dma("sync",
                sb["remstk"][64 * hi: 64 * hi + 64, :, :],
                scrB.ap().rearrange("d (pt hi) q -> d pt hi q", hi=2)[0:64, :, hi, :])
        dma("sync",
            sb["denr"][:, :],
            scrB.ap()[64, :, :])
        wt("gpsimd", "y", dump_marks[(s, "C")])
        dma("gpsimd",
            sb["glr_l"][:, :],
            scrC.ap()[0, :, :, :].rearrange("pt hi q -> (pt hi) q"))
        dma("gpsimd",
            sb["glr_r"][:, :],
            scrC.ap()[1, :, :, :].rearrange("pt hi q -> (pt hi) q"))
        glgr_rdy = plan.val("g")
        gathered = plan.val("y")
        load_marks[(s,)] = gathered

        if KDBG and s == 0:
            wt("sync", "y", gathered)
            dma("sync", dbg["d_pvfm"].ap()[:, :, :], sb["pvfm"][:, :, :])
            dma("sync", dbg["d_denl"].ap()[:, :], sb["denl"][:, :])
            dma("sync", dbg["d_remstk"].ap()[:, :, :], sb["remstk"][:, :, :])
            dma("sync", dbg["d_denr"].ap()[:, :], sb["denr"][:, :])
        return glgr_rdy, gathered

    def emit_coeffs(s, glgr_rdy, gathered):
        wt("scalar", "g", glgr_rdy)
        wt("scalar", "y", gathered)
        for gn in ("glr_l", "glr_r"):
            op("scalar", lambda e, S, g_=gn: e.activation(sb[g_][:, :], sb[g_][:, :],
                                                          AF.Exp, scale=-1.0), inc="s")
        sv = plan.val("s")
        wt("vector", "s", sv)
        for gn in ("glr_l", "glr_r"):
            op("vector", lambda e, S, g_=gn: e.tensor_scalar(sb[g_][:, :], sb[g_][:, :],
                                                             1.0, None, AX.add), inc="v")
            op("vector", lambda e, S, g_=gn: e.reciprocal(sb[g_][:, :], sb[g_][:, :]), inc="v")
        wt("vector", "y", gathered)
        op("vector", lambda e, S: e.reciprocal(sb["recl"][:, :], sb["denl"][:, :]), inc="v")
        op("vector", lambda e, S: e.tensor_tensor(sb["clbf"][:, :], sb["glr_l"][:, :],
                                                  sb["recl"][:, :], AX.mult), inc="v")
        op("vector", lambda e, S: e.tensor_copy(sb["crbf"][:, :], sb["glr_r"][:, :]), inc="v")
        if KDBG and s == 0:
            wt("sync", "v")
            dma("sync", dbg["d_gll"].ap()[:, :], sb["glr_l"][:, :])
            dma("sync", dbg["d_glr"].ap()[:, :], sb["glr_r"][:, :])
        op("vector", lambda e, S: e.reciprocal(sb["recl"][0:8, :], sb["denr"][:, :]), inc="v")
        op("vector", lambda e, S: e.tensor_copy(sb["recrbf"][:, :], sb["recl"][0:8, :]), inc="v")
        marks[("coef", s)] = plan.val("v")

    def emit_combine(s):
        rout_slot, rout_rdy = wslots[("rout", s)]
        wt("tensor", "v", marks[("coef", s)])
        for pt in range(4):
            b = banks.acquire()
            op("tensor", lambda e, S, b_=b, pt_=pt: e.matmul(
                pbh[b_][:, 0:512],
                sb["sel8"][:, pt_ * 128: pt_ * 128 + 128],
                sb["recrbf"][:, :],
                start=True, stop=True), inc="t")
            tv = plan.val("t")
            wt("vector", "t", tv)
            op("vector", lambda e, S, b_=b, pt_=pt: e.tensor_tensor(
                sb["remnorm"][:, pt_, :], sb["remstk"][:, pt_, :],
                pbh[b_][:, 0:512], AX.mult), inc="v")
            banks.release(b, "vector")
        wt("tensor", "g", rout_rdy)
        wt("tensor", "v", plan.val("v"))
        for pt in range(8):
            bl_ = banks.acquire()
            op("tensor", lambda e, S, b_=bl_, pt_=pt: e.matmul(
                pbh[b_][:, 0:512],
                sb["sel16"][:, pt_ * 128: pt_ * 128 + 128],
                sb["clbf"][:, :],
                start=True, stop=True), inc="t")
            tvl = plan.val("t")
            bc_ = banks.acquire()
            op("tensor", lambda e, S, b_=bc_, pt_=pt: e.matmul(
                pbh[b_][:, 0:512],
                sb["sel16"][:, pt_ * 128: pt_ * 128 + 128],
                sb["crbf"][:, :],
                start=True, stop=True), inc="t")
            tvc = plan.val("t")
            br_ = banks.acquire()
            for kc in range(4):
                op("tensor", lambda e, S, b_=br_, kc_=kc, pt_=pt, rs=rout_slot: e.matmul(
                    pbh[b_][:, 0:512],
                    sb["wring"][:, rs, kc_, pt_ * 128: pt_ * 128 + 128],
                    sb["remnorm"][:, kc_, :],
                    start=(kc_ == 0), stop=(kc_ == 3)),
                   inc=("t" if kc == 3 else None))
            tvr = plan.val("t")
            wt("vector", "t", tvl)
            op("vector", lambda e, S, b_=bl_, pt_=pt: e.tensor_tensor(
                sb["mixed"][:, pt_, :], sb["pvfm"][:, pt_, :], pbh[b_][:, 0:512],
                AX.mult), inc="v")
            banks.release(bl_, "vector")
            wt("vector", "t", tvc)
            op("vector", lambda e, S, b_=bc_: e.tensor_copy(
                sb["crsb"][:, :], pbh[b_][:, 0:512]), inc="v")
            banks.release(bc_, "vector")
            wt("vector", "t", tvr)
            op("vector", lambda e, S, b_=br_: e.tensor_tensor(
                sb["tmpv"][:, :], pbh[b_][:, 0:512], sb["crsb"][:, :], AX.mult), inc="v")
            banks.release(br_, "vector")
            op("vector", lambda e, S, pt_=pt: e.tensor_tensor(
                sb["mixed"][:, pt_, :], sb["mixed"][:, pt_, :], sb["tmpv"][:, :],
                AX.add), inc="v")
        wring_lastuse[rout_slot] = plan.val("t")
        marks[("cmb", s)] = plan.val("v")
        if KDBG and s == 0:
            wt("sync", "v")
            dma("sync", dbg["d_mixed"].ap()[:, :, :], sb["mixed"][:, :, :])

    def emit_woutln1(s):
        wout_slot, wout_rdy = wslots[("wout", s)]
        wt("tensor", "g", wout_rdy)
        wt("tensor", "v", marks[("cmb", s)])
        for qt in range(QPS):
            qt_g = s * QPS + qt
            tslot = qt_g % 2
            if xtmb_lastuse[tslot] is not None:
                wt("sync", "v", xtmb_lastuse[tslot])
            dma("sync", sb["xtmb"][:, tslot, :], di["xtm"].ap()[qt_g * 128:(qt_g + 1) * 128, :])
            xtrdy = plan.val("y")

            bA = banks.acquire()
            bB = banks.acquire()
            for kc in range(8):
                for half, b in ((0, bA), (1, bB)):
                    op("tensor", lambda e, S, b_=b, kc_=kc, qt_=qt, h_=half, ws=wout_slot: e.matmul(
                        pbh[b_][:, 0:512],
                        sb["mixed"][:, kc_, qt_ * 128: qt_ * 128 + 128],
                        sb["wring"][:, ws, kc_, h_ * 512: h_ * 512 + 512],
                        start=(kc_ == 0), stop=(kc_ == 7)),
                       inc=("t" if (kc == 7 and half == 1) else None))
            tv = plan.val("t")
            wt("vector", "t", tv)
            wt("vector", "y", xtrdy)
            op("vector", lambda e, S, b_=bA, ts=tslot: e.tensor_tensor(
                sb["x2pre"][:, 0:512], sb["xtmb"][:, ts, 0:512], pbh[b_][:, 0:512], AX.add), inc="v")
            op("vector", lambda e, S, b_=bB, ts=tslot: e.tensor_tensor(
                sb["x2pre"][:, 512:1024], sb["xtmb"][:, ts, 512:1024], pbh[b_][:, 0:512], AX.add), inc="v")
            banks.release(bA, "vector")
            banks.release(bB, "vector")
            xtmb_lastuse[tslot] = plan.val("v")
            if KDBG and qt_g == 0:
                wt("sync", "v")
                dma("sync", dbg["d_x2pre"].ap()[:, :], sb["x2pre"][:, :])
            op("vector", lambda e, S: e.reduce_sum(sb["ssum"][:, :], sb["x2pre"][:, :], AXX), inc="v")
            vv = plan.val("v")
            wt("scalar", "v", vv)
            op("scalar", lambda e, S: e.activation(sb["junk"][:, :], sb["x2pre"][:, :],
                                                   AF.Square, accum_out=sb["ssq"][:, :]), inc="s")
            sv = plan.val("s")
            wt("vector", "s", sv)
            op("vector", lambda e, S: e.tensor_scalar(sb["nmu"][:, :], sb["ssum"][:, :],
                                                      -1.0 / D, None, AX.mult), inc="v")
            op("vector", lambda e, S: e.tensor_scalar(sb["msq"][:, :], sb["ssq"][:, :],
                                                      1.0 / D, None, AX.mult), inc="v")
            op("vector", lambda e, S: e.tensor_scalar(sb["mu"][:, :], sb["ssum"][:, :],
                                                      1.0 / D, None, AX.mult), inc="v")
            vspace()
            op("vector", lambda e, S: e.scalar_tensor_tensor(
                sb["var"][:, :], sb["mu"][:, :], sb["nmu"][:, 0:1], sb["msq"][:, :],
                AX.mult, AX.add), inc="v")
            vspace()
            op("vector", lambda e, S: e.tensor_scalar(sb["var"][:, :], sb["var"][:, :],
                                                      1e-5, None, AX.add), inc="v")
            vv = plan.val("v")
            wt("scalar", "v", vv)
            op("scalar", lambda e, S: e.activation(sb["lnv"][:, :], sb["var"][:, :],
                                                   AF.Ln), inc="s")
            op("scalar", lambda e, S: e.activation(sb["aa"][:, :], sb["lnv"][:, :],
                                                   AF.Exp, scale=-0.5), inc="s")
            sv = plan.val("s")
            wt("vector", "s", sv)
            op("vector", lambda e, S: e.tensor_scalar(sb["bb"][:, :], sb["mu"][:, :],
                                                      sb["aa"][:, 0:1], -1.0, AX.mult, AX.mult), inc="v")
            vspace()
            if KDBG and qt_g == 0:
                wt("sync", "s")
                wt("sync", "v")
                for i_, nm_ in enumerate(("ssum", "ssq", "mu", "msq", "mu2", "var", "lnv", "aa")):
                    dma("sync", dbg["d_stats"].ap()[i_, :, :], sb[nm_][:, :])
            if x2tm_lastuse[0] is not None:
                wt("vector", "y", x2tm_lastuse[0])
            op("vector", lambda e, S: e.tensor_scalar(
                sb["x2tm"][:, 0, :], sb["x2pre"][:, :],
                sb["aa"][:, 0:1], sb["bb"][:, 0:1], AX.mult, AX.add), inc="v")
            wt("sync", "v")
            dma("sync", x2scr.ap()[qt_g * 128:(qt_g + 1) * 128, :], sb["x2tm"][:, 0, :])
            x2tm_lastuse[0] = plan.val("y")
            spill_marks.append(plan.val("y"))
        wring_lastuse[wout_slot] = plan.val("t")

    xrdy0 = load_xfs(0)
    emit_proj(0, xrdy0)
    emit_gkgv()
    for s in range(NSUB):
        emit_remote(s)
        emit_local(s)
        emit_glog(s)
        glgr_rdy, gathered = emit_gather_loads(s)
        emit_coeffs(s, glgr_rdy, gathered)
        if s + 1 < NSUB:
            xrdy = load_xfs(s + 1)
            emit_proj(s + 1, xrdy)
        wslots[("rout", s)] = load_wring("rout", 4, 1024)
        wslots[("wout", s)] = load_wring("wout", 8, 1024)
        emit_combine(s)
        emit_woutln1(s)

    spills_done = plan.val("y")
    if KDBG:
        wt("sync", "y", spills_done)
        dma("sync", dbg["d_x2"].ap()[:, :], x2scr.ap()[:, :])

    marks = {s: plan.val(s) for s in ("t", "v", "s", "g", "y")}
    for eng in ENGS:
        for s_, v_ in marks.items():
            wt(eng, s_, v_)

    A.close()

    Bst = ExitStack()
    for name, shape, dt in (
        ("x2fm", [128, 8, T], BF16),
        ("xbf", [128, 2, D], BF16),
        ("wE", [128, 2, 8, D], BF16),
        ("acc", [128, NQT, D], F32),
        ("gfm", [8, T], F32),
        ("uu", [128, EE], F32),
        ("usum", [128, 1], F32),
        ("t1", [128, EE], F32),
        ("g2", [128, EE], F32),
        ("m1", [128, 1], F32),
        ("m2", [128, 1], F32),
        ("wtm", [128, NQT, EE], F32),
        ("wfm", [8, T], BF16),
        ("xt2b", [128, 2, D], F32),
        ("outb", [128, 2, D], F32),
        ("x3pre", [128, D], F32),
    ):
        sbuf_on(Bst, name, shape, dt)

    xbf_lastuse = [None, None]
    for qt in range(NQT):
        slot = qt % 2
        if xbf_lastuse[slot] is not None:
            wt("gpsimd", "t", xbf_lastuse[slot])
        dma("gpsimd", sb["xbf"][:, slot, :], x2scr.ap()[qt * 128:(qt + 1) * 128, :])
        xrdy = plan.val("g")
        wt("tensor", "g", xrdy)
        for fc in range(8):
            b = banks.acquire()
            op("tensor", lambda e, S, b_=b, sl=slot, fc_=fc: e.transpose(
                pbh[b_].bitcast(BF16)[:, 0:128], sb["xbf"][:, sl, fc_ * 128: fc_ * 128 + 128],
                sb["ident"][:, :]), inc="t")
            tv = plan.val("t")
            wt("vector", "t", tv)
            op("vector", lambda e, S, b_=b, fc_=fc, qt_=qt: e.tensor_copy(
                sb["x2fm"][:, fc_, qt_ * 128:(qt_ + 1) * 128], pbh[b_].bitcast(BF16)[:, 0:128]), inc="v")
            banks.release(b, "vector")
        xbf_lastuse[slot] = plan.val("t")

    wt("tensor", "v")
    for nck in range(4):
        b = banks.acquire()
        for kc in range(8):
            op("tensor", lambda e, S, b_=b, kc_=kc, n_=nck: e.matmul(
                pbh[b_][0:8, 0:512],
                sb["gwT"][:, kc_, :],
                sb["x2fm"][:, kc_, n_ * 512:(n_ + 1) * 512],
                start=(kc_ == 0), stop=(kc_ == 7)), inc=("t" if kc == 7 else None))
        tv = plan.val("t")
        wt("vector", "t", tv)
        op("vector", lambda e, S, b_=b, n_=nck: e.tensor_copy(
            sb["gfm"][:, n_ * 512:(n_ + 1) * 512], pbh[b_][0:8, 0:512]), inc="v")
        banks.release(b, "vector")
    for qt in range(NQT):
        b = banks.acquire()
        wt("tensor", "v")
        op("tensor", lambda e, S, b_=b, qt_=qt: e.transpose(
            pbh[b_][:, 0:8], sb["gfm"][0:8, qt_ * 128:(qt_ + 1) * 128],
            sb["identf"][0:8, 0:8]), inc="t")
        tv = plan.val("t")
        wt("vector", "t", tv)
        op("vector", lambda e, S, b_=b: e.tensor_tensor(
            sb["uu"][:, :], pbh[b_][:, 0:8], sb["gbrep"][:, :], AX.add), inc="v")
        banks.release(b, "vector")
        vv = plan.val("v")
        wt("scalar", "v", vv)
        op("scalar", lambda e, S: e.activation(sb["uu"][:, :], sb["uu"][:, :],
                                               AF.Exp, accum_out=sb["usum"][:, :]), inc="s")
        sv = plan.val("s")
        wt("vector", "s", sv)
        op("vector", lambda e, S: e.reduce_max(sb["m1"][:, :], sb["uu"][:, :], AXX), inc="v")
        vspace()
        op("vector", lambda e, S: e.tensor_scalar(sb["t1"][:, :], sb["uu"][:, :],
                                                  sb["m1"][:, 0:1], None, AX.is_ge), inc="v")
        vspace()
        op("vector", lambda e, S: e.scalar_tensor_tensor(
            sb["g2"][:, :], sb["t1"][:, :], -1e30, sb["uu"][:, :], AX.mult, AX.add), inc="v")
        vspace()
        op("vector", lambda e, S: e.reduce_max(sb["m2"][:, :], sb["g2"][:, :], AXX), inc="v")
        vspace()
        op("vector", lambda e, S: e.tensor_scalar(sb["t1"][:, :], sb["uu"][:, :],
                                                  sb["m2"][:, 0:1], None, AX.is_ge), inc="v")
        vspace()
        op("vector", lambda e, S: e.tensor_tensor(sb["g2"][:, :], sb["uu"][:, :],
                                                  sb["t1"][:, :], AX.mult), inc="v")
        op("vector", lambda e, S: e.reciprocal(sb["m1"][:, :], sb["usum"][:, :]), inc="v")
        vspace()
        op("vector", lambda e, S, qt_=qt: e.tensor_scalar(
            sb["wtm"][:, qt_, :], sb["g2"][:, :], sb["m1"][:, 0:1], None, AX.mult), inc="v")
        b2 = banks.acquire()
        wt("tensor", "v")
        op("tensor", lambda e, S, b_=b2, qt_=qt: e.transpose(
            pbh[b_][0:8, 0:128], sb["wtm"][:, qt_, :], sb["identf"][:, :]), inc="t")
        tv = plan.val("t")
        wt("vector", "t", tv)
        op("vector", lambda e, S, b_=b2, qt_=qt: e.tensor_copy(
            sb["wfm"][:, qt_ * 128:(qt_ + 1) * 128], pbh[b_][0:8, 0:128]), inc="v")
        banks.release(b2, "vector")

    if KDBG:
        wt("sync", "v")
        dma("sync", dbg["d_w"].ap()[:, :, :], sb["wtm"][:, :, :])
        dma("sync", dbg["d_x2fm"].ap()[:, :, :], sb["x2fm"][:, :, :])

    wt("tensor", "v")
    for qt in range(NQT):
        bA = banks.acquire()
        bB = banks.acquire()
        for half, b in ((0, bA), (1, bB)):
            op("tensor", lambda e, S, b_=b, qt_=qt, h_=half: e.matmul(
                pbh[b_][:, 0:512],
                sb["wfm"][:, qt_ * 128:(qt_ + 1) * 128],
                sb["ebsb"][:, h_ * 512:(h_ + 1) * 512],
                start=True, stop=True), inc="t")
        tv = plan.val("t")
        wt("vector", "t", tv)
        for half, b in ((0, bA), (1, bB)):
            op("vector", lambda e, S, b_=b, qt_=qt, h_=half: e.tensor_copy(
                sb["acc"][:, qt_, h_ * 512:(h_ + 1) * 512], pbh[b_][:, 0:512]), inc="v")
        banks.release(bA, "vector")
        banks.release(bB, "vector")

    wE_lastuse = [None, None]
    wE_rdy = [0, 0]
    xt2b_lastuse = [None, None]
    outb_lastuse = [None, None]

    def emit_ln2(qt):
        slot = qt % 2
        if xt2b_lastuse[slot] is not None:
            wt("sync", "v", xt2b_lastuse[slot])
        dma("sync", sb["xt2b"][:, slot, :], x2scr.ap()[qt * 128:(qt + 1) * 128, :])
        xrdy = plan.val("y")
        wt("vector", "y", xrdy)
        op("vector", lambda e, S, qt_=qt, sl=slot: e.tensor_tensor(
            sb["x3pre"][:, :], sb["acc"][:, qt_, :], sb["xt2b"][:, sl, :], AX.add), inc="v")
        xt2b_lastuse[slot] = plan.val("v")
        op("vector", lambda e, S: e.reduce_sum(sb["ssum"][:, :], sb["x3pre"][:, :], AXX), inc="v")
        vv = plan.val("v")
        wt("scalar", "v", vv)
        op("scalar", lambda e, S: e.activation(sb["junk"][:, :], sb["x3pre"][:, :],
                                               AF.Square, accum_out=sb["ssq"][:, :]), inc="s")
        sv = plan.val("s")
        wt("vector", "s", sv)
        op("vector", lambda e, S: e.tensor_scalar(sb["nmu"][:, :], sb["ssum"][:, :],
                                                  -1.0 / D, None, AX.mult), inc="v")
        op("vector", lambda e, S: e.tensor_scalar(sb["msq"][:, :], sb["ssq"][:, :],
                                                  1.0 / D, None, AX.mult), inc="v")
        op("vector", lambda e, S: e.tensor_scalar(sb["mu"][:, :], sb["ssum"][:, :],
                                                  1.0 / D, None, AX.mult), inc="v")
        vspace()
        op("vector", lambda e, S: e.scalar_tensor_tensor(
            sb["var"][:, :], sb["mu"][:, :], sb["nmu"][:, 0:1], sb["msq"][:, :],
            AX.mult, AX.add), inc="v")
        vspace()
        op("vector", lambda e, S: e.tensor_scalar(sb["var"][:, :], sb["var"][:, :],
                                                  1e-5, None, AX.add), inc="v")
        vv = plan.val("v")
        wt("scalar", "v", vv)
        op("scalar", lambda e, S: e.activation(sb["lnv"][:, :], sb["var"][:, :],
                                               AF.Ln), inc="s")
        op("scalar", lambda e, S: e.activation(sb["aa"][:, :], sb["lnv"][:, :],
                                               AF.Exp, scale=-0.5), inc="s")
        sv = plan.val("s")
        wt("vector", "s", sv)
        op("vector", lambda e, S: e.tensor_scalar(sb["bb"][:, :], sb["mu"][:, :],
                                                  sb["aa"][:, 0:1], -1.0, AX.mult, AX.mult), inc="v")
        vspace()
        if outb_lastuse[slot] is not None:
            wt("vector", "y", outb_lastuse[slot])
        op("vector", lambda e, S, sl=slot: e.tensor_scalar(
            sb["outb"][:, sl, :], sb["x3pre"][:, :],
            sb["aa"][:, 0:1], sb["bb"][:, 0:1], AX.mult, AX.add), inc="v")
        wt("sync", "v")
        dma("sync", y_out.ap()[qt * 128:(qt + 1) * 128, :], sb["outb"][:, slot, :])
        outb_lastuse[slot] = plan.val("y")

    for e_i in range(2):
        slot = e_i % 2
        dma("gpsimd", sb["wE"][:, slot, :, :],
            di["ewT"].ap()[e_i:e_i + 1, :, :].rearrange("z (c p) o -> p (z c) o", p=128))
        wE_rdy[slot] = plan.val("g")
    for e_i in range(EE):
        slot = e_i % 2
        wt("tensor", "g", wE_rdy[slot])
        for qt in range(NQT):
            bA = banks.acquire()
            bB = banks.acquire()
            for kc in range(8):
                for half, b in ((0, bA), (1, bB)):
                    op("tensor", lambda e, S, b_=b, kc_=kc, qt_=qt, sl=slot, h_=half: e.matmul(
                        pbh[b_][:, 0:512],
                        sb["x2fm"][:, kc_, qt_ * 128:(qt_ + 1) * 128],
                        sb["wE"][:, sl, kc_, h_ * 512:(h_ + 1) * 512],
                        start=(kc_ == 0), stop=(kc_ == 7)),
                       inc=("t" if (kc == 7 and half == 1) else None))
            tv = plan.val("t")
            wt("vector", "t", tv)
            for half, b in ((0, bA), (1, bB)):
                op("vector", lambda e, S, b_=b, qt_=qt, h_=half, ei=e_i: e.scalar_tensor_tensor(
                    sb["acc"][:, qt_, h_ * 512:(h_ + 1) * 512],
                    pbh[b_][:, 0:512],
                    sb["wtm"][:, qt_, ei:ei + 1],
                    sb["acc"][:, qt_, h_ * 512:(h_ + 1) * 512],
                    AX.mult, AX.add), inc="v")
            banks.release(bA, "vector")
            banks.release(bB, "vector")
            if e_i == EE - 1:
                if KDBG:
                    wt("sync", "v")
                    dma("sync", dbg["d_acc"].ap()[:, qt:qt + 1, :], sb["acc"][:, qt:qt + 1, :])
                emit_ln2(qt)
        wE_lastuse[slot] = plan.val("t")
        nxt = e_i + 2
        if nxt < EE:
            wt("gpsimd", "t", wE_lastuse[slot])
            dma("gpsimd", sb["wE"][:, slot, :, :],
                di["ewT"].ap()[nxt:nxt + 1, :, :].rearrange("z (c p) o -> p (z c) o", p=128))
            wE_rdy[slot] = plan.val("g")

    wt("sync", "y", plan.val("y"))

    with nc.Block() as block:
        @block.tensor
        def _(e):
            for fn in plan.ops["tensor"]:
                fn(e, sems)

        @block.vector
        def _(e):
            for fn in plan.ops["vector"]:
                fn(e, sems)

        @block.scalar
        def _(e):
            for fn in plan.ops["scalar"]:
                fn(e, sems)

        @block.gpsimd
        def _(e):
            for fn in plan.ops["gpsimd"]:
                fn(e, sems)

        @block.sync
        def _(e):
            for fn in plan.ops["sync"]:
                fn(e, sems)

    Bst.close()
    stack.close()
    return nc


def _host_masks():
    kk = np.arange(512)[:, None]
    qq = np.arange(256)[None, :]
    base = (kk > qq) & (kk <= qq + 256)
    masknm = base.astype(np.float32)
    maskb0 = (base & (kk >= 256)).astype(np.float32)

    def lay(m):
        return np.ascontiguousarray(
            m.reshape(4, 128, 256).transpose(1, 0, 2).reshape(128, 1024)
        ).astype(ml_dtypes.bfloat16)

    return lay(masknm), lay(maskb0)


def kernel(L0_attn_x, L0_attn_wq, L0_attn_wk, L0_attn_wv, L0_attn_wgq, L0_attn_wgk,
           L0_attn_wgv, L0_attn_r_out_w, L0_attn_gate_local_w, L0_attn_gate_remote_w,
           L0_attn_w_out, L0_ffn_gate_w, L0_ffn_gate_b, L0_ffn_expert_w, L0_ffn_expert_b):
    x = np.ascontiguousarray(np.asarray(L0_attn_x, dtype=np.float32)[0])
    xT = np.ascontiguousarray(x.T)

    f32 = lambda a: np.ascontiguousarray(np.asarray(a, dtype=np.float32))
    wq, wk, wv = f32(L0_attn_wq), f32(L0_attn_wk), f32(L0_attn_wv)
    wgq, wgk, wgv = f32(L0_attn_wgq), f32(L0_attn_wgk), f32(L0_attn_wgv)
    rout, wout = f32(L0_attn_r_out_w), f32(L0_attn_w_out)
    glw, grw = f32(L0_attn_gate_local_w), f32(L0_attn_gate_remote_w)
    gw, gb = f32(L0_ffn_gate_w), f32(L0_ffn_gate_b)
    ew, eb = f32(L0_ffn_expert_w), f32(L0_ffn_expert_b)

    ewT = np.ascontiguousarray(ew.transpose(0, 2, 1))
    gwT = np.ascontiguousarray(gw.T)
    gbrep = np.ascontiguousarray(np.broadcast_to(gb[None, :], (128, EE)))

    gstat = np.zeros((128, 8, 4), np.float32)
    for pt in range(8):
        gstat[0:64, pt, 0] = glw[2 * pt]
        gstat[64:128, pt, 1] = glw[2 * pt + 1]
        gstat[0:64, pt, 2] = grw[2 * pt]
        gstat[64:128, pt, 3] = grw[2 * pt + 1]

    masknm, maskb0 = _host_masks()
    ident = np.eye(128, dtype=np.float32).astype(ml_dtypes.bfloat16)
    identf = np.eye(128, dtype=np.float32)
    sel16 = np.zeros((16, EE, 128), np.float32)
    for pt in range(8):
        for p in range(128):
            sel16[2 * pt + (p >= 64), pt, p] = 1.0
    sel16 = np.ascontiguousarray(sel16.reshape(16, EE * 128)).astype(ml_dtypes.bfloat16)
    sel8 = np.zeros((8, 4, 128), np.float32)
    for pt in range(4):
        for p in range(128):
            sel8[2 * pt + (p >= 64), pt, p] = 1.0
    sel8 = np.ascontiguousarray(sel8.reshape(8, 4 * 128)).astype(ml_dtypes.bfloat16)

    nc = build_nc()

    in_maps = []
    for c in range(N_CORES):
        lo = c * T
        if c == 0:
            xfm = np.concatenate([np.zeros((D, HALO), np.float32), xT[:, 0:T]], axis=1)
        else:
            xfm = xT[:, lo - HALO: lo + T]
        xtm = x[lo: lo + T]
        gtok = np.arange(lo, lo + T)
        lm = (np.arange(NLAT)[:, None] <= (gtok[None, :] // CHK)).astype(np.float32)
        in_maps.append({
            "xfm": np.ascontiguousarray(xfm),
            "xtm": np.ascontiguousarray(xtm),
            "wq": wq, "wk": wk, "wv": wv,
            "wgq": wgq, "wgk": wgk, "wgv": wgv,
            "rout": rout, "wout": wout,
            "ewT": ewT, "eb": eb, "gwT": gwT, "gbrep": gbrep,
            "gstat": gstat,
            "masknm": masknm, "maskb0": (maskb0 if c == 0 else masknm),
            "lmask": np.ascontiguousarray(lm.astype(ml_dtypes.bfloat16)),
            "ident": ident, "identf": identf, "sel16": sel16, "sel8": sel8,
        })

    res = run_bass_kernel_spmd(nc, in_maps, core_ids=list(range(N_CORES)))
    kernel.last_results = res.results
    out = np.concatenate([res.results[c]["y"] for c in range(N_CORES)], axis=0)
    return out.reshape(1, SEQ, D)


# version 10
# speedup vs baseline: 1.1473x; 1.1473x over previous
import os
import numpy as np
import ml_dtypes
from contextlib import ExitStack

import concourse.bass as bass
import concourse.mybir as mybir
from concourse.bass_utils import run_bass_kernel_spmd

F32 = mybir.dt.float32
BF16 = mybir.dt.bfloat16
AX = mybir.AluOpType
AF = mybir.ActivationFunctionType
AXX = mybir.AxisListType.X

N_CORES = 8
D = 1024
SEQ = 16384
T = 2048
HALO = 256
SUB = 512
NSUB = T // SUB
HTOK = SUB + HALO
NH, HD = 16, 64
LHN = 8
NLAT, CHK = 64, 256
EE = 8
NQT = T // 128
QPS = SUB // 128

ENGS = ("tensor", "vector", "scalar", "gpsimd", "sync")
PSEM = {"tensor": "t", "vector": "v", "scalar": "s", "gpsimd": "g", "sync": "y"}


class Plan:
    def __init__(self):
        self.ops = {e: [] for e in ENGS}
        self.cnt = {}
        self.lastwait = {e: {} for e in ENGS}

    def sem(self, name):
        self.cnt[name] = 0

    def bump(self, name, n=1):
        self.cnt[name] += n
        return self.cnt[name]

    def val(self, name):
        return self.cnt[name]


def build_nc():
    nc = bass.Bass("TRN2", target_bir_lowering=False)
    plan = Plan()
    for s in ("t", "v", "s", "g", "y", "cc"):
        plan.sem(s)

    di = {}
    def inp(name, shape, dt=F32):
        di[name] = nc.dram_tensor(name, list(shape), dt, kind="ExternalInput")
        return di[name]

    inp("xfm", [D, HALO + T])
    inp("xtm", [T, D])
    inp("wq", [D, D]); inp("wk", [D, D]); inp("wv", [D, D])
    inp("wgq", [D, 512]); inp("wgk", [D, 512]); inp("wgv", [D, 512])
    inp("rout", [512, D]); inp("wout", [D, D])
    inp("ewT", [EE, D, D]); inp("eb", [EE, D])
    inp("gwT", [D, EE]); inp("gbrep", [128, EE])
    inp("gstat", [128, EE, 4])
    inp("masknm", [128, 1024], BF16)
    inp("maskb0", [128, 1024], BF16)
    inp("lmask", [64, T], BF16)
    inp("ident", [128, 128], BF16)
    inp("identf", [128, 128], F32)
    inp("sel16", [16, EE * 128], BF16)
    inp("sel8", [8, 4 * 128], BF16)

    y_out = nc.dram_tensor("y", [T, D], F32, kind="ExternalOutput")
    x2scr = nc.dram_tensor("x2scr", [T, D], F32)
    agin = nc.dram_tensor("agin", [D, 8], F32)
    agout = nc.dram_tensor("agout", [N_CORES, D, 8], F32)
    scrA = nc.dram_tensor("scrA", [65, NH, 2, 256], BF16)
    scrB = nc.dram_tensor("scrB", [65, LHN, SUB], BF16)
    scrC = nc.dram_tensor("scrC", [2, 8, 2, SUB], BF16)

    KDBG = bool(os.environ.get("KDBG"))
    dbg = {}
    if KDBG:
        for nm, shape, dt in (
            ("d_q", [128, 8, SUB], BF16), ("d_k", [128, 8, HTOK], BF16),
            ("d_v", [128, 6, NH * 65], BF16), ("d_gq", [128, 4, SUB], BF16),
            ("d_pool", [128, 8, NLAT], F32), ("d_gk", [128, 4, NLAT], BF16),
            ("d_gvp", [64, LHN * 65], BF16),
            ("d_pvfm", [128, 8, SUB], BF16), ("d_denl", [16, SUB], BF16),
            ("d_remstk", [128, 4, SUB], BF16), ("d_denr", [8, SUB], BF16),
            ("d_gll", [16, SUB], F32), ("d_glr", [16, SUB], F32),
            ("d_mixed", [128, 8, SUB], BF16),
            ("d_x2", [T, D], F32),
            ("d_x2pre", [128, D], F32),
            ("d_stats", [8, 128, 1], F32),
            ("d_w", [128, NQT, EE], F32),
            ("d_x2fm", [128, 8, T], BF16),
            ("d_acc", [128, NQT, D], F32),
        ):
            dbg[nm] = nc.dram_tensor(nm, list(shape), dt, kind="ExternalOutput")

    stack = ExitStack()
    sems = {}
    for s in ("t", "v", "s", "g", "y", "cc"):
        sems[s] = stack.enter_context(nc.semaphore("sem_" + s))

    def wt(eng, sem, v=None):
        if v is None:
            v = plan.val(sem)
        if v <= 0:
            return
        lw = plan.lastwait[eng]
        if lw.get(sem, -1) >= v:
            return
        lw[sem] = v
        plan.ops[eng].append(lambda e, S, s_=sem, vv=v: e.wait_ge(S[s_], vv))

    def op(eng, fn, inc=None, n=1):
        if inc is None:
            plan.ops[eng].append(lambda e, S, f=fn: f(e, S))
        else:
            plan.bump(inc, n)
            plan.ops[eng].append(lambda e, S, f=fn, s_=inc, k=n: f(e, S).then_inc(S[s_], k))

    def dma(queue, out_ap, in_ap):
        op(queue, lambda e, S, o=out_ap, i=in_ap: e.dma_start(out=o, in_=i),
           inc=PSEM[queue], n=16)

    pbig = stack.enter_context(nc.psum_tensor("pbig", [128, 8, 512], F32))
    pbh = [pbig[:, i, :] for i in range(8)]

    class Banks:
        def __init__(self):
            self.freemark = [None] * 8
            self.rr = 0

        def acquire(self):
            b = self.rr % 8
            self.rr += 1
            fm = self.freemark[b]
            if fm is not None:
                wt("tensor", fm[0], fm[1])
            return b

        def release(self, b, eng):
            self.freemark[b] = (PSEM[eng], plan.val(PSEM[eng]))

    banks = Banks()

    sb = {}
    def sbuf_on(st, name, shape, dt):
        sb[name] = st.enter_context(nc.sbuf_tensor("s_" + name, list(shape), dt))
        return sb[name]

    for name, shape, dt in (
        ("masknm", [128, 1024], BF16), ("maskb0", [128, 1024], BF16),
        ("lmask", [64, T], BF16), ("ident", [128, 128], BF16),
        ("identf", [128, 128], F32),
        ("sel16", [16, EE * 128], BF16), ("sel8", [8, 4 * 128], BF16),
        ("gstat", [128, EE * 4], BF16), ("gwT", [128, 8, EE], BF16),
        ("gbrep", [128, EE], F32), ("ebsb", [8, D], BF16),
        ("pooled", [128, 8, NLAT], F32), ("pooledbf", [128, 8, NLAT], BF16),
        ("gksb", [128, 4, NLAT], BF16), ("gvp", [64, LHN * 65], BF16),
        ("junk", [128, D], BF16), ("ssum", [128, 1], F32), ("ssq", [128, 1], F32),
        ("mu", [128, 1], F32), ("msq", [128, 1], F32), ("mu2", [128, 1], F32),
        ("var", [128, 1], F32), ("lnv", [128, 1], F32), ("aa", [128, 1], F32),
        ("bb", [128, 1], F32), ("nmu", [128, 1], F32),
        ("spc0", [128, 1], F32), ("spc1", [128, 1], F32),
    ):
        sbuf_on(stack, name, shape, dt)

    A = ExitStack()
    for name, shape, dt in (
        ("xfs", [128, 2, 8, HTOK], BF16),
        ("wring", [128, 2, 8, 1024], BF16),
        ("qsb", [128, 8, SUB], BF16),
        ("ksb", [128, 8, HTOK], BF16),
        ("vsb", [128, 6, NH * 65], BF16),
        ("gqsb", [128, 4, SUB], BF16),
        ("esb", [128, 2, 2, 4, 256], BF16),
        ("ersb", [64, 2, SUB], BF16),
        ("pvs", [65, NH, 2, 256], BF16),
        ("pvfm", [128, 8, SUB], BF16),
        ("denl", [16, SUB], BF16),
        ("rems", [65, LHN, SUB], BF16),
        ("remstk", [128, 4, SUB], BF16),
        ("denr", [8, SUB], BF16),
        ("glr_l", [16, SUB], F32),
        ("glr_r", [16, SUB], F32),
        ("recl", [16, SUB], F32),
        ("clbf", [16, SUB], BF16),
        ("crbf", [16, SUB], BF16),
        ("recrbf", [8, SUB], BF16),
        ("remnorm", [128, 4, SUB], BF16),
        ("crsb", [128, SUB], BF16),
        ("tmpv", [128, SUB], BF16),
        ("mixed", [128, 8, SUB], BF16),
        ("xtmb", [128, 2, D], F32),
        ("x2pre", [128, D], F32),
        ("x2tm", [128, 1, D], F32),
        ("gvtmp", [128, 4, NLAT], BF16),
    ):
        sbuf_on(A, name, shape, dt)

    dma("sync", sb["masknm"][:, :], di["masknm"].ap()[:, :])
    dma("sync", sb["maskb0"][:, :], di["maskb0"].ap()[:, :])
    dma("sync", sb["lmask"][:, :], di["lmask"].ap()[:, :])
    dma("sync", sb["ident"][:, :], di["ident"].ap()[:, :])
    dma("sync", sb["identf"][:, :], di["identf"].ap()[:, :])
    dma("sync", sb["sel16"][:, :], di["sel16"].ap()[:, :])
    dma("sync", sb["sel8"][:, :], di["sel8"].ap()[:, :])
    dma("sync", sb["gbrep"][:, :], di["gbrep"].ap()[:, :])
    dma("gpsimd", sb["gstat"][:, :], di["gstat"].ap().rearrange("p e f -> p (e f)"))
    dma("gpsimd", sb["gwT"][:, :, :], di["gwT"].ap().rearrange("(c p) e -> p c e", p=128))
    dma("gpsimd", sb["ebsb"][:, :], di["eb"].ap()[:, :])

    op("vector", lambda e, S: e.memset(sb["spc0"][:, :], 0.0), inc="v")
    op("vector", lambda e, S: e.memset(sb["spc1"][:, :], 0.0), inc="v")

    def vspace():
        op("vector", lambda e, S: e.tensor_copy(sb["spc1"][:, :], sb["spc0"][:, :]), inc="v")

    for j in range(NSUB):
        if j >= 1:
            wt("gpsimd", "v")
        dma("gpsimd", sb["xfs"][:, 1, :, 0:SUB],
            di["xfm"].ap().rearrange("(c p) t -> p c t", p=128)[:, :, HALO + j * SUB: HALO + (j + 1) * SUB])
        gval = plan.val("g")
        wt("vector", "g", gval)
        for c in range(8):
            op("vector",
               lambda e, S, cc=c, j_=j: e.reduce_sum(
                   sb["pooled"][:, cc, j_ * 2:(j_ + 1) * 2],
                   sb["xfs"][:, 1, cc, 0:SUB].rearrange("p (s q) -> p s q", q=256),
                   AXX),
               inc="v")
    vspace()
    op("vector", lambda e, S: e.tensor_scalar(
        sb["pooled"][:, :, 0:8],
        sb["pooled"][:, :, 0:8],
        1.0 / 256.0, None, AX.mult), inc="v")
    pool_reads_done = plan.val("v")
    wt("sync", "v")
    dma("sync", agin.ap().rearrange("(c p) n -> p c n", p=128), sb["pooled"][:, :, 0:8])
    wt("gpsimd", "y")
    op("gpsimd", lambda e, S: e.collective_compute(
        "AllGather", AX.bypass,
        replica_groups=[list(range(N_CORES))],
        ins=[agin.ap().opt()],
        outs=[agout.ap().opt()]), inc="cc")
    wt("sync", "cc")
    for r_ in range(N_CORES):
        dma("sync", sb["pooled"][:, :, 8 * r_: 8 * r_ + 8],
            agout.ap()[r_, :, :].rearrange("(c p) n -> p c n", p=128))
    agload_done = plan.val("y")

    wring_lastuse = [0, 0]
    wring_next = 0

    def load_wring(name, chunks, width):
        nonlocal wring_next
        slot = wring_next % 2
        wring_next += 1
        if wring_lastuse[slot] > 0:
            wt("gpsimd", "t", wring_lastuse[slot])
        dma("gpsimd", sb["wring"][:, slot, 0:chunks, 0:width],
            di[name].ap().rearrange("(c p) o -> p c o", p=128))
        return slot, plan.val("g")

    xfs_lastuse = [None, None]
    xtmb_lastuse = [None, None]
    x2tm_lastuse = [None, None]
    marks = {}
    dump_marks = {}
    load_marks = {}
    spill_marks = []
    wslots = {}

    def load_xfs(s):
        slot = s % 2
        if s == 0:
            pass
        if slot == 1:
            wt("gpsimd", "v", pool_reads_done)
        if xfs_lastuse[slot] is not None:
            wt("gpsimd", "t", xfs_lastuse[slot])
        dma("gpsimd", sb["xfs"][:, slot, :, :],
            di["xfm"].ap().rearrange("(c p) t -> p c t", p=128)[:, :, s * SUB: s * SUB + HTOK])
        return plan.val("g")

    evac_rr = [0]

    def evac_copy(b, out_ap_fn):
        tv = plan.val("t")
        eng = "vector" if (evac_rr[0] % 2 == 0) else "scalar"
        evac_rr[0] += 1
        wt(eng, "t", tv)
        if eng == "vector":
            op(eng, lambda e, S, b_=b, f=out_ap_fn: e.tensor_copy(*f(b_)), inc="v")
        else:
            op(eng, lambda e, S, b_=b, f=out_ap_fn: e.copy(*f(b_)), inc="s")
        banks.release(b, eng)

    def emit_proj(s, xready, wq_pre=None, wk_pre=None):
        slot = s % 2

        def proj_fm(dstn, wslot, ocs, own_only, nlist):
            for oc in ocs:
                for (a, c) in nlist:
                    b = banks.acquire()
                    for kc in range(8):
                        op("tensor", lambda e, S, b_=b, kc_=kc, oc_=oc, ws=wslot, sl=slot,
                           a_=a, c_=c, oo=own_only: e.matmul(
                               pbh[b_][:, 0:c_ - a_],
                               sb["wring"][:, ws, kc_, oc_ * 128: oc_ * 128 + 128],
                               (sb["xfs"][:, sl, kc_, HALO:HTOK] if oo
                                else sb["xfs"][:, sl, kc_, :])[:, a_:c_],
                               start=(kc_ == 0), stop=(kc_ == 7)),
                           inc=("t" if kc == 7 else None))
                    evac_copy(b, lambda b_, oc_=oc, a_=a, c_=c, d=dstn:
                              (sb[d][:, oc_, a_:c_], pbh[b_][:, 0:c_ - a_]))

        wq_slot, wq_rdy = wq_pre if wq_pre is not None else load_wring("wq", 8, 1024)
        wk_slot, wk_rdy = wk_pre if wk_pre is not None else load_wring("wk", 8, 1024)
        wt("tensor", "g", max(xready, wq_rdy))
        proj_fm("qsb", wq_slot, range(8), True, [(0, 512)])
        wring_lastuse[wq_slot] = plan.val("t")
        wv_slot, wv_rdy = load_wring("wv", 8, 1024)
        wt("tensor", "g", wk_rdy)
        proj_fm("ksb", wk_slot, range(8), False, [(0, 512), (512, 768)])
        wring_lastuse[wk_slot] = plan.val("t")
        wgq_slot, wgq_rdy = load_wring("wgq", 8, 512)
        wt("tensor", "g", wv_rdy)
        for tc in range(6):
            for half in range(2):
                b = banks.acquire()
                for kc in range(8):
                    op("tensor", lambda e, S, b_=b, kc_=kc, tc_=tc, h_=half, sl=slot,
                       ws=wv_slot: e.matmul(
                        pbh[b_][:, 0:512],
                        sb["xfs"][:, sl, kc_, tc_ * 128: tc_ * 128 + 128],
                        sb["wring"][:, ws, kc_, h_ * 512: h_ * 512 + 512],
                        start=(kc_ == 0), stop=(kc_ == 7)),
                       inc=("t" if kc == 7 else None))
                evac_copy(b, lambda b_, tc_=tc, h_=half:
                          (sb["vsb"].ap().rearrange("p t (h c) -> p t h c", c=65)[:, tc_, h_ * 8:h_ * 8 + 8, 0:64],
                           pbh[b_][:, 0:512].rearrange("p (h d) -> p h d", d=64)))
        wring_lastuse[wv_slot] = plan.val("t")
        if s == 0:
            op("vector", lambda e, S: e.memset(
                sb["vsb"].ap().rearrange("p t (h c) -> p t h c", c=65)[:, :, :, 64:65], 1.0), inc="v")
        wt("tensor", "g", wgq_rdy)
        proj_fm("gqsb", wgq_slot, range(4), True, [(0, 512)])
        wring_lastuse[wgq_slot] = plan.val("t")
        xfs_lastuse[slot] = plan.val("t")
        marks[("pev", s)] = (plan.val("v"), plan.val("s"))
        if s == 0:
            wslots["wgk"] = load_wring("wgk", 8, 512)
            wslots["wgv"] = load_wring("wgv", 8, 512)

        if KDBG and s == 0:
            wt("sync", "v")
            dma("sync", dbg["d_q"].ap()[:, :, :], sb["qsb"][:, :, :])
            dma("sync", dbg["d_k"].ap()[:, :, :], sb["ksb"][:, :, :])
            dma("sync", dbg["d_v"].ap()[:, :, :], sb["vsb"][:, :, :])
            dma("sync", dbg["d_gq"].ap()[:, :, :], sb["gqsb"][:, :, :])

    def emit_gkgv():
        ws_gk = wslots["wgk"]
        ws_gv = wslots["wgv"]
        wt("vector", "y", agload_done)
        op("vector", lambda e, S: e.tensor_copy(
            sb["pooledbf"].ap().rearrange("p c n -> p (c n)"),
            sb["pooled"].ap().rearrange("p c n -> p (c n)")), inc="v")
        wt("tensor", "g", ws_gv[1])
        wt("tensor", "v")
        for dstn, wslot in (("gksb", ws_gk[0]), ("gvtmp", ws_gv[0])):
            for oc in range(4):
                b = banks.acquire()
                for kc in range(8):
                    op("tensor", lambda e, S, b_=b, kc_=kc, oc_=oc, ws=wslot: e.matmul(
                        pbh[b_][:, 0:NLAT],
                        sb["wring"][:, ws, kc_, oc_ * 128: oc_ * 128 + 128],
                        sb["pooledbf"][:, kc_, :],
                        start=(kc_ == 0), stop=(kc_ == 7)),
                       inc=("t" if kc == 7 else None))
                tv = plan.val("t")
                wt("vector", "t", tv)
                op("vector", lambda e, S, b_=b, oc_=oc, d=dstn: e.tensor_copy(
                    sb[d][:, oc_, :], pbh[b_][:, 0:NLAT]), inc="v")
                banks.release(b, "vector")
        wring_lastuse[wslots["wgk"][0]] = plan.val("t")
        wring_lastuse[wslots["wgv"][0]] = plan.val("t")
        op("vector", lambda e, S: e.memset(
            sb["gvp"].ap().rearrange("n (h c) -> n h c", c=65)[:, :, 64:65], 1.0), inc="v")
        for pt in range(4):
            b = banks.acquire()
            wt("tensor", "v")
            op("tensor", lambda e, S, b_=b, pt_=pt: e.transpose(
                pbig.bitcast(BF16)[0:64, b_, 0:128], sb["gvtmp"][:, pt_, :], sb["ident"][:, :]), inc="t")
            tv = plan.val("t")
            wt("vector", "t", tv)
            op("vector", lambda e, S, b_=b, pt_=pt: e.tensor_copy(
                sb["gvp"].ap().rearrange("n (h c) -> n h c", c=65)[0:64, 2 * pt_:2 * pt_ + 2, 0:64],
                pbig.bitcast(BF16)[0:64, b_, 0:128].rearrange("n (h d) -> n h d", d=64)), inc="v")
            banks.release(b, "vector")
        if KDBG:
            wt("sync", "v")
            dma("sync", dbg["d_pool"].ap()[:, :, :], sb["pooled"][:, :, :])
            dma("sync", dbg["d_gk"].ap()[:, :, :], sb["gksb"][:, :, :])
            dma("sync", dbg["d_gvp"].ap()[:, :], sb["gvp"][:, :])

    def emit_remote(s):
        if (s - 1, "C") in dump_marks:
            wt("vector", "y", dump_marks[(s - 1, "C")])
        wt("tensor", "v", marks[("pev", s)][0])
        wt("tensor", "s", marks[("pev", s)][1])
        mask_g = {}
        pend = []

        def rem_pv(lh, es):
            wt("tensor", "g", mask_g[lh])
            b2 = banks.acquire()
            op("tensor", lambda e, S, b_=b2, lh_=lh, es_=es: e.matmul(
                pbh[b_][0:65, 0:512],
                sb["gvp"][0:64, lh_ * 65: lh_ * 65 + 65],
                sb["ersb"][0:64, es_, :],
                start=True, stop=True), inc="t")
            tv = plan.val("t")
            wt("vector", "t", tv)
            op("vector", lambda e, S, b_=b2, lh_=lh: e.tensor_copy(
                sb["rems"][:, lh_, :], pbh[b_][0:65, 0:512]), inc="v")
            banks.release(b2, "vector")

        for lh in range(LHN):
            es = lh % 2
            base = 64 * (lh % 2)
            b = banks.acquire()
            op("tensor", lambda e, S, b_=b, lh_=lh, ba=base: e.matmul(
                pbh[b_][0:64, 0:512],
                sb["gksb"][ba:ba + 64, lh_ // 2, :],
                sb["gqsb"][ba:ba + 64, lh_ // 2, :],
                start=True, stop=True), inc="t")
            tv = plan.val("t")
            wt("scalar", "t", tv)
            op("scalar", lambda e, S, b_=b, es_=es: e.activation(
                sb["ersb"][0:64, es_, :], pbh[b_][0:64, 0:512], AF.Exp, scale=0.125), inc="s")
            banks.release(b, "scalar")
            sv = plan.val("s")
            wt("gpsimd", "s", sv)
            op("gpsimd", lambda e, S, es_=es, si=s: e.tensor_tensor(
                sb["ersb"][0:64, es_, :], sb["ersb"][0:64, es_, :],
                sb["lmask"][0:64, si * SUB:(si + 1) * SUB], AX.mult), inc="g", n=1)
            mask_g[lh] = plan.val("g")
            if lh >= 1:
                rem_pv(lh - 1, (lh - 1) % 2)
        rem_pv(LHN - 1, (LHN - 1) % 2)
        rem_ev = plan.val("v")
        if (s - 1,) in load_marks:
            wt("sync", "y", load_marks[(s - 1,)])
        wt("sync", "v", rem_ev)
        dma("sync", scrB.ap()[:, :, :], sb["rems"][:, :, :])
        dump_marks[(s, "B")] = plan.val("y")

    def emit_local(s):
        if (s - 1, "A") in dump_marks:
            wt("vector", "y", dump_marks[(s - 1, "A")])
        wt("tensor", "v", marks[("pev", s)][0])
        wt("tensor", "s", marks[("pev", s)][1])
        for b8 in range(8):
            fm = banks.freemark[b8]
            if fm is not None:
                wt("tensor", fm[0], fm[1])
        NU = 16
        exp_mark = {}
        mask_mark = {}
        pvev_mark = {}

        def emit_scores(u):
            hp, blk = divmod(u, 2)
            p = u % 2
            if u - 2 in exp_mark:
                wt("tensor", "s", exp_mark[u - 2])
            if u - 2 in pvev_mark:
                wt("tensor", "v", pvev_mark[u - 2])
            n_mm = 0
            for hi in range(2):
                h = hp * 2 + hi
                base = 64 * (h % 2)
                for kc in range(4):
                    bank = 4 * p + hi * 2 + kc // 2
                    c0 = (kc % 2) * 256
                    kt = 2 * blk + kc
                    n_mm += 1
                    op("tensor", lambda e, S, bk=bank, cc=c0, h_=h, ba=base, kt_=kt, bl=blk: e.matmul(
                        pbig[:, bk, cc:cc + 256],
                        sb["ksb"][ba:ba + 64, h_ // 2, kt_ * 128: kt_ * 128 + 128],
                        sb["qsb"][ba:ba + 64, h_ // 2, bl * 256: bl * 256 + 256],
                        start=True, stop=True),
                       inc=("t" if n_mm == 8 else None))
            tv = plan.val("t")
            es = u % 2
            wt("scalar", "t", tv)
            op("scalar", lambda e, S, p_=p, es_=es: e.activation(
                sb["esb"].ap().rearrange("x r h k q -> x r h (k q)")[:, es_, :, :],
                pbig[:, 4 * p_:4 * p_ + 4, :],
                AF.Exp, scale=0.125), inc="s")
            exp_mark[u] = plan.val("s")
            sv = plan.val("s")
            wt("vector", "s", sv)
            mk = "maskb0" if (s == 0 and blk == 0) else "masknm"
            for hi in range(2):
                op("vector", lambda e, S, es_=es, hi_=hi, m=mk: e.tensor_tensor(
                    sb["esb"].ap().rearrange("x r h k q -> x r h (k q)")[:, es_, hi_, :],
                    sb["esb"].ap().rearrange("x r h k q -> x r h (k q)")[:, es_, hi_, :],
                    sb[m][:, :], AX.mult), inc="v")
            mask_mark[u] = plan.val("v")

        def emit_pv(u):
            hp, blk = divmod(u, 2)
            p = u % 2
            es = u % 2
            bank = 4 * p
            wt("tensor", "v", mask_mark[u])
            for hi in range(2):
                h = hp * 2 + hi
                c0 = hi * 256
                for kc in range(4):
                    kt = 2 * blk + kc
                    op("tensor", lambda e, S, bk=bank, cc=c0, h_=h, kt_=kt, es_=es, hi_=hi, kc_=kc: e.matmul(
                        pbig[0:65, bk, cc:cc + 256],
                        sb["vsb"][:, kt_, h_ * 65: h_ * 65 + 65],
                        sb["esb"][:, es_, hi_, kc_, :],
                        start=(kc_ == 0), stop=(kc_ == 3)),
                       inc=("t" if (hi == 1 and kc == 3) else None))
            tv = plan.val("t")
            wt("vector", "t", tv)
            op("vector", lambda e, S, bk=bank, hp_=hp, bl=blk: e.tensor_copy(
                sb["pvs"][:, 2 * hp_:2 * hp_ + 2, bl, :],
                pbig[0:65, bk, :].rearrange("d (h q) -> d h q", q=256)), inc="v")
            pvev_mark[u] = plan.val("v")

        for u in range(NU):
            emit_scores(u)
            if u >= 1:
                emit_pv(u - 1)
        emit_pv(NU - 1)
        for b8 in range(8):
            banks.freemark[b8] = ("v", plan.val("v"))

        wt("sync", "v", plan.val("v"))
        dma("sync", scrA.ap()[:, :, :, :], sb["pvs"][:, :, :, :])
        dump_marks[(s, "A")] = plan.val("y")

    def emit_glog(s):
        wt("vector", "y", dump_marks[(s, "B")])
        for pt in range(8):
            b = banks.acquire()
            op("tensor", lambda e, S, b_=b, pt_=pt: e.matmul(
                pbh[b_][0:4, 0:512],
                sb["gstat"][:, pt_ * 4: pt_ * 4 + 4],
                sb["qsb"][:, pt_, :],
                start=True, stop=True), inc="t")
            tv = plan.val("t")
            wt("vector", "t", tv)
            op("vector", lambda e, S, b_=b, pt_=pt: e.tensor_copy(
                sb["rems"][0:4, pt_, :], pbh[b_][0:4, 0:512]), inc="v")
            banks.release(b, "vector")
        wt("sync", "v", plan.val("v"))
        for k in range(2):
            for hi in range(2):
                dma("sync",
                    scrC.ap()[k:k + 1, :, hi, :],
                    sb["rems"][2 * k + hi: 2 * k + hi + 1, 0:8, :])
        dump_marks[(s, "C")] = plan.val("y")

    def emit_gather_loads(s):
        wt("sync", "y", dump_marks[(s, "C")])
        if ("cmb", s - 1) in marks:
            wt("sync", "v", marks[("cmb", s - 1)])
        for hi in range(2):
            dma("sync",
                sb["pvfm"][64 * hi: 64 * hi + 64, :, :],
                scrA.ap().rearrange("d (pt hi) b q -> d pt hi (b q)", hi=2)[0:64, :, hi, :])
        dma("sync",
            sb["denl"][:, :],
            scrA.ap().rearrange("d h b q -> d h (b q)")[64, :, :])
        for hi in range(2):
            dma("sync",
                sb["remstk"][64 * hi: 64 * hi + 64, :, :],
                scrB.ap().rearrange("d (pt hi) q -> d pt hi q", hi=2)[0:64, :, hi, :])
        dma("sync",
            sb["denr"][:, :],
            scrB.ap()[64, :, :])
        wt("gpsimd", "y", dump_marks[(s, "C")])
        dma("gpsimd",
            sb["glr_l"][:, :],
            scrC.ap()[0, :, :, :].rearrange("pt hi q -> (pt hi) q"))
        dma("gpsimd",
            sb["glr_r"][:, :],
            scrC.ap()[1, :, :, :].rearrange("pt hi q -> (pt hi) q"))
        glgr_rdy = plan.val("g")
        gathered = plan.val("y")
        load_marks[(s,)] = gathered

        if KDBG and s == 0:
            wt("sync", "y", gathered)
            dma("sync", dbg["d_pvfm"].ap()[:, :, :], sb["pvfm"][:, :, :])
            dma("sync", dbg["d_denl"].ap()[:, :], sb["denl"][:, :])
            dma("sync", dbg["d_remstk"].ap()[:, :, :], sb["remstk"][:, :, :])
            dma("sync", dbg["d_denr"].ap()[:, :], sb["denr"][:, :])
        return glgr_rdy, gathered

    def emit_coeffs(s, glgr_rdy, gathered):
        wt("scalar", "g", glgr_rdy)
        wt("scalar", "y", gathered)
        for gn in ("glr_l", "glr_r"):
            op("scalar", lambda e, S, g_=gn: e.activation(sb[g_][:, :], sb[g_][:, :],
                                                          AF.Exp, scale=-1.0), inc="s")
        sv = plan.val("s")
        wt("vector", "s", sv)
        for gn in ("glr_l", "glr_r"):
            op("vector", lambda e, S, g_=gn: e.tensor_scalar(sb[g_][:, :], sb[g_][:, :],
                                                             1.0, None, AX.add), inc="v")
            op("vector", lambda e, S, g_=gn: e.reciprocal(sb[g_][:, :], sb[g_][:, :]), inc="v")
        wt("vector", "y", gathered)
        op("vector", lambda e, S: e.reciprocal(sb["recl"][:, :], sb["denl"][:, :]), inc="v")
        op("vector", lambda e, S: e.tensor_tensor(sb["clbf"][:, :], sb["glr_l"][:, :],
                                                  sb["recl"][:, :], AX.mult), inc="v")
        op("vector", lambda e, S: e.tensor_copy(sb["crbf"][:, :], sb["glr_r"][:, :]), inc="v")
        if KDBG and s == 0:
            wt("sync", "v")
            dma("sync", dbg["d_gll"].ap()[:, :], sb["glr_l"][:, :])
            dma("sync", dbg["d_glr"].ap()[:, :], sb["glr_r"][:, :])
        op("vector", lambda e, S: e.reciprocal(sb["recl"][0:8, :], sb["denr"][:, :]), inc="v")
        op("vector", lambda e, S: e.tensor_copy(sb["recrbf"][:, :], sb["recl"][0:8, :]), inc="v")
        marks[("coef", s)] = plan.val("v")

    def emit_combine(s):
        rout_slot, rout_rdy = wslots[("rout", s)]
        wt("tensor", "v", marks[("coef", s)])
        for pt in range(4):
            b = banks.acquire()
            op("tensor", lambda e, S, b_=b, pt_=pt: e.matmul(
                pbh[b_][:, 0:512],
                sb["sel8"][:, pt_ * 128: pt_ * 128 + 128],
                sb["recrbf"][:, :],
                start=True, stop=True), inc="t")
            tv = plan.val("t")
            wt("vector", "t", tv)
            op("vector", lambda e, S, b_=b, pt_=pt: e.tensor_tensor(
                sb["remnorm"][:, pt_, :], sb["remstk"][:, pt_, :],
                pbh[b_][:, 0:512], AX.mult), inc="v")
            banks.release(b, "vector")
        wt("tensor", "g", rout_rdy)
        wt("tensor", "v", plan.val("v"))
        for pt in range(8):
            bl_ = banks.acquire()
            op("tensor", lambda e, S, b_=bl_, pt_=pt: e.matmul(
                pbh[b_][:, 0:512],
                sb["sel16"][:, pt_ * 128: pt_ * 128 + 128],
                sb["clbf"][:, :],
                start=True, stop=True), inc="t")
            tvl = plan.val("t")
            bc_ = banks.acquire()
            op("tensor", lambda e, S, b_=bc_, pt_=pt: e.matmul(
                pbh[b_][:, 0:512],
                sb["sel16"][:, pt_ * 128: pt_ * 128 + 128],
                sb["crbf"][:, :],
                start=True, stop=True), inc="t")
            tvc = plan.val("t")
            br_ = banks.acquire()
            for kc in range(4):
                op("tensor", lambda e, S, b_=br_, kc_=kc, pt_=pt, rs=rout_slot: e.matmul(
                    pbh[b_][:, 0:512],
                    sb["wring"][:, rs, kc_, pt_ * 128: pt_ * 128 + 128],
                    sb["remnorm"][:, kc_, :],
                    start=(kc_ == 0), stop=(kc_ == 3)),
                   inc=("t" if kc == 3 else None))
            tvr = plan.val("t")
            wt("vector", "t", tvl)
            op("vector", lambda e, S, b_=bl_, pt_=pt: e.tensor_tensor(
                sb["mixed"][:, pt_, :], sb["pvfm"][:, pt_, :], pbh[b_][:, 0:512],
                AX.mult), inc="v")
            banks.release(bl_, "vector")
            wt("vector", "t", tvc)
            op("vector", lambda e, S, b_=bc_: e.tensor_copy(
                sb["crsb"][:, :], pbh[b_][:, 0:512]), inc="v")
            banks.release(bc_, "vector")
            wt("vector", "t", tvr)
            op("vector", lambda e, S, b_=br_: e.tensor_tensor(
                sb["tmpv"][:, :], pbh[b_][:, 0:512], sb["crsb"][:, :], AX.mult), inc="v")
            banks.release(br_, "vector")
            op("vector", lambda e, S, pt_=pt: e.tensor_tensor(
                sb["mixed"][:, pt_, :], sb["mixed"][:, pt_, :], sb["tmpv"][:, :],
                AX.add), inc="v")
        wring_lastuse[rout_slot] = plan.val("t")
        marks[("cmb", s)] = plan.val("v")
        if KDBG and s == 0:
            wt("sync", "v")
            dma("sync", dbg["d_mixed"].ap()[:, :, :], sb["mixed"][:, :, :])

    def emit_woutln1(s):
        wout_slot, wout_rdy = wslots[("wout", s)]
        wt("tensor", "g", wout_rdy)
        wt("tensor", "v", marks[("cmb", s)])
        for qt in range(QPS):
            qt_g = s * QPS + qt
            tslot = qt_g % 2
            if xtmb_lastuse[tslot] is not None:
                wt("sync", "v", xtmb_lastuse[tslot])
            dma("sync", sb["xtmb"][:, tslot, :], di["xtm"].ap()[qt_g * 128:(qt_g + 1) * 128, :])
            xtrdy = plan.val("y")

            bA = banks.acquire()
            bB = banks.acquire()
            for kc in range(8):
                for half, b in ((0, bA), (1, bB)):
                    op("tensor", lambda e, S, b_=b, kc_=kc, qt_=qt, h_=half, ws=wout_slot: e.matmul(
                        pbh[b_][:, 0:512],
                        sb["mixed"][:, kc_, qt_ * 128: qt_ * 128 + 128],
                        sb["wring"][:, ws, kc_, h_ * 512: h_ * 512 + 512],
                        start=(kc_ == 0), stop=(kc_ == 7)),
                       inc=("t" if (kc == 7 and half == 1) else None))
            tv = plan.val("t")
            wt("vector", "t", tv)
            wt("vector", "y", xtrdy)
            op("vector", lambda e, S, b_=bA, ts=tslot: e.tensor_tensor(
                sb["x2pre"][:, 0:512], sb["xtmb"][:, ts, 0:512], pbh[b_][:, 0:512], AX.add), inc="v")
            op("vector", lambda e, S, b_=bB, ts=tslot: e.tensor_tensor(
                sb["x2pre"][:, 512:1024], sb["xtmb"][:, ts, 512:1024], pbh[b_][:, 0:512], AX.add), inc="v")
            banks.release(bA, "vector")
            banks.release(bB, "vector")
            xtmb_lastuse[tslot] = plan.val("v")
            if KDBG and qt_g == 0:
                wt("sync", "v")
                dma("sync", dbg["d_x2pre"].ap()[:, :], sb["x2pre"][:, :])
            op("vector", lambda e, S: e.reduce_sum(sb["ssum"][:, :], sb["x2pre"][:, :], AXX), inc="v")
            vv = plan.val("v")
            wt("scalar", "v", vv)
            op("scalar", lambda e, S: e.activation(sb["junk"][:, :], sb["x2pre"][:, :],
                                                   AF.Square, accum_out=sb["ssq"][:, :]), inc="s")
            sv = plan.val("s")
            wt("vector", "s", sv)
            op("vector", lambda e, S: e.tensor_scalar(sb["nmu"][:, :], sb["ssum"][:, :],
                                                      -1.0 / D, None, AX.mult), inc="v")
            op("vector", lambda e, S: e.tensor_scalar(sb["msq"][:, :], sb["ssq"][:, :],
                                                      1.0 / D, None, AX.mult), inc="v")
            op("vector", lambda e, S: e.tensor_scalar(sb["mu"][:, :], sb["ssum"][:, :],
                                                      1.0 / D, None, AX.mult), inc="v")
            vspace()
            op("vector", lambda e, S: e.scalar_tensor_tensor(
                sb["var"][:, :], sb["mu"][:, :], sb["nmu"][:, 0:1], sb["msq"][:, :],
                AX.mult, AX.add), inc="v")
            vspace()
            op("vector", lambda e, S: e.tensor_scalar(sb["var"][:, :], sb["var"][:, :],
                                                      1e-5, None, AX.add), inc="v")
            vv = plan.val("v")
            wt("scalar", "v", vv)
            op("scalar", lambda e, S: e.activation(sb["lnv"][:, :], sb["var"][:, :],
                                                   AF.Ln), inc="s")
            op("scalar", lambda e, S: e.activation(sb["aa"][:, :], sb["lnv"][:, :],
                                                   AF.Exp, scale=-0.5), inc="s")
            sv = plan.val("s")
            wt("vector", "s", sv)
            op("vector", lambda e, S: e.tensor_scalar(sb["bb"][:, :], sb["mu"][:, :],
                                                      sb["aa"][:, 0:1], -1.0, AX.mult, AX.mult), inc="v")
            vspace()
            if KDBG and qt_g == 0:
                wt("sync", "s")
                wt("sync", "v")
                for i_, nm_ in enumerate(("ssum", "ssq", "mu", "msq", "mu2", "var", "lnv", "aa")):
                    dma("sync", dbg["d_stats"].ap()[i_, :, :], sb[nm_][:, :])
            if x2tm_lastuse[0] is not None:
                wt("vector", "y", x2tm_lastuse[0])
            op("vector", lambda e, S: e.tensor_scalar(
                sb["x2tm"][:, 0, :], sb["x2pre"][:, :],
                sb["aa"][:, 0:1], sb["bb"][:, 0:1], AX.mult, AX.add), inc="v")
            wt("sync", "v")
            dma("sync", x2scr.ap()[qt_g * 128:(qt_g + 1) * 128, :], sb["x2tm"][:, 0, :])
            x2tm_lastuse[0] = plan.val("y")
            spill_marks.append(plan.val("y"))
        wring_lastuse[wout_slot] = plan.val("t")

    xrdy0 = load_xfs(0)
    emit_proj(0, xrdy0)
    emit_gkgv()
    for s in range(NSUB):
        emit_remote(s)
        emit_local(s)
        emit_glog(s)
        if s + 1 < NSUB:
            xrdy = load_xfs(s + 1)
            wq_pre = load_wring("wq", 8, 1024)
            wk_pre = load_wring("wk", 8, 1024)
        glgr_rdy, gathered = emit_gather_loads(s)
        emit_coeffs(s, glgr_rdy, gathered)
        if s + 1 < NSUB:
            emit_proj(s + 1, xrdy, wq_pre, wk_pre)
        wslots[("rout", s)] = load_wring("rout", 4, 1024)
        wslots[("wout", s)] = load_wring("wout", 8, 1024)
        emit_combine(s)
        emit_woutln1(s)

    spills_done = plan.val("y")
    if KDBG:
        wt("sync", "y", spills_done)
        dma("sync", dbg["d_x2"].ap()[:, :], x2scr.ap()[:, :])

    marks = {s: plan.val(s) for s in ("t", "v", "s", "g", "y")}
    for eng in ENGS:
        for s_, v_ in marks.items():
            wt(eng, s_, v_)

    A.close()

    Bst = ExitStack()
    for name, shape, dt in (
        ("x2fm", [128, 8, T], BF16),
        ("xbf", [128, 2, D], BF16),
        ("wE", [128, 2, 8, D], BF16),
        ("acc", [128, NQT, D], F32),
        ("gfm", [8, T], F32),
        ("uu", [128, EE], F32),
        ("usum", [128, 1], F32),
        ("t1", [128, EE], F32),
        ("g2", [128, EE], F32),
        ("m1", [128, 1], F32),
        ("m2", [128, 1], F32),
        ("wtm", [128, NQT, EE], F32),
        ("wfm", [8, T], BF16),
        ("xt2b", [128, 2, D], F32),
        ("outb", [128, 2, D], F32),
        ("x3pre", [128, D], F32),
    ):
        sbuf_on(Bst, name, shape, dt)

    xbf_lastuse = [None, None]
    for qt in range(NQT):
        slot = qt % 2
        if xbf_lastuse[slot] is not None:
            wt("gpsimd", "t", xbf_lastuse[slot])
        dma("gpsimd", sb["xbf"][:, slot, :], x2scr.ap()[qt * 128:(qt + 1) * 128, :])
        xrdy = plan.val("g")
        wt("tensor", "g", xrdy)
        for fc in range(8):
            b = banks.acquire()
            op("tensor", lambda e, S, b_=b, sl=slot, fc_=fc: e.transpose(
                pbig.bitcast(BF16)[:, b_, 0:128], sb["xbf"][:, sl, fc_ * 128: fc_ * 128 + 128],
                sb["ident"][:, :]), inc="t")
            tv = plan.val("t")
            wt("vector", "t", tv)
            op("vector", lambda e, S, b_=b, fc_=fc, qt_=qt: e.tensor_copy(
                sb["x2fm"][:, fc_, qt_ * 128:(qt_ + 1) * 128], pbig.bitcast(BF16)[:, b_, 0:128]), inc="v")
            banks.release(b, "vector")
        xbf_lastuse[slot] = plan.val("t")

    wt("tensor", "v")
    for nck in range(4):
        b = banks.acquire()
        for kc in range(8):
            op("tensor", lambda e, S, b_=b, kc_=kc, n_=nck: e.matmul(
                pbh[b_][0:8, 0:512],
                sb["gwT"][:, kc_, :],
                sb["x2fm"][:, kc_, n_ * 512:(n_ + 1) * 512],
                start=(kc_ == 0), stop=(kc_ == 7)), inc=("t" if kc == 7 else None))
        tv = plan.val("t")
        wt("vector", "t", tv)
        op("vector", lambda e, S, b_=b, n_=nck: e.tensor_copy(
            sb["gfm"][:, n_ * 512:(n_ + 1) * 512], pbh[b_][0:8, 0:512]), inc="v")
        banks.release(b, "vector")
    for qt in range(NQT):
        b = banks.acquire()
        wt("tensor", "v")
        op("tensor", lambda e, S, b_=b, qt_=qt: e.transpose(
            pbh[b_][:, 0:8], sb["gfm"][0:8, qt_ * 128:(qt_ + 1) * 128],
            sb["identf"][0:8, 0:8]), inc="t")
        tv = plan.val("t")
        wt("vector", "t", tv)
        op("vector", lambda e, S, b_=b: e.tensor_tensor(
            sb["uu"][:, :], pbh[b_][:, 0:8], sb["gbrep"][:, :], AX.add), inc="v")
        banks.release(b, "vector")
        vv = plan.val("v")
        wt("scalar", "v", vv)
        op("scalar", lambda e, S: e.activation(sb["uu"][:, :], sb["uu"][:, :],
                                               AF.Exp, accum_out=sb["usum"][:, :]), inc="s")
        sv = plan.val("s")
        wt("vector", "s", sv)
        op("vector", lambda e, S: e.reduce_max(sb["m1"][:, :], sb["uu"][:, :], AXX), inc="v")
        vspace()
        op("vector", lambda e, S: e.tensor_scalar(sb["t1"][:, :], sb["uu"][:, :],
                                                  sb["m1"][:, 0:1], None, AX.is_ge), inc="v")
        vspace()
        op("vector", lambda e, S: e.scalar_tensor_tensor(
            sb["g2"][:, :], sb["t1"][:, :], -1e30, sb["uu"][:, :], AX.mult, AX.add), inc="v")
        vspace()
        op("vector", lambda e, S: e.reduce_max(sb["m2"][:, :], sb["g2"][:, :], AXX), inc="v")
        vspace()
        op("vector", lambda e, S: e.tensor_scalar(sb["t1"][:, :], sb["uu"][:, :],
                                                  sb["m2"][:, 0:1], None, AX.is_ge), inc="v")
        vspace()
        op("vector", lambda e, S: e.tensor_tensor(sb["g2"][:, :], sb["uu"][:, :],
                                                  sb["t1"][:, :], AX.mult), inc="v")
        op("vector", lambda e, S: e.reciprocal(sb["m1"][:, :], sb["usum"][:, :]), inc="v")
        vspace()
        op("vector", lambda e, S, qt_=qt: e.tensor_scalar(
            sb["wtm"][:, qt_, :], sb["g2"][:, :], sb["m1"][:, 0:1], None, AX.mult), inc="v")
        b2 = banks.acquire()
        wt("tensor", "v")
        op("tensor", lambda e, S, b_=b2, qt_=qt: e.transpose(
            pbh[b_][0:8, 0:128], sb["wtm"][:, qt_, :], sb["identf"][:, :]), inc="t")
        tv = plan.val("t")
        wt("vector", "t", tv)
        op("vector", lambda e, S, b_=b2, qt_=qt: e.tensor_copy(
            sb["wfm"][:, qt_ * 128:(qt_ + 1) * 128], pbh[b_][0:8, 0:128]), inc="v")
        banks.release(b2, "vector")

    if KDBG:
        wt("sync", "v")
        dma("sync", dbg["d_w"].ap()[:, :, :], sb["wtm"][:, :, :])
        dma("sync", dbg["d_x2fm"].ap()[:, :, :], sb["x2fm"][:, :, :])

    wt("tensor", "v")
    for qt in range(NQT):
        bA = banks.acquire()
        bB = banks.acquire()
        for half, b in ((0, bA), (1, bB)):
            op("tensor", lambda e, S, b_=b, qt_=qt, h_=half: e.matmul(
                pbh[b_][:, 0:512],
                sb["wfm"][:, qt_ * 128:(qt_ + 1) * 128],
                sb["ebsb"][:, h_ * 512:(h_ + 1) * 512],
                start=True, stop=True), inc="t")
        tv = plan.val("t")
        wt("vector", "t", tv)
        for half, b in ((0, bA), (1, bB)):
            op("vector", lambda e, S, b_=b, qt_=qt, h_=half: e.tensor_copy(
                sb["acc"][:, qt_, h_ * 512:(h_ + 1) * 512], pbh[b_][:, 0:512]), inc="v")
        banks.release(bA, "vector")
        banks.release(bB, "vector")

    wE_lastuse = [None, None]
    wE_rdy = [0, 0]
    xt2b_lastuse = [None, None]
    outb_lastuse = [None, None]

    def emit_ln2(qt):
        slot = qt % 2
        if xt2b_lastuse[slot] is not None:
            wt("sync", "v", xt2b_lastuse[slot])
        dma("sync", sb["xt2b"][:, slot, :], x2scr.ap()[qt * 128:(qt + 1) * 128, :])
        xrdy = plan.val("y")
        wt("vector", "y", xrdy)
        op("vector", lambda e, S, qt_=qt, sl=slot: e.tensor_tensor(
            sb["x3pre"][:, :], sb["acc"][:, qt_, :], sb["xt2b"][:, sl, :], AX.add), inc="v")
        xt2b_lastuse[slot] = plan.val("v")
        op("vector", lambda e, S: e.reduce_sum(sb["ssum"][:, :], sb["x3pre"][:, :], AXX), inc="v")
        vv = plan.val("v")
        wt("scalar", "v", vv)
        op("scalar", lambda e, S: e.activation(sb["junk"][:, :], sb["x3pre"][:, :],
                                               AF.Square, accum_out=sb["ssq"][:, :]), inc="s")
        sv = plan.val("s")
        wt("vector", "s", sv)
        op("vector", lambda e, S: e.tensor_scalar(sb["nmu"][:, :], sb["ssum"][:, :],
                                                  -1.0 / D, None, AX.mult), inc="v")
        op("vector", lambda e, S: e.tensor_scalar(sb["msq"][:, :], sb["ssq"][:, :],
                                                  1.0 / D, None, AX.mult), inc="v")
        op("vector", lambda e, S: e.tensor_scalar(sb["mu"][:, :], sb["ssum"][:, :],
                                                  1.0 / D, None, AX.mult), inc="v")
        vspace()
        op("vector", lambda e, S: e.scalar_tensor_tensor(
            sb["var"][:, :], sb["mu"][:, :], sb["nmu"][:, 0:1], sb["msq"][:, :],
            AX.mult, AX.add), inc="v")
        vspace()
        op("vector", lambda e, S: e.tensor_scalar(sb["var"][:, :], sb["var"][:, :],
                                                  1e-5, None, AX.add), inc="v")
        vv = plan.val("v")
        wt("scalar", "v", vv)
        op("scalar", lambda e, S: e.activation(sb["lnv"][:, :], sb["var"][:, :],
                                               AF.Ln), inc="s")
        op("scalar", lambda e, S: e.activation(sb["aa"][:, :], sb["lnv"][:, :],
                                               AF.Exp, scale=-0.5), inc="s")
        sv = plan.val("s")
        wt("vector", "s", sv)
        op("vector", lambda e, S: e.tensor_scalar(sb["bb"][:, :], sb["mu"][:, :],
                                                  sb["aa"][:, 0:1], -1.0, AX.mult, AX.mult), inc="v")
        vspace()
        if outb_lastuse[slot] is not None:
            wt("vector", "y", outb_lastuse[slot])
        op("vector", lambda e, S, sl=slot: e.tensor_scalar(
            sb["outb"][:, sl, :], sb["x3pre"][:, :],
            sb["aa"][:, 0:1], sb["bb"][:, 0:1], AX.mult, AX.add), inc="v")
        wt("sync", "v")
        dma("sync", y_out.ap()[qt * 128:(qt + 1) * 128, :], sb["outb"][:, slot, :])
        outb_lastuse[slot] = plan.val("y")

    for e_i in range(2):
        slot = e_i % 2
        dma("gpsimd", sb["wE"][:, slot, :, :],
            di["ewT"].ap()[e_i:e_i + 1, :, :].rearrange("z (c p) o -> p (z c) o", p=128))
        wE_rdy[slot] = plan.val("g")
    for e_i in range(EE):
        slot = e_i % 2
        wt("tensor", "g", wE_rdy[slot])
        for qt in range(NQT):
            bA = banks.acquire()
            bB = banks.acquire()
            for kc in range(8):
                for half, b in ((0, bA), (1, bB)):
                    op("tensor", lambda e, S, b_=b, kc_=kc, qt_=qt, sl=slot, h_=half: e.matmul(
                        pbh[b_][:, 0:512],
                        sb["x2fm"][:, kc_, qt_ * 128:(qt_ + 1) * 128],
                        sb["wE"][:, sl, kc_, h_ * 512:(h_ + 1) * 512],
                        start=(kc_ == 0), stop=(kc_ == 7)),
                       inc=("t" if (kc == 7 and half == 1) else None))
            tv = plan.val("t")
            wt("vector", "t", tv)
            for half, b in ((0, bA), (1, bB)):
                op("vector", lambda e, S, b_=b, qt_=qt, h_=half, ei=e_i: e.scalar_tensor_tensor(
                    sb["acc"][:, qt_, h_ * 512:(h_ + 1) * 512],
                    pbh[b_][:, 0:512],
                    sb["wtm"][:, qt_, ei:ei + 1],
                    sb["acc"][:, qt_, h_ * 512:(h_ + 1) * 512],
                    AX.mult, AX.add), inc="v")
            banks.release(bA, "vector")
            banks.release(bB, "vector")
            if e_i == EE - 1:
                if KDBG:
                    wt("sync", "v")
                    dma("sync", dbg["d_acc"].ap()[:, qt:qt + 1, :], sb["acc"][:, qt:qt + 1, :])
                if qt >= 1:
                    emit_ln2(qt - 1)
                if qt == NQT - 1:
                    emit_ln2(qt)
        wE_lastuse[slot] = plan.val("t")
        nxt = e_i + 2
        if nxt < EE:
            wt("gpsimd", "t", wE_lastuse[slot])
            dma("gpsimd", sb["wE"][:, slot, :, :],
                di["ewT"].ap()[nxt:nxt + 1, :, :].rearrange("z (c p) o -> p (z c) o", p=128))
            wE_rdy[slot] = plan.val("g")

    wt("sync", "y", plan.val("y"))

    with nc.Block() as block:
        @block.tensor
        def _(e):
            for fn in plan.ops["tensor"]:
                fn(e, sems)

        @block.vector
        def _(e):
            for fn in plan.ops["vector"]:
                fn(e, sems)

        @block.scalar
        def _(e):
            for fn in plan.ops["scalar"]:
                fn(e, sems)

        @block.gpsimd
        def _(e):
            for fn in plan.ops["gpsimd"]:
                fn(e, sems)

        @block.sync
        def _(e):
            for fn in plan.ops["sync"]:
                fn(e, sems)

    Bst.close()
    stack.close()
    return nc


def _host_masks():
    kk = np.arange(512)[:, None]
    qq = np.arange(256)[None, :]
    base = (kk > qq) & (kk <= qq + 256)
    masknm = base.astype(np.float32)
    maskb0 = (base & (kk >= 256)).astype(np.float32)

    def lay(m):
        return np.ascontiguousarray(
            m.reshape(4, 128, 256).transpose(1, 0, 2).reshape(128, 1024)
        ).astype(ml_dtypes.bfloat16)

    return lay(masknm), lay(maskb0)


def kernel(L0_attn_x, L0_attn_wq, L0_attn_wk, L0_attn_wv, L0_attn_wgq, L0_attn_wgk,
           L0_attn_wgv, L0_attn_r_out_w, L0_attn_gate_local_w, L0_attn_gate_remote_w,
           L0_attn_w_out, L0_ffn_gate_w, L0_ffn_gate_b, L0_ffn_expert_w, L0_ffn_expert_b):
    x = np.ascontiguousarray(np.asarray(L0_attn_x, dtype=np.float32)[0])
    xT = np.ascontiguousarray(x.T)

    f32 = lambda a: np.ascontiguousarray(np.asarray(a, dtype=np.float32))
    wq, wk, wv = f32(L0_attn_wq), f32(L0_attn_wk), f32(L0_attn_wv)
    wgq, wgk, wgv = f32(L0_attn_wgq), f32(L0_attn_wgk), f32(L0_attn_wgv)
    rout, wout = f32(L0_attn_r_out_w), f32(L0_attn_w_out)
    glw, grw = f32(L0_attn_gate_local_w), f32(L0_attn_gate_remote_w)
    gw, gb = f32(L0_ffn_gate_w), f32(L0_ffn_gate_b)
    ew, eb = f32(L0_ffn_expert_w), f32(L0_ffn_expert_b)

    ewT = np.ascontiguousarray(ew.transpose(0, 2, 1))
    gwT = np.ascontiguousarray(gw.T)
    gbrep = np.ascontiguousarray(np.broadcast_to(gb[None, :], (128, EE)))

    gstat = np.zeros((128, 8, 4), np.float32)
    for pt in range(8):
        gstat[0:64, pt, 0] = glw[2 * pt]
        gstat[64:128, pt, 1] = glw[2 * pt + 1]
        gstat[0:64, pt, 2] = grw[2 * pt]
        gstat[64:128, pt, 3] = grw[2 * pt + 1]

    masknm, maskb0 = _host_masks()
    ident = np.eye(128, dtype=np.float32).astype(ml_dtypes.bfloat16)
    identf = np.eye(128, dtype=np.float32)
    sel16 = np.zeros((16, EE, 128), np.float32)
    for pt in range(8):
        for p in range(128):
            sel16[2 * pt + (p >= 64), pt, p] = 1.0
    sel16 = np.ascontiguousarray(sel16.reshape(16, EE * 128)).astype(ml_dtypes.bfloat16)
    sel8 = np.zeros((8, 4, 128), np.float32)
    for pt in range(4):
        for p in range(128):
            sel8[2 * pt + (p >= 64), pt, p] = 1.0
    sel8 = np.ascontiguousarray(sel8.reshape(8, 4 * 128)).astype(ml_dtypes.bfloat16)

    nc = build_nc()

    in_maps = []
    for c in range(N_CORES):
        lo = c * T
        if c == 0:
            xfm = np.concatenate([np.zeros((D, HALO), np.float32), xT[:, 0:T]], axis=1)
        else:
            xfm = xT[:, lo - HALO: lo + T]
        xtm = x[lo: lo + T]
        gtok = np.arange(lo, lo + T)
        lm = (np.arange(NLAT)[:, None] <= (gtok[None, :] // CHK)).astype(np.float32)
        in_maps.append({
            "xfm": np.ascontiguousarray(xfm),
            "xtm": np.ascontiguousarray(xtm),
            "wq": wq, "wk": wk, "wv": wv,
            "wgq": wgq, "wgk": wgk, "wgv": wgv,
            "rout": rout, "wout": wout,
            "ewT": ewT, "eb": eb, "gwT": gwT, "gbrep": gbrep,
            "gstat": gstat,
            "masknm": masknm, "maskb0": (maskb0 if c == 0 else masknm),
            "lmask": np.ascontiguousarray(lm.astype(ml_dtypes.bfloat16)),
            "ident": ident, "identf": identf, "sel16": sel16, "sel8": sel8,
        })

    res = run_bass_kernel_spmd(nc, in_maps, core_ids=list(range(N_CORES)))
    kernel.last_results = res.results
    out = np.concatenate([res.results[c]["y"] for c in range(N_CORES)], axis=0)
    return out.reshape(1, SEQ, D)


# version 11
# speedup vs baseline: 1.1546x; 1.0064x over previous
import os
import numpy as np
import ml_dtypes
from contextlib import ExitStack

import concourse.bass as bass
import concourse.mybir as mybir
from concourse.bass_utils import run_bass_kernel_spmd

F32 = mybir.dt.float32
BF16 = mybir.dt.bfloat16
AX = mybir.AluOpType
AF = mybir.ActivationFunctionType
AXX = mybir.AxisListType.X

N_CORES = 8
D = 1024
SEQ = 16384
T = 2048
HALO = 256
SUB = 512
NSUB = T // SUB
HTOK = SUB + HALO
NH, HD = 16, 64
LHN = 8
NLAT, CHK = 64, 256
EE = 8
NQT = T // 128
QPS = SUB // 128

ENGS = ("tensor", "vector", "scalar", "gpsimd", "sync")
PSEM = {"tensor": "t", "vector": "v", "scalar": "s", "gpsimd": "g", "sync": "y"}


class Plan:
    def __init__(self):
        self.ops = {e: [] for e in ENGS}
        self.cnt = {}
        self.lastwait = {e: {} for e in ENGS}

    def sem(self, name):
        self.cnt[name] = 0

    def bump(self, name, n=1):
        self.cnt[name] += n
        return self.cnt[name]

    def val(self, name):
        return self.cnt[name]


def build_nc():
    nc = bass.Bass("TRN2", target_bir_lowering=False)
    plan = Plan()
    for s in ("t", "v", "s", "g", "y", "cc"):
        plan.sem(s)

    di = {}
    def inp(name, shape, dt=F32):
        di[name] = nc.dram_tensor(name, list(shape), dt, kind="ExternalInput")
        return di[name]

    inp("xfm", [D, HALO + T])
    inp("xtm", [T, D])
    inp("wq", [D, D]); inp("wk", [D, D]); inp("wv", [D, D])
    inp("wgq", [D, 512]); inp("wgk", [D, 512]); inp("wgv", [D, 512])
    inp("rout", [512, D]); inp("wout", [D, D])
    inp("ewT", [EE, D, D]); inp("eb", [EE, D])
    inp("gwT", [D, EE]); inp("gbrep", [128, EE])
    inp("gstat", [128, EE, 4])
    inp("masknm", [128, 1024], BF16)
    inp("maskb0", [128, 1024], BF16)
    inp("lmask", [64, T], BF16)
    inp("ident", [128, 128], BF16)
    inp("identf", [128, 128], F32)
    inp("sel16", [16, EE * 128], BF16)
    inp("sel8", [8, 4 * 128], BF16)

    y_out = nc.dram_tensor("y", [T, D], F32, kind="ExternalOutput")
    x2scr = nc.dram_tensor("x2scr", [T, D], F32)
    agin = nc.dram_tensor("agin", [D, 8], F32)
    agout = nc.dram_tensor("agout", [N_CORES, D, 8], F32)
    scrA = nc.dram_tensor("scrA", [65, NH, 2, 256], BF16)
    scrB = nc.dram_tensor("scrB", [65, LHN, SUB], BF16)
    scrC = nc.dram_tensor("scrC", [2, 8, 2, SUB], BF16)

    KDBG = bool(os.environ.get("KDBG"))
    dbg = {}
    if KDBG:
        for nm, shape, dt in (
            ("d_q", [128, 8, SUB], BF16), ("d_k", [128, 8, HTOK], BF16),
            ("d_v", [128, 6, NH * 65], BF16), ("d_gq", [128, 4, SUB], BF16),
            ("d_pool", [128, 8, NLAT], F32), ("d_gk", [128, 4, NLAT], BF16),
            ("d_gvp", [64, LHN * 65], BF16),
            ("d_pvfm", [128, 8, SUB], BF16), ("d_denl", [16, SUB], BF16),
            ("d_remstk", [128, 4, SUB], BF16), ("d_denr", [8, SUB], BF16),
            ("d_gll", [16, SUB], F32), ("d_glr", [16, SUB], F32),
            ("d_mixed", [128, 8, SUB], BF16),
            ("d_x2", [T, D], F32),
            ("d_x2pre", [128, D], F32),
            ("d_stats", [8, 128, 1], F32),
            ("d_w", [128, NQT, EE], F32),
            ("d_x2fm", [128, 8, T], BF16),
            ("d_acc", [128, NQT, D], F32),
        ):
            dbg[nm] = nc.dram_tensor(nm, list(shape), dt, kind="ExternalOutput")

    stack = ExitStack()
    sems = {}
    for s in ("t", "v", "s", "g", "y", "cc"):
        sems[s] = stack.enter_context(nc.semaphore("sem_" + s))

    def wt(eng, sem, v=None):
        if v is None:
            v = plan.val(sem)
        if v <= 0:
            return
        lw = plan.lastwait[eng]
        if lw.get(sem, -1) >= v:
            return
        lw[sem] = v
        plan.ops[eng].append(lambda e, S, s_=sem, vv=v: e.wait_ge(S[s_], vv))

    def op(eng, fn, inc=None, n=1):
        if inc is None:
            plan.ops[eng].append(lambda e, S, f=fn: f(e, S))
        else:
            plan.bump(inc, n)
            plan.ops[eng].append(lambda e, S, f=fn, s_=inc, k=n: f(e, S).then_inc(S[s_], k))

    def dma(queue, out_ap, in_ap):
        op(queue, lambda e, S, o=out_ap, i=in_ap: e.dma_start(out=o, in_=i),
           inc=PSEM[queue], n=16)

    pbig = stack.enter_context(nc.psum_tensor("pbig", [128, 8, 512], F32))
    pbh = [pbig[:, i, :] for i in range(8)]

    class Banks:
        def __init__(self):
            self.freemark = [None] * 8
            self.rr = 0

        def acquire(self):
            b = self.rr % 8
            self.rr += 1
            fm = self.freemark[b]
            if fm is not None:
                for eng_s, val in (fm if isinstance(fm, list) else [fm]):
                    wt("tensor", eng_s, val)
            return b

        def align4(self):
            if self.rr % 4 != 0:
                self.rr += 4 - (self.rr % 4)

        def release(self, b, eng):
            self.freemark[b] = (PSEM[eng], plan.val(PSEM[eng]))

    banks = Banks()

    sb = {}
    def sbuf_on(st, name, shape, dt):
        sb[name] = st.enter_context(nc.sbuf_tensor("s_" + name, list(shape), dt))
        return sb[name]

    for name, shape, dt in (
        ("masknm", [128, 1024], BF16), ("maskb0", [128, 1024], BF16),
        ("lmask", [64, T], BF16), ("ident", [128, 128], BF16),
        ("identf", [128, 128], F32),
        ("sel16", [16, EE * 128], BF16), ("sel8", [8, 4 * 128], BF16),
        ("gstat", [128, EE * 4], BF16), ("gwT", [128, 8, EE], BF16),
        ("gbrep", [128, EE], F32), ("ebsb", [8, D], BF16),
        ("pooled", [128, 8, NLAT], F32), ("pooledbf", [128, 8, NLAT], BF16),
        ("gksb", [128, 4, NLAT], BF16), ("gvp", [64, LHN * 65], BF16),
        ("junk", [128, D], BF16), ("ssum", [128, 1], F32), ("ssq", [128, 1], F32),
        ("mu", [128, 1], F32), ("msq", [128, 1], F32), ("mu2", [128, 1], F32),
        ("var", [128, 1], F32), ("lnv", [128, 1], F32), ("aa", [128, 1], F32),
        ("bb", [128, 1], F32), ("nmu", [128, 1], F32),
        ("spc0", [128, 1], F32), ("spc1", [128, 1], F32),
    ):
        sbuf_on(stack, name, shape, dt)

    A = ExitStack()
    for name, shape, dt in (
        ("xfs", [128, 2, 8, HTOK], BF16),
        ("wring", [128, 2, 8, 1024], BF16),
        ("qsb", [128, 8, SUB], BF16),
        ("ksb", [128, 8, HTOK], BF16),
        ("vsb", [128, 6, NH * 65], BF16),
        ("gqsb", [128, 4, SUB], BF16),
        ("esb", [128, 2, 2, 4, 256], BF16),
        ("ersb", [64, 2, 4, SUB], BF16),
        ("pvs", [65, 8, 2, 256], BF16),
        ("pvfm", [128, 8, SUB], BF16),
        ("denl", [16, SUB], BF16),
        ("rems", [65, LHN, SUB], BF16),
        ("remstk", [128, 4, SUB], BF16),
        ("denr", [8, SUB], BF16),
        ("glr_l", [16, SUB], F32),
        ("glr_r", [16, SUB], F32),
        ("recl", [16, SUB], F32),
        ("clbf", [16, SUB], BF16),
        ("crbf", [16, SUB], BF16),
        ("recrbf", [8, SUB], BF16),
        ("remnorm", [128, 4, SUB], BF16),
        ("crsb", [128, SUB], BF16),
        ("tmpv", [128, SUB], BF16),
        ("mixed", [128, 8, SUB], BF16),
        ("xtmb", [128, 2, D], F32),
        ("x2pre", [128, D], F32),
        ("x2tm", [128, 1, D], F32),
        ("gvtmp", [128, 4, NLAT], BF16),
    ):
        sbuf_on(A, name, shape, dt)

    dma("sync", sb["masknm"][:, :], di["masknm"].ap()[:, :])
    dma("sync", sb["maskb0"][:, :], di["maskb0"].ap()[:, :])
    dma("sync", sb["lmask"][:, :], di["lmask"].ap()[:, :])
    dma("sync", sb["ident"][:, :], di["ident"].ap()[:, :])
    dma("sync", sb["identf"][:, :], di["identf"].ap()[:, :])
    dma("sync", sb["sel16"][:, :], di["sel16"].ap()[:, :])
    dma("sync", sb["sel8"][:, :], di["sel8"].ap()[:, :])
    dma("sync", sb["gbrep"][:, :], di["gbrep"].ap()[:, :])
    dma("gpsimd", sb["gstat"][:, :], di["gstat"].ap().rearrange("p e f -> p (e f)"))
    dma("gpsimd", sb["gwT"][:, :, :], di["gwT"].ap().rearrange("(c p) e -> p c e", p=128))
    dma("gpsimd", sb["ebsb"][:, :], di["eb"].ap()[:, :])

    op("vector", lambda e, S: e.memset(sb["spc0"][:, :], 0.0), inc="v")
    op("vector", lambda e, S: e.memset(sb["spc1"][:, :], 0.0), inc="v")

    def vspace():
        op("vector", lambda e, S: e.tensor_copy(sb["spc1"][:, :], sb["spc0"][:, :]), inc="v")

    for j in range(NSUB):
        if j >= 1:
            wt("gpsimd", "v")
        dma("gpsimd", sb["xfs"][:, 1, :, 0:SUB],
            di["xfm"].ap().rearrange("(c p) t -> p c t", p=128)[:, :, HALO + j * SUB: HALO + (j + 1) * SUB])
        gval = plan.val("g")
        wt("vector", "g", gval)
        for c in range(8):
            op("vector",
               lambda e, S, cc=c, j_=j: e.reduce_sum(
                   sb["pooled"][:, cc, j_ * 2:(j_ + 1) * 2],
                   sb["xfs"][:, 1, cc, 0:SUB].rearrange("p (s q) -> p s q", q=256),
                   AXX),
               inc="v")
    vspace()
    op("vector", lambda e, S: e.tensor_scalar(
        sb["pooled"][:, :, 0:8],
        sb["pooled"][:, :, 0:8],
        1.0 / 256.0, None, AX.mult), inc="v")
    pool_reads_done = plan.val("v")
    wt("sync", "v")
    dma("sync", agin.ap().rearrange("(c p) n -> p c n", p=128), sb["pooled"][:, :, 0:8])
    wt("gpsimd", "y")
    op("gpsimd", lambda e, S: e.collective_compute(
        "AllGather", AX.bypass,
        replica_groups=[list(range(N_CORES))],
        ins=[agin.ap().opt()],
        outs=[agout.ap().opt()]), inc="cc")
    wt("sync", "cc")
    for r_ in range(N_CORES):
        dma("sync", sb["pooled"][:, :, 8 * r_: 8 * r_ + 8],
            agout.ap()[r_, :, :].rearrange("(c p) n -> p c n", p=128))
    agload_done = plan.val("y")

    wring_lastuse = [0, 0]
    wring_next = 0

    def load_wring(name, chunks, width):
        nonlocal wring_next
        slot = wring_next % 2
        wring_next += 1
        if wring_lastuse[slot] > 0:
            wt("gpsimd", "t", wring_lastuse[slot])
        dma("gpsimd", sb["wring"][:, slot, 0:chunks, 0:width],
            di[name].ap().rearrange("(c p) o -> p c o", p=128))
        return slot, plan.val("g")

    xfs_lastuse = [None, None]
    xtmb_lastuse = [None, None]
    x2tm_lastuse = [None, None]
    marks = {}
    dump_marks = {}
    load_marks = {}
    spill_marks = []
    wslots = {}

    def load_xfs(s):
        slot = s % 2
        if s == 0:
            pass
        if slot == 1:
            wt("gpsimd", "v", pool_reads_done)
        if xfs_lastuse[slot] is not None:
            wt("gpsimd", "t", xfs_lastuse[slot])
        dma("gpsimd", sb["xfs"][:, slot, :, :],
            di["xfm"].ap().rearrange("(c p) t -> p c t", p=128)[:, :, s * SUB: s * SUB + HTOK])
        return plan.val("g")

    evac_rr = [0]

    def evac_copy(b, out_ap_fn):
        tv = plan.val("t")
        eng = "vector" if (evac_rr[0] % 2 == 0) else "scalar"
        evac_rr[0] += 1
        wt(eng, "t", tv)
        if eng == "vector":
            op(eng, lambda e, S, b_=b, f=out_ap_fn: e.tensor_copy(*f(b_)), inc="v")
        else:
            op(eng, lambda e, S, b_=b, f=out_ap_fn: e.copy(*f(b_)), inc="s")
        banks.release(b, eng)

    def emit_proj(s, xready, wq_pre=None, wk_pre=None):
        slot = s % 2

        def proj_fm(dstn, wslot, ocs, own_only, nlist):
            for oc in ocs:
                for (a, c) in nlist:
                    b = banks.acquire()
                    for kc in range(8):
                        op("tensor", lambda e, S, b_=b, kc_=kc, oc_=oc, ws=wslot, sl=slot,
                           a_=a, c_=c, oo=own_only: e.matmul(
                               pbh[b_][:, 0:c_ - a_],
                               sb["wring"][:, ws, kc_, oc_ * 128: oc_ * 128 + 128],
                               (sb["xfs"][:, sl, kc_, HALO:HTOK] if oo
                                else sb["xfs"][:, sl, kc_, :])[:, a_:c_],
                               start=(kc_ == 0), stop=(kc_ == 7)),
                           inc=("t" if kc == 7 else None))
                    evac_copy(b, lambda b_, oc_=oc, a_=a, c_=c, d=dstn:
                              (sb[d][:, oc_, a_:c_], pbh[b_][:, 0:c_ - a_]))

        wq_slot, wq_rdy = wq_pre if wq_pre is not None else load_wring("wq", 8, 1024)
        wk_slot, wk_rdy = wk_pre if wk_pre is not None else load_wring("wk", 8, 1024)
        wt("tensor", "g", max(xready, wq_rdy))
        proj_fm("qsb", wq_slot, range(8), True, [(0, 512)])
        wring_lastuse[wq_slot] = plan.val("t")
        wv_slot, wv_rdy = load_wring("wv", 8, 1024)
        wt("tensor", "g", wk_rdy)
        proj_fm("ksb", wk_slot, range(8), False, [(0, 512), (512, 768)])
        wring_lastuse[wk_slot] = plan.val("t")
        wgq_slot, wgq_rdy = load_wring("wgq", 8, 512)
        wt("tensor", "g", wv_rdy)
        for tc in range(6):
            for half in range(2):
                b = banks.acquire()
                for kc in range(8):
                    op("tensor", lambda e, S, b_=b, kc_=kc, tc_=tc, h_=half, sl=slot,
                       ws=wv_slot: e.matmul(
                        pbh[b_][:, 0:512],
                        sb["xfs"][:, sl, kc_, tc_ * 128: tc_ * 128 + 128],
                        sb["wring"][:, ws, kc_, h_ * 512: h_ * 512 + 512],
                        start=(kc_ == 0), stop=(kc_ == 7)),
                       inc=("t" if kc == 7 else None))
                evac_copy(b, lambda b_, tc_=tc, h_=half:
                          (sb["vsb"].ap().rearrange("p t (h c) -> p t h c", c=65)[:, tc_, h_ * 8:h_ * 8 + 8, 0:64],
                           pbh[b_][:, 0:512].rearrange("p (h d) -> p h d", d=64)))
        wring_lastuse[wv_slot] = plan.val("t")
        if s == 0:
            op("vector", lambda e, S: e.memset(
                sb["vsb"].ap().rearrange("p t (h c) -> p t h c", c=65)[:, :, :, 64:65], 1.0), inc="v")
        wt("tensor", "g", wgq_rdy)
        proj_fm("gqsb", wgq_slot, range(4), True, [(0, 512)])
        wring_lastuse[wgq_slot] = plan.val("t")
        xfs_lastuse[slot] = plan.val("t")
        marks[("pev", s)] = (plan.val("v"), plan.val("s"))
        if s == 0:
            wslots["wgk"] = load_wring("wgk", 8, 512)
            wslots["wgv"] = load_wring("wgv", 8, 512)

        if KDBG and s == 0:
            wt("sync", "v")
            dma("sync", dbg["d_q"].ap()[:, :, :], sb["qsb"][:, :, :])
            dma("sync", dbg["d_k"].ap()[:, :, :], sb["ksb"][:, :, :])
            dma("sync", dbg["d_v"].ap()[:, :, :], sb["vsb"][:, :, :])
            dma("sync", dbg["d_gq"].ap()[:, :, :], sb["gqsb"][:, :, :])

    def emit_gkgv():
        ws_gk = wslots["wgk"]
        ws_gv = wslots["wgv"]
        wt("vector", "y", agload_done)
        op("vector", lambda e, S: e.tensor_copy(
            sb["pooledbf"].ap().rearrange("p c n -> p (c n)"),
            sb["pooled"].ap().rearrange("p c n -> p (c n)")), inc="v")
        wt("tensor", "g", ws_gv[1])
        wt("tensor", "v")
        for dstn, wslot in (("gksb", ws_gk[0]), ("gvtmp", ws_gv[0])):
            for oc in range(4):
                b = banks.acquire()
                for kc in range(8):
                    op("tensor", lambda e, S, b_=b, kc_=kc, oc_=oc, ws=wslot: e.matmul(
                        pbh[b_][:, 0:NLAT],
                        sb["wring"][:, ws, kc_, oc_ * 128: oc_ * 128 + 128],
                        sb["pooledbf"][:, kc_, :],
                        start=(kc_ == 0), stop=(kc_ == 7)),
                       inc=("t" if kc == 7 else None))
                tv = plan.val("t")
                wt("vector", "t", tv)
                op("vector", lambda e, S, b_=b, oc_=oc, d=dstn: e.tensor_copy(
                    sb[d][:, oc_, :], pbh[b_][:, 0:NLAT]), inc="v")
                banks.release(b, "vector")
        wring_lastuse[wslots["wgk"][0]] = plan.val("t")
        wring_lastuse[wslots["wgv"][0]] = plan.val("t")
        op("vector", lambda e, S: e.memset(
            sb["gvp"].ap().rearrange("n (h c) -> n h c", c=65)[:, :, 64:65], 1.0), inc="v")
        for pt in range(4):
            b = banks.acquire()
            wt("tensor", "v")
            op("tensor", lambda e, S, b_=b, pt_=pt: e.transpose(
                pbig.bitcast(BF16)[0:64, b_, 0:128], sb["gvtmp"][:, pt_, :], sb["ident"][:, :]), inc="t")
            tv = plan.val("t")
            wt("vector", "t", tv)
            op("vector", lambda e, S, b_=b, pt_=pt: e.tensor_copy(
                sb["gvp"].ap().rearrange("n (h c) -> n h c", c=65)[0:64, 2 * pt_:2 * pt_ + 2, 0:64],
                pbig.bitcast(BF16)[0:64, b_, 0:128].rearrange("n (h d) -> n h d", d=64)), inc="v")
            banks.release(b, "vector")
        if KDBG:
            wt("sync", "v")
            dma("sync", dbg["d_pool"].ap()[:, :, :], sb["pooled"][:, :, :])
            dma("sync", dbg["d_gk"].ap()[:, :, :], sb["gksb"][:, :, :])
            dma("sync", dbg["d_gvp"].ap()[:, :], sb["gvp"][:, :])

    def emit_remote(s):
        if (s - 1, "C") in dump_marks:
            wt("vector", "y", dump_marks[(s - 1, "C")])
        wt("tensor", "v", marks[("pev", s)][0])
        wt("tensor", "s", marks[("pev", s)][1])
        if (s - 1, "C") in dump_marks:
            wt("vector", "y", dump_marks[(s - 1, "C")])
            wt("scalar", "y", dump_marks[(s - 1, "C")])
        mask_g = {}

        def rem_pv(lh):
            bt, i4 = divmod(lh, 4)
            wt("tensor", "v", mask_g[bt])
            b2 = banks.acquire()
            op("tensor", lambda e, S, b_=b2, lh_=lh, bt_=bt, i_=i4: e.matmul(
                pbh[b_][0:65, 0:512],
                sb["gvp"][0:64, lh_ * 65: lh_ * 65 + 65],
                sb["ersb"][0:64, bt_, i_, :],
                start=True, stop=True), inc="t")
            evac_copy(b2, lambda b_, lh_=lh: (sb["rems"][:, lh_, :], pbh[b_][0:65, 0:512]))

        def lsc_batch(bt):
            banks.align4()
            bs = [banks.acquire() for _ in range(4)]
            for i, lh in enumerate(range(4 * bt, 4 * bt + 4)):
                base = 64 * (lh % 2)
                op("tensor", lambda e, S, b_=bs[i], lh_=lh, ba=base: e.matmul(
                    pbh[b_][0:64, 0:512],
                    sb["gksb"][ba:ba + 64, lh_ // 2, :],
                    sb["gqsb"][ba:ba + 64, lh_ // 2, :],
                    start=True, stop=True), inc=("t" if i == 3 else None))
            tv = plan.val("t")
            wt("scalar", "t", tv)
            op("scalar", lambda e, S, b0=bs[0], bt_=bt: e.activation(
                sb["ersb"][0:64, bt_, :, :],
                pbig[0:64, b0:b0 + 4, :],
                AF.Exp, scale=0.125), inc="s")
            for b_ in bs:
                banks.release(b_, "scalar")
            sv = plan.val("s")
            wt("vector", "s", sv)
            for i in range(4):
                op("vector", lambda e, S, bt_=bt, i_=i, si=s: e.tensor_tensor(
                    sb["ersb"][0:64, bt_, i_, :], sb["ersb"][0:64, bt_, i_, :],
                    sb["lmask"][0:64, si * SUB:(si + 1) * SUB], AX.mult), inc="v")
            mask_g[bt] = plan.val("v")

        lsc_batch(0)
        lsc_batch(1)
        for lh in range(4):
            rem_pv(lh)
        for lh in range(4, 8):
            rem_pv(lh)
        if (s - 1,) in load_marks:
            wt("sync", "y", load_marks[(s - 1,)])
        wt("sync", "v", plan.val("v"))
        wt("sync", "s", plan.val("s"))
        dma("sync", scrB.ap()[:, :, :], sb["rems"][:, :, :])
        dump_marks[(s, "B")] = plan.val("y")

    def emit_local(s):
        wt("tensor", "v", marks[("pev", s)][0])
        wt("tensor", "s", marks[("pev", s)][1])
        if (s - 1, "A") in dump_marks:
            wt("vector", "y", dump_marks[(s - 1, "A")])
            wt("scalar", "y", dump_marks[(s - 1, "A")])
        for b8 in range(8):
            fm = banks.freemark[b8]
            if fm is not None:
                for eng_s, val in (fm if isinstance(fm, list) else [fm]):
                    wt("tensor", eng_s, val)
        NU = 16
        exp_mark = {}
        mask_mark = {}
        pvev_mark = {}

        def emit_scores(u):
            hp, blk = divmod(u, 2)
            p = u % 2
            if u - 2 in exp_mark:
                wt("tensor", "s", exp_mark[u - 2])
            tiles = [(hi, kc) for hi in range(2) for kc in range(4)
                     if hi * 2 + kc // 2 != 0] + [(0, 0), (0, 1)]
            n_mm = 0
            for hi, kc in tiles:
                n_mm += 1
                if n_mm == 7 and u - 2 in pvev_mark:
                    wt("tensor", pvev_mark[u - 2][0], pvev_mark[u - 2][1])
                h = hp * 2 + hi
                base = 64 * (h % 2)
                bank = 4 * p + hi * 2 + kc // 2
                c0 = (kc % 2) * 256
                kt = 2 * blk + kc
                op("tensor", lambda e, S, bk=bank, cc=c0, h_=h, ba=base, kt_=kt, bl=blk: e.matmul(
                    pbig[:, bk, cc:cc + 256],
                    sb["ksb"][ba:ba + 64, h_ // 2, kt_ * 128: kt_ * 128 + 128],
                    sb["qsb"][ba:ba + 64, h_ // 2, bl * 256: bl * 256 + 256],
                    start=True, stop=True),
                   inc=("t" if n_mm == 8 else None))
            tv = plan.val("t")
            es = u % 2
            wt("scalar", "t", tv)
            op("scalar", lambda e, S, p_=p, es_=es: e.activation(
                sb["esb"].ap().rearrange("x r h k q -> x r h (k q)")[:, es_, :, :],
                pbig[:, 4 * p_:4 * p_ + 4, :],
                AF.Exp, scale=0.125), inc="s")
            exp_mark[u] = plan.val("s")
            sv = plan.val("s")
            wt("vector", "s", sv)
            mk = "maskb0" if (s == 0 and blk == 0) else "masknm"
            for hi in range(2):
                op("vector", lambda e, S, es_=es, hi_=hi, m=mk: e.tensor_tensor(
                    sb["esb"].ap().rearrange("x r h k q -> x r h (k q)")[:, es_, hi_, :],
                    sb["esb"].ap().rearrange("x r h k q -> x r h (k q)")[:, es_, hi_, :],
                    sb[m][:, :], AX.mult), inc="v")
            mask_mark[u] = plan.val("v")

        def emit_pv(u):
            hp, blk = divmod(u, 2)
            p = u % 2
            es = u % 2
            bank = 4 * p
            wt("tensor", "v", mask_mark[u])
            for hi in range(2):
                h = hp * 2 + hi
                c0 = hi * 256
                for kc in range(4):
                    kt = 2 * blk + kc
                    op("tensor", lambda e, S, bk=bank, cc=c0, h_=h, kt_=kt, es_=es, hi_=hi, kc_=kc: e.matmul(
                        pbig[0:65, bk, cc:cc + 256],
                        sb["vsb"][:, kt_, h_ * 65: h_ * 65 + 65],
                        sb["esb"][:, es_, hi_, kc_, :],
                        start=(kc_ == 0), stop=(kc_ == 3)),
                       inc=("t" if (hi == 1 and kc == 3) else None))
            tv = plan.val("t")
            eng = "vector" if (u % 2 == 0) else "scalar"
            wt(eng, "t", tv)
            if u in (8, 9):
                wt(eng, "y", dump_marks[(s, "A0")])
            if eng == "vector":
                op(eng, lambda e, S, bk=bank, hp_=hp, bl=blk: e.tensor_copy(
                    sb["pvs"][:, (2 * hp_) % 8: (2 * hp_) % 8 + 2, bl, :],
                    pbig[0:65, bk, :].rearrange("d (h q) -> d h q", q=256)), inc="v")
            else:
                op(eng, lambda e, S, bk=bank, hp_=hp, bl=blk: e.copy(
                    sb["pvs"][:, (2 * hp_) % 8: (2 * hp_) % 8 + 2, bl, :],
                    pbig[0:65, bk, :].rearrange("d (h q) -> d h q", q=256)), inc="s")
            pvev_mark[u] = (PSEM[eng], plan.val(PSEM[eng]))

        for u in range(NU):
            emit_scores(u)
            if u >= 1:
                emit_pv(u - 1)
            if u == 8:
                wt("sync", "v", plan.val("v"))
                wt("sync", "s", plan.val("s"))
                dma("sync", scrA.ap()[:, 0:8, :, :], sb["pvs"][:, :, :, :])
                dump_marks[(s, "A0")] = plan.val("y")
        emit_pv(NU - 1)
        vve = plan.val("v")
        sve = plan.val("s")
        for b8 in range(8):
            banks.freemark[b8] = [("v", vve), ("s", sve)]

        wt("sync", "v", vve)
        wt("sync", "s", sve)
        dma("sync", scrA.ap()[:, 8:16, :, :], sb["pvs"][:, :, :, :])
        dump_marks[(s, "A")] = plan.val("y")

    def emit_glog(s):
        wt("vector", "y", dump_marks[(s, "B")])
        for pt in range(8):
            b = banks.acquire()
            op("tensor", lambda e, S, b_=b, pt_=pt: e.matmul(
                pbh[b_][0:4, 0:512],
                sb["gstat"][:, pt_ * 4: pt_ * 4 + 4],
                sb["qsb"][:, pt_, :],
                start=True, stop=True), inc="t")
            tv = plan.val("t")
            wt("vector", "t", tv)
            op("vector", lambda e, S, b_=b, pt_=pt: e.tensor_copy(
                sb["rems"][0:4, pt_, :], pbh[b_][0:4, 0:512]), inc="v")
            banks.release(b, "vector")
        wt("sync", "v", plan.val("v"))
        for k in range(2):
            for hi in range(2):
                dma("sync",
                    scrC.ap()[k:k + 1, :, hi, :],
                    sb["rems"][2 * k + hi: 2 * k + hi + 1, 0:8, :])
        dump_marks[(s, "C")] = plan.val("y")

    def emit_gather_loads(s):
        wt("sync", "y", dump_marks[(s, "C")])
        if ("cmb", s - 1) in marks:
            wt("sync", "v", marks[("cmb", s - 1)])
        for hi in range(2):
            dma("sync",
                sb["pvfm"][64 * hi: 64 * hi + 64, :, :],
                scrA.ap().rearrange("d (pt hi) b q -> d pt hi (b q)", hi=2)[0:64, :, hi, :])
        dma("sync",
            sb["denl"][:, :],
            scrA.ap().rearrange("d h b q -> d h (b q)")[64, :, :])
        for hi in range(2):
            dma("sync",
                sb["remstk"][64 * hi: 64 * hi + 64, :, :],
                scrB.ap().rearrange("d (pt hi) q -> d pt hi q", hi=2)[0:64, :, hi, :])
        dma("sync",
            sb["denr"][:, :],
            scrB.ap()[64, :, :])
        wt("gpsimd", "y", dump_marks[(s, "C")])
        dma("gpsimd",
            sb["glr_l"][:, :],
            scrC.ap()[0, :, :, :].rearrange("pt hi q -> (pt hi) q"))
        dma("gpsimd",
            sb["glr_r"][:, :],
            scrC.ap()[1, :, :, :].rearrange("pt hi q -> (pt hi) q"))
        glgr_rdy = plan.val("g")
        gathered = plan.val("y")
        load_marks[(s,)] = gathered

        if KDBG and s == 0:
            wt("sync", "y", gathered)
            dma("sync", dbg["d_pvfm"].ap()[:, :, :], sb["pvfm"][:, :, :])
            dma("sync", dbg["d_denl"].ap()[:, :], sb["denl"][:, :])
            dma("sync", dbg["d_remstk"].ap()[:, :, :], sb["remstk"][:, :, :])
            dma("sync", dbg["d_denr"].ap()[:, :], sb["denr"][:, :])
        return glgr_rdy, gathered

    def emit_coeffs(s, glgr_rdy, gathered):
        wt("scalar", "g", glgr_rdy)
        wt("scalar", "y", gathered)
        for gn in ("glr_l", "glr_r"):
            op("scalar", lambda e, S, g_=gn: e.activation(sb[g_][:, :], sb[g_][:, :],
                                                          AF.Exp, scale=-1.0), inc="s")
        sv = plan.val("s")
        wt("vector", "s", sv)
        for gn in ("glr_l", "glr_r"):
            op("vector", lambda e, S, g_=gn: e.tensor_scalar(sb[g_][:, :], sb[g_][:, :],
                                                             1.0, None, AX.add), inc="v")
            op("vector", lambda e, S, g_=gn: e.reciprocal(sb[g_][:, :], sb[g_][:, :]), inc="v")
        wt("vector", "y", gathered)
        op("vector", lambda e, S: e.reciprocal(sb["recl"][:, :], sb["denl"][:, :]), inc="v")
        op("vector", lambda e, S: e.tensor_tensor(sb["clbf"][:, :], sb["glr_l"][:, :],
                                                  sb["recl"][:, :], AX.mult), inc="v")
        op("vector", lambda e, S: e.tensor_copy(sb["crbf"][:, :], sb["glr_r"][:, :]), inc="v")
        if KDBG and s == 0:
            wt("sync", "v")
            dma("sync", dbg["d_gll"].ap()[:, :], sb["glr_l"][:, :])
            dma("sync", dbg["d_glr"].ap()[:, :], sb["glr_r"][:, :])
        op("vector", lambda e, S: e.reciprocal(sb["recl"][0:8, :], sb["denr"][:, :]), inc="v")
        op("vector", lambda e, S: e.tensor_copy(sb["recrbf"][:, :], sb["recl"][0:8, :]), inc="v")
        marks[("coef", s)] = plan.val("v")

    def emit_combine(s):
        rout_slot, rout_rdy = wslots[("rout", s)]
        wt("tensor", "v", marks[("coef", s)])
        for pt in range(4):
            b = banks.acquire()
            op("tensor", lambda e, S, b_=b, pt_=pt: e.matmul(
                pbh[b_][:, 0:512],
                sb["sel8"][:, pt_ * 128: pt_ * 128 + 128],
                sb["recrbf"][:, :],
                start=True, stop=True), inc="t")
            tv = plan.val("t")
            wt("vector", "t", tv)
            op("vector", lambda e, S, b_=b, pt_=pt: e.tensor_tensor(
                sb["remnorm"][:, pt_, :], sb["remstk"][:, pt_, :],
                pbh[b_][:, 0:512], AX.mult), inc="v")
            banks.release(b, "vector")
        wt("tensor", "g", rout_rdy)
        wt("tensor", "v", plan.val("v"))
        for pt in range(8):
            bl_ = banks.acquire()
            op("tensor", lambda e, S, b_=bl_, pt_=pt: e.matmul(
                pbh[b_][:, 0:512],
                sb["sel16"][:, pt_ * 128: pt_ * 128 + 128],
                sb["clbf"][:, :],
                start=True, stop=True), inc="t")
            tvl = plan.val("t")
            bc_ = banks.acquire()
            op("tensor", lambda e, S, b_=bc_, pt_=pt: e.matmul(
                pbh[b_][:, 0:512],
                sb["sel16"][:, pt_ * 128: pt_ * 128 + 128],
                sb["crbf"][:, :],
                start=True, stop=True), inc="t")
            tvc = plan.val("t")
            br_ = banks.acquire()
            for kc in range(4):
                op("tensor", lambda e, S, b_=br_, kc_=kc, pt_=pt, rs=rout_slot: e.matmul(
                    pbh[b_][:, 0:512],
                    sb["wring"][:, rs, kc_, pt_ * 128: pt_ * 128 + 128],
                    sb["remnorm"][:, kc_, :],
                    start=(kc_ == 0), stop=(kc_ == 3)),
                   inc=("t" if kc == 3 else None))
            tvr = plan.val("t")
            wt("vector", "t", tvl)
            op("vector", lambda e, S, b_=bl_, pt_=pt: e.tensor_tensor(
                sb["mixed"][:, pt_, :], sb["pvfm"][:, pt_, :], pbh[b_][:, 0:512],
                AX.mult), inc="v")
            banks.release(bl_, "vector")
            wt("vector", "t", tvc)
            op("vector", lambda e, S, b_=bc_: e.tensor_copy(
                sb["crsb"][:, :], pbh[b_][:, 0:512]), inc="v")
            banks.release(bc_, "vector")
            wt("vector", "t", tvr)
            op("vector", lambda e, S, b_=br_: e.tensor_tensor(
                sb["tmpv"][:, :], pbh[b_][:, 0:512], sb["crsb"][:, :], AX.mult), inc="v")
            banks.release(br_, "vector")
            op("vector", lambda e, S, pt_=pt: e.tensor_tensor(
                sb["mixed"][:, pt_, :], sb["mixed"][:, pt_, :], sb["tmpv"][:, :],
                AX.add), inc="v")
        wring_lastuse[rout_slot] = plan.val("t")
        marks[("cmb", s)] = plan.val("v")
        if KDBG and s == 0:
            wt("sync", "v")
            dma("sync", dbg["d_mixed"].ap()[:, :, :], sb["mixed"][:, :, :])

    def emit_woutln1(s):
        wout_slot, wout_rdy = wslots[("wout", s)]
        wt("tensor", "g", wout_rdy)
        wt("tensor", "v", marks[("cmb", s)])
        for qt in range(QPS):
            qt_g = s * QPS + qt
            tslot = qt_g % 2
            if xtmb_lastuse[tslot] is not None:
                wt("sync", "v", xtmb_lastuse[tslot])
            dma("sync", sb["xtmb"][:, tslot, :], di["xtm"].ap()[qt_g * 128:(qt_g + 1) * 128, :])
            xtrdy = plan.val("y")

            bA = banks.acquire()
            bB = banks.acquire()
            for kc in range(8):
                for half, b in ((0, bA), (1, bB)):
                    op("tensor", lambda e, S, b_=b, kc_=kc, qt_=qt, h_=half, ws=wout_slot: e.matmul(
                        pbh[b_][:, 0:512],
                        sb["mixed"][:, kc_, qt_ * 128: qt_ * 128 + 128],
                        sb["wring"][:, ws, kc_, h_ * 512: h_ * 512 + 512],
                        start=(kc_ == 0), stop=(kc_ == 7)),
                       inc=("t" if (kc == 7 and half == 1) else None))
            tv = plan.val("t")
            wt("vector", "t", tv)
            wt("vector", "y", xtrdy)
            op("vector", lambda e, S, b_=bA, ts=tslot: e.tensor_tensor(
                sb["x2pre"][:, 0:512], sb["xtmb"][:, ts, 0:512], pbh[b_][:, 0:512], AX.add), inc="v")
            op("vector", lambda e, S, b_=bB, ts=tslot: e.tensor_tensor(
                sb["x2pre"][:, 512:1024], sb["xtmb"][:, ts, 512:1024], pbh[b_][:, 0:512], AX.add), inc="v")
            banks.release(bA, "vector")
            banks.release(bB, "vector")
            xtmb_lastuse[tslot] = plan.val("v")
            if KDBG and qt_g == 0:
                wt("sync", "v")
                dma("sync", dbg["d_x2pre"].ap()[:, :], sb["x2pre"][:, :])
            op("vector", lambda e, S: e.reduce_sum(sb["ssum"][:, :], sb["x2pre"][:, :], AXX), inc="v")
            vv = plan.val("v")
            wt("scalar", "v", vv)
            op("scalar", lambda e, S: e.activation(sb["junk"][:, :], sb["x2pre"][:, :],
                                                   AF.Square, accum_out=sb["ssq"][:, :]), inc="s")
            sv = plan.val("s")
            wt("vector", "s", sv)
            op("vector", lambda e, S: e.tensor_scalar(sb["nmu"][:, :], sb["ssum"][:, :],
                                                      -1.0 / D, None, AX.mult), inc="v")
            op("vector", lambda e, S: e.tensor_scalar(sb["msq"][:, :], sb["ssq"][:, :],
                                                      1.0 / D, None, AX.mult), inc="v")
            op("vector", lambda e, S: e.tensor_scalar(sb["mu"][:, :], sb["ssum"][:, :],
                                                      1.0 / D, None, AX.mult), inc="v")
            vspace()
            op("vector", lambda e, S: e.scalar_tensor_tensor(
                sb["var"][:, :], sb["mu"][:, :], sb["nmu"][:, 0:1], sb["msq"][:, :],
                AX.mult, AX.add), inc="v")
            vspace()
            op("vector", lambda e, S: e.tensor_scalar(sb["var"][:, :], sb["var"][:, :],
                                                      1e-5, None, AX.add), inc="v")
            vv = plan.val("v")
            wt("scalar", "v", vv)
            op("scalar", lambda e, S: e.activation(sb["lnv"][:, :], sb["var"][:, :],
                                                   AF.Ln), inc="s")
            op("scalar", lambda e, S: e.activation(sb["aa"][:, :], sb["lnv"][:, :],
                                                   AF.Exp, scale=-0.5), inc="s")
            sv = plan.val("s")
            wt("vector", "s", sv)
            op("vector", lambda e, S: e.tensor_scalar(sb["bb"][:, :], sb["mu"][:, :],
                                                      sb["aa"][:, 0:1], -1.0, AX.mult, AX.mult), inc="v")
            vspace()
            if KDBG and qt_g == 0:
                wt("sync", "s")
                wt("sync", "v")
                for i_, nm_ in enumerate(("ssum", "ssq", "mu", "msq", "mu2", "var", "lnv", "aa")):
                    dma("sync", dbg["d_stats"].ap()[i_, :, :], sb[nm_][:, :])
            if x2tm_lastuse[0] is not None:
                wt("vector", "y", x2tm_lastuse[0])
            op("vector", lambda e, S: e.tensor_scalar(
                sb["x2tm"][:, 0, :], sb["x2pre"][:, :],
                sb["aa"][:, 0:1], sb["bb"][:, 0:1], AX.mult, AX.add), inc="v")
            wt("sync", "v")
            dma("sync", x2scr.ap()[qt_g * 128:(qt_g + 1) * 128, :], sb["x2tm"][:, 0, :])
            x2tm_lastuse[0] = plan.val("y")
            spill_marks.append(plan.val("y"))
        wring_lastuse[wout_slot] = plan.val("t")

    xrdy0 = load_xfs(0)
    emit_proj(0, xrdy0)
    emit_gkgv()
    for s in range(NSUB):
        emit_remote(s)
        emit_local(s)
        emit_glog(s)
        if s + 1 < NSUB:
            xrdy = load_xfs(s + 1)
            wq_pre = load_wring("wq", 8, 1024)
            wk_pre = load_wring("wk", 8, 1024)
        glgr_rdy, gathered = emit_gather_loads(s)
        emit_coeffs(s, glgr_rdy, gathered)
        if s + 1 < NSUB:
            emit_proj(s + 1, xrdy, wq_pre, wk_pre)
        wslots[("rout", s)] = load_wring("rout", 4, 1024)
        wslots[("wout", s)] = load_wring("wout", 8, 1024)
        emit_combine(s)
        emit_woutln1(s)

    spills_done = plan.val("y")
    if KDBG:
        wt("sync", "y", spills_done)
        dma("sync", dbg["d_x2"].ap()[:, :], x2scr.ap()[:, :])

    marks = {s: plan.val(s) for s in ("t", "v", "s", "g", "y")}
    for eng in ENGS:
        for s_, v_ in marks.items():
            wt(eng, s_, v_)

    A.close()

    Bst = ExitStack()
    for name, shape, dt in (
        ("x2fm", [128, 8, T], BF16),
        ("xbf", [128, 2, D], BF16),
        ("wE", [128, 2, 8, D], BF16),
        ("acc", [128, NQT, D], F32),
        ("gfm", [8, T], F32),
        ("uu", [128, EE], F32),
        ("usum", [128, 1], F32),
        ("t1", [128, EE], F32),
        ("g2", [128, EE], F32),
        ("m1", [128, 1], F32),
        ("m2", [128, 1], F32),
        ("wtm", [128, NQT, EE], F32),
        ("wfm", [8, T], BF16),
        ("xt2b", [128, 2, D], F32),
        ("outb", [128, 2, D], F32),
        ("x3pre", [128, D], F32),
    ):
        sbuf_on(Bst, name, shape, dt)

    xbf_lastuse = [None, None]
    for qt in range(NQT):
        slot = qt % 2
        if xbf_lastuse[slot] is not None:
            wt("gpsimd", "t", xbf_lastuse[slot])
        dma("gpsimd", sb["xbf"][:, slot, :], x2scr.ap()[qt * 128:(qt + 1) * 128, :])
        xrdy = plan.val("g")
        wt("tensor", "g", xrdy)
        for fc in range(8):
            b = banks.acquire()
            op("tensor", lambda e, S, b_=b, sl=slot, fc_=fc: e.transpose(
                pbig.bitcast(BF16)[:, b_, 0:128], sb["xbf"][:, sl, fc_ * 128: fc_ * 128 + 128],
                sb["ident"][:, :]), inc="t")
            tv = plan.val("t")
            wt("vector", "t", tv)
            op("vector", lambda e, S, b_=b, fc_=fc, qt_=qt: e.tensor_copy(
                sb["x2fm"][:, fc_, qt_ * 128:(qt_ + 1) * 128], pbig.bitcast(BF16)[:, b_, 0:128]), inc="v")
            banks.release(b, "vector")
        xbf_lastuse[slot] = plan.val("t")

    wt("tensor", "v")
    for nck in range(4):
        b = banks.acquire()
        for kc in range(8):
            op("tensor", lambda e, S, b_=b, kc_=kc, n_=nck: e.matmul(
                pbh[b_][0:8, 0:512],
                sb["gwT"][:, kc_, :],
                sb["x2fm"][:, kc_, n_ * 512:(n_ + 1) * 512],
                start=(kc_ == 0), stop=(kc_ == 7)), inc=("t" if kc == 7 else None))
        tv = plan.val("t")
        wt("vector", "t", tv)
        op("vector", lambda e, S, b_=b, n_=nck: e.tensor_copy(
            sb["gfm"][:, n_ * 512:(n_ + 1) * 512], pbh[b_][0:8, 0:512]), inc="v")
        banks.release(b, "vector")
    for qt in range(NQT):
        b = banks.acquire()
        wt("tensor", "v")
        op("tensor", lambda e, S, b_=b, qt_=qt: e.transpose(
            pbh[b_][:, 0:8], sb["gfm"][0:8, qt_ * 128:(qt_ + 1) * 128],
            sb["identf"][0:8, 0:8]), inc="t")
        tv = plan.val("t")
        wt("vector", "t", tv)
        op("vector", lambda e, S, b_=b: e.tensor_tensor(
            sb["uu"][:, :], pbh[b_][:, 0:8], sb["gbrep"][:, :], AX.add), inc="v")
        banks.release(b, "vector")
        vv = plan.val("v")
        wt("scalar", "v", vv)
        op("scalar", lambda e, S: e.activation(sb["uu"][:, :], sb["uu"][:, :],
                                               AF.Exp, accum_out=sb["usum"][:, :]), inc="s")
        sv = plan.val("s")
        wt("vector", "s", sv)
        op("vector", lambda e, S: e.reduce_max(sb["m1"][:, :], sb["uu"][:, :], AXX), inc="v")
        vspace()
        op("vector", lambda e, S: e.tensor_scalar(sb["t1"][:, :], sb["uu"][:, :],
                                                  sb["m1"][:, 0:1], None, AX.is_ge), inc="v")
        vspace()
        op("vector", lambda e, S: e.scalar_tensor_tensor(
            sb["g2"][:, :], sb["t1"][:, :], -1e30, sb["uu"][:, :], AX.mult, AX.add), inc="v")
        vspace()
        op("vector", lambda e, S: e.reduce_max(sb["m2"][:, :], sb["g2"][:, :], AXX), inc="v")
        vspace()
        op("vector", lambda e, S: e.tensor_scalar(sb["t1"][:, :], sb["uu"][:, :],
                                                  sb["m2"][:, 0:1], None, AX.is_ge), inc="v")
        vspace()
        op("vector", lambda e, S: e.tensor_tensor(sb["g2"][:, :], sb["uu"][:, :],
                                                  sb["t1"][:, :], AX.mult), inc="v")
        op("vector", lambda e, S: e.reciprocal(sb["m1"][:, :], sb["usum"][:, :]), inc="v")
        vspace()
        op("vector", lambda e, S, qt_=qt: e.tensor_scalar(
            sb["wtm"][:, qt_, :], sb["g2"][:, :], sb["m1"][:, 0:1], None, AX.mult), inc="v")
        b2 = banks.acquire()
        wt("tensor", "v")
        op("tensor", lambda e, S, b_=b2, qt_=qt: e.transpose(
            pbh[b_][0:8, 0:128], sb["wtm"][:, qt_, :], sb["identf"][:, :]), inc="t")
        tv = plan.val("t")
        wt("vector", "t", tv)
        op("vector", lambda e, S, b_=b2, qt_=qt: e.tensor_copy(
            sb["wfm"][:, qt_ * 128:(qt_ + 1) * 128], pbh[b_][0:8, 0:128]), inc="v")
        banks.release(b2, "vector")

    if KDBG:
        wt("sync", "v")
        dma("sync", dbg["d_w"].ap()[:, :, :], sb["wtm"][:, :, :])
        dma("sync", dbg["d_x2fm"].ap()[:, :, :], sb["x2fm"][:, :, :])

    wt("tensor", "v")
    for qt in range(NQT):
        bA = banks.acquire()
        bB = banks.acquire()
        for half, b in ((0, bA), (1, bB)):
            op("tensor", lambda e, S, b_=b, qt_=qt, h_=half: e.matmul(
                pbh[b_][:, 0:512],
                sb["wfm"][:, qt_ * 128:(qt_ + 1) * 128],
                sb["ebsb"][:, h_ * 512:(h_ + 1) * 512],
                start=True, stop=True), inc="t")
        tv = plan.val("t")
        wt("vector", "t", tv)
        for half, b in ((0, bA), (1, bB)):
            op("vector", lambda e, S, b_=b, qt_=qt, h_=half: e.tensor_copy(
                sb["acc"][:, qt_, h_ * 512:(h_ + 1) * 512], pbh[b_][:, 0:512]), inc="v")
        banks.release(bA, "vector")
        banks.release(bB, "vector")

    wE_lastuse = [None, None]
    wE_rdy = [0, 0]
    xt2b_lastuse = [None, None]
    outb_lastuse = [None, None]

    def emit_ln2(qt):
        slot = qt % 2
        if xt2b_lastuse[slot] is not None:
            wt("sync", "v", xt2b_lastuse[slot])
        dma("sync", sb["xt2b"][:, slot, :], x2scr.ap()[qt * 128:(qt + 1) * 128, :])
        xrdy = plan.val("y")
        wt("vector", "y", xrdy)
        op("vector", lambda e, S, qt_=qt, sl=slot: e.tensor_tensor(
            sb["x3pre"][:, :], sb["acc"][:, qt_, :], sb["xt2b"][:, sl, :], AX.add), inc="v")
        xt2b_lastuse[slot] = plan.val("v")
        op("vector", lambda e, S: e.reduce_sum(sb["ssum"][:, :], sb["x3pre"][:, :], AXX), inc="v")
        vv = plan.val("v")
        wt("scalar", "v", vv)
        op("scalar", lambda e, S: e.activation(sb["junk"][:, :], sb["x3pre"][:, :],
                                               AF.Square, accum_out=sb["ssq"][:, :]), inc="s")
        sv = plan.val("s")
        wt("vector", "s", sv)
        op("vector", lambda e, S: e.tensor_scalar(sb["nmu"][:, :], sb["ssum"][:, :],
                                                  -1.0 / D, None, AX.mult), inc="v")
        op("vector", lambda e, S: e.tensor_scalar(sb["msq"][:, :], sb["ssq"][:, :],
                                                  1.0 / D, None, AX.mult), inc="v")
        op("vector", lambda e, S: e.tensor_scalar(sb["mu"][:, :], sb["ssum"][:, :],
                                                  1.0 / D, None, AX.mult), inc="v")
        vspace()
        op("vector", lambda e, S: e.scalar_tensor_tensor(
            sb["var"][:, :], sb["mu"][:, :], sb["nmu"][:, 0:1], sb["msq"][:, :],
            AX.mult, AX.add), inc="v")
        vspace()
        op("vector", lambda e, S: e.tensor_scalar(sb["var"][:, :], sb["var"][:, :],
                                                  1e-5, None, AX.add), inc="v")
        vv = plan.val("v")
        wt("scalar", "v", vv)
        op("scalar", lambda e, S: e.activation(sb["lnv"][:, :], sb["var"][:, :],
                                               AF.Ln), inc="s")
        op("scalar", lambda e, S: e.activation(sb["aa"][:, :], sb["lnv"][:, :],
                                               AF.Exp, scale=-0.5), inc="s")
        sv = plan.val("s")
        wt("vector", "s", sv)
        op("vector", lambda e, S: e.tensor_scalar(sb["bb"][:, :], sb["mu"][:, :],
                                                  sb["aa"][:, 0:1], -1.0, AX.mult, AX.mult), inc="v")
        vspace()
        if outb_lastuse[slot] is not None:
            wt("vector", "y", outb_lastuse[slot])
        op("vector", lambda e, S, sl=slot: e.tensor_scalar(
            sb["outb"][:, sl, :], sb["x3pre"][:, :],
            sb["aa"][:, 0:1], sb["bb"][:, 0:1], AX.mult, AX.add), inc="v")
        wt("sync", "v")
        dma("sync", y_out.ap()[qt * 128:(qt + 1) * 128, :], sb["outb"][:, slot, :])
        outb_lastuse[slot] = plan.val("y")

    for e_i in range(2):
        slot = e_i % 2
        dma("gpsimd", sb["wE"][:, slot, :, :],
            di["ewT"].ap()[e_i:e_i + 1, :, :].rearrange("z (c p) o -> p (z c) o", p=128))
        wE_rdy[slot] = plan.val("g")
    for e_i in range(EE):
        slot = e_i % 2
        wt("tensor", "g", wE_rdy[slot])
        for qt in range(NQT):
            bA = banks.acquire()
            bB = banks.acquire()
            for kc in range(8):
                for half, b in ((0, bA), (1, bB)):
                    op("tensor", lambda e, S, b_=b, kc_=kc, qt_=qt, sl=slot, h_=half: e.matmul(
                        pbh[b_][:, 0:512],
                        sb["x2fm"][:, kc_, qt_ * 128:(qt_ + 1) * 128],
                        sb["wE"][:, sl, kc_, h_ * 512:(h_ + 1) * 512],
                        start=(kc_ == 0), stop=(kc_ == 7)),
                       inc=("t" if (kc == 7 and half == 1) else None))
            tv = plan.val("t")
            wt("vector", "t", tv)
            for half, b in ((0, bA), (1, bB)):
                op("vector", lambda e, S, b_=b, qt_=qt, h_=half, ei=e_i: e.scalar_tensor_tensor(
                    sb["acc"][:, qt_, h_ * 512:(h_ + 1) * 512],
                    pbh[b_][:, 0:512],
                    sb["wtm"][:, qt_, ei:ei + 1],
                    sb["acc"][:, qt_, h_ * 512:(h_ + 1) * 512],
                    AX.mult, AX.add), inc="v")
            banks.release(bA, "vector")
            banks.release(bB, "vector")
            if e_i == EE - 1:
                if KDBG:
                    wt("sync", "v")
                    dma("sync", dbg["d_acc"].ap()[:, qt:qt + 1, :], sb["acc"][:, qt:qt + 1, :])
                if qt >= 1:
                    emit_ln2(qt - 1)
                if qt == NQT - 1:
                    emit_ln2(qt)
        wE_lastuse[slot] = plan.val("t")
        nxt = e_i + 2
        if nxt < EE:
            wt("gpsimd", "t", wE_lastuse[slot])
            dma("gpsimd", sb["wE"][:, slot, :, :],
                di["ewT"].ap()[nxt:nxt + 1, :, :].rearrange("z (c p) o -> p (z c) o", p=128))
            wE_rdy[slot] = plan.val("g")

    wt("sync", "y", plan.val("y"))

    with nc.Block() as block:
        @block.tensor
        def _(e):
            for fn in plan.ops["tensor"]:
                fn(e, sems)

        @block.vector
        def _(e):
            for fn in plan.ops["vector"]:
                fn(e, sems)

        @block.scalar
        def _(e):
            for fn in plan.ops["scalar"]:
                fn(e, sems)

        @block.gpsimd
        def _(e):
            for fn in plan.ops["gpsimd"]:
                fn(e, sems)

        @block.sync
        def _(e):
            for fn in plan.ops["sync"]:
                fn(e, sems)

    Bst.close()
    stack.close()
    return nc


def _host_masks():
    kk = np.arange(512)[:, None]
    qq = np.arange(256)[None, :]
    base = (kk > qq) & (kk <= qq + 256)
    masknm = base.astype(np.float32)
    maskb0 = (base & (kk >= 256)).astype(np.float32)

    def lay(m):
        return np.ascontiguousarray(
            m.reshape(4, 128, 256).transpose(1, 0, 2).reshape(128, 1024)
        ).astype(ml_dtypes.bfloat16)

    return lay(masknm), lay(maskb0)


def kernel(L0_attn_x, L0_attn_wq, L0_attn_wk, L0_attn_wv, L0_attn_wgq, L0_attn_wgk,
           L0_attn_wgv, L0_attn_r_out_w, L0_attn_gate_local_w, L0_attn_gate_remote_w,
           L0_attn_w_out, L0_ffn_gate_w, L0_ffn_gate_b, L0_ffn_expert_w, L0_ffn_expert_b):
    x = np.ascontiguousarray(np.asarray(L0_attn_x, dtype=np.float32)[0])
    xT = np.ascontiguousarray(x.T)

    f32 = lambda a: np.ascontiguousarray(np.asarray(a, dtype=np.float32))
    wq, wk, wv = f32(L0_attn_wq), f32(L0_attn_wk), f32(L0_attn_wv)
    wgq, wgk, wgv = f32(L0_attn_wgq), f32(L0_attn_wgk), f32(L0_attn_wgv)
    rout, wout = f32(L0_attn_r_out_w), f32(L0_attn_w_out)
    glw, grw = f32(L0_attn_gate_local_w), f32(L0_attn_gate_remote_w)
    gw, gb = f32(L0_ffn_gate_w), f32(L0_ffn_gate_b)
    ew, eb = f32(L0_ffn_expert_w), f32(L0_ffn_expert_b)

    ewT = np.ascontiguousarray(ew.transpose(0, 2, 1))
    gwT = np.ascontiguousarray(gw.T)
    gbrep = np.ascontiguousarray(np.broadcast_to(gb[None, :], (128, EE)))

    gstat = np.zeros((128, 8, 4), np.float32)
    for pt in range(8):
        gstat[0:64, pt, 0] = glw[2 * pt]
        gstat[64:128, pt, 1] = glw[2 * pt + 1]
        gstat[0:64, pt, 2] = grw[2 * pt]
        gstat[64:128, pt, 3] = grw[2 * pt + 1]

    masknm, maskb0 = _host_masks()
    ident = np.eye(128, dtype=np.float32).astype(ml_dtypes.bfloat16)
    identf = np.eye(128, dtype=np.float32)
    sel16 = np.zeros((16, EE, 128), np.float32)
    for pt in range(8):
        for p in range(128):
            sel16[2 * pt + (p >= 64), pt, p] = 1.0
    sel16 = np.ascontiguousarray(sel16.reshape(16, EE * 128)).astype(ml_dtypes.bfloat16)
    sel8 = np.zeros((8, 4, 128), np.float32)
    for pt in range(4):
        for p in range(128):
            sel8[2 * pt + (p >= 64), pt, p] = 1.0
    sel8 = np.ascontiguousarray(sel8.reshape(8, 4 * 128)).astype(ml_dtypes.bfloat16)

    nc = build_nc()

    in_maps = []
    for c in range(N_CORES):
        lo = c * T
        if c == 0:
            xfm = np.concatenate([np.zeros((D, HALO), np.float32), xT[:, 0:T]], axis=1)
        else:
            xfm = xT[:, lo - HALO: lo + T]
        xtm = x[lo: lo + T]
        gtok = np.arange(lo, lo + T)
        lm = (np.arange(NLAT)[:, None] <= (gtok[None, :] // CHK)).astype(np.float32)
        in_maps.append({
            "xfm": np.ascontiguousarray(xfm),
            "xtm": np.ascontiguousarray(xtm),
            "wq": wq, "wk": wk, "wv": wv,
            "wgq": wgq, "wgk": wgk, "wgv": wgv,
            "rout": rout, "wout": wout,
            "ewT": ewT, "eb": eb, "gwT": gwT, "gbrep": gbrep,
            "gstat": gstat,
            "masknm": masknm, "maskb0": (maskb0 if c == 0 else masknm),
            "lmask": np.ascontiguousarray(lm.astype(ml_dtypes.bfloat16)),
            "ident": ident, "identf": identf, "sel16": sel16, "sel8": sel8,
        })

    res = run_bass_kernel_spmd(nc, in_maps, core_ids=list(range(N_CORES)))
    kernel.last_results = res.results
    out = np.concatenate([res.results[c]["y"] for c in range(N_CORES)], axis=0)
    return out.reshape(1, SEQ, D)


# version 13
# speedup vs baseline: 1.2156x; 1.0528x over previous
import os
import numpy as np
import ml_dtypes
from contextlib import ExitStack

import concourse.bass as bass
import concourse.mybir as mybir
from concourse.bass_utils import run_bass_kernel_spmd

F32 = mybir.dt.float32
BF16 = mybir.dt.bfloat16
AX = mybir.AluOpType
AF = mybir.ActivationFunctionType
AXX = mybir.AxisListType.X

N_CORES = 8
D = 1024
SEQ = 16384
T = 2048
HALO = 256
SUB = 512
NSUB = T // SUB
HTOK = SUB + HALO
NH, HD = 16, 64
LHN = 8
NLAT, CHK = 64, 256
EE = 8
NQT = T // 128
QPS = SUB // 128

ENGS = ("tensor", "vector", "scalar", "gpsimd", "sync")
PSEM = {"tensor": "t", "vector": "v", "scalar": "s", "gpsimd": "g", "sync": "y"}


class Plan:
    def __init__(self):
        self.ops = {e: [] for e in ENGS}
        self.cnt = {}
        self.lastwait = {e: {} for e in ENGS}

    def sem(self, name):
        self.cnt[name] = 0

    def bump(self, name, n=1):
        self.cnt[name] += n
        return self.cnt[name]

    def val(self, name):
        return self.cnt[name]


def build_nc():
    nc = bass.Bass("TRN2", target_bir_lowering=False)
    plan = Plan()
    for s in ("t", "v", "s", "g", "y", "cc"):
        plan.sem(s)

    di = {}
    def inp(name, shape, dt=F32):
        di[name] = nc.dram_tensor(name, list(shape), dt, kind="ExternalInput")
        return di[name]

    inp("xfm", [D, HALO + T])
    inp("xtm", [T, D])
    inp("wq", [D, D]); inp("wk", [D, D]); inp("wv", [D, D])
    inp("wgq", [D, 512]); inp("wgk", [D, 512]); inp("wgv", [D, 512])
    inp("rout", [512, D]); inp("wout", [D, D])
    inp("ewT", [EE, D, D]); inp("eb", [EE, D])
    inp("gwT", [D, EE]); inp("gbrep", [128, EE])
    inp("gstat", [128, EE, 4])
    inp("masknm", [128, 1024], BF16)
    inp("maskb0", [128, 1024], BF16)
    inp("lmask", [64, T], BF16)
    inp("ident", [128, 128], BF16)
    inp("identf", [128, 128], F32)
    inp("sel16", [16, EE * 128], BF16)
    inp("sel8", [8, 4 * 128], BF16)

    y_out = nc.dram_tensor("y", [T, D], F32, kind="ExternalOutput")
    x2scr = nc.dram_tensor("x2scr", [T, D], F32)
    agin = nc.dram_tensor("agin", [D, 8], F32)
    agout = nc.dram_tensor("agout", [N_CORES, D, 8], F32)
    scrA = nc.dram_tensor("scrA", [65, NH, 2, 256], BF16)
    scrB = nc.dram_tensor("scrB", [65, LHN, SUB], BF16)
    scrC = nc.dram_tensor("scrC", [2, 8, 2, SUB], BF16)

    KDBG = bool(os.environ.get("KDBG"))
    dbg = {}
    if KDBG:
        for nm, shape, dt in (
            ("d_q", [128, 8, SUB], BF16), ("d_k", [128, 8, HTOK], BF16),
            ("d_v", [128, 6, NH * 65], BF16), ("d_gq", [128, 4, SUB], BF16),
            ("d_pool", [128, 8, NLAT], F32), ("d_gk", [128, 4, NLAT], BF16),
            ("d_gvp", [64, LHN * 65], BF16),
            ("d_pvfm", [128, 8, SUB], BF16), ("d_denl", [16, SUB], BF16),
            ("d_remstk", [128, 4, SUB], BF16), ("d_denr", [8, SUB], BF16),
            ("d_gll", [16, SUB], F32), ("d_glr", [16, SUB], F32),
            ("d_mixed", [128, 8, SUB], BF16),
            ("d_x2", [T, D], F32),
            ("d_x2pre", [128, D], F32),
            ("d_stats", [8, 128, 1], F32),
            ("d_w", [128, NQT, EE], F32),
            ("d_x2fm", [128, 8, T], BF16),
            ("d_acc", [128, NQT, D], F32),
        ):
            dbg[nm] = nc.dram_tensor(nm, list(shape), dt, kind="ExternalOutput")

    stack = ExitStack()
    sems = {}
    for s in ("t", "v", "s", "g", "y", "cc"):
        sems[s] = stack.enter_context(nc.semaphore("sem_" + s))

    def wt(eng, sem, v=None):
        if v is None:
            v = plan.val(sem)
        if v <= 0:
            return
        lw = plan.lastwait[eng]
        if lw.get(sem, -1) >= v:
            return
        lw[sem] = v
        plan.ops[eng].append(lambda e, S, s_=sem, vv=v: e.wait_ge(S[s_], vv))

    def op(eng, fn, inc=None, n=1):
        if inc is None:
            plan.ops[eng].append(lambda e, S, f=fn: f(e, S))
        else:
            plan.bump(inc, n)
            plan.ops[eng].append(lambda e, S, f=fn, s_=inc, k=n: f(e, S).then_inc(S[s_], k))

    def dma(queue, out_ap, in_ap):
        op(queue, lambda e, S, o=out_ap, i=in_ap: e.dma_start(out=o, in_=i),
           inc=PSEM[queue], n=16)

    pbig = stack.enter_context(nc.psum_tensor("pbig", [128, 8, 512], F32))
    pbh = [pbig[:, i, :] for i in range(8)]

    class Banks:
        def __init__(self):
            self.freemark = [None] * 8
            self.rr = 0

        def acquire(self):
            b = self.rr % 8
            self.rr += 1
            fm = self.freemark[b]
            if fm is not None:
                for eng_s, val in (fm if isinstance(fm, list) else [fm]):
                    wt("tensor", eng_s, val)
            return b

        def align4(self):
            if self.rr % 4 != 0:
                self.rr += 4 - (self.rr % 4)

        def release(self, b, eng):
            self.freemark[b] = (PSEM[eng], plan.val(PSEM[eng]))

    banks = Banks()

    sb = {}
    def sbuf_on(st, name, shape, dt):
        sb[name] = st.enter_context(nc.sbuf_tensor("s_" + name, list(shape), dt))
        return sb[name]

    for name, shape, dt in (
        ("masknm", [128, 1024], BF16), ("maskb0", [128, 1024], BF16),
        ("lmask", [64, T], BF16), ("ident", [128, 128], BF16),
        ("identf", [128, 128], F32),
        ("sel16", [16, EE * 128], BF16), ("sel8", [8, 4 * 128], BF16),
        ("gstat", [128, EE * 4], BF16), ("gwT", [128, 8, EE], BF16),
        ("gbrep", [128, EE], F32), ("ebsb", [8, D], BF16),
        ("pooled", [128, 8, NLAT], F32), ("pooledbf", [128, 8, NLAT], BF16),
        ("gksb", [128, 4, NLAT], BF16), ("gvp", [64, LHN * 65], BF16),
        ("junk", [128, D], BF16), ("ssum", [128, 2], F32), ("ssq", [128, 2], F32),
        ("mu", [128, 2], F32), ("msq", [128, 2], F32), ("mu2", [128, 1], F32),
        ("var", [128, 1], F32), ("lnv", [128, 1], F32), ("aa", [128, 1], F32),
        ("bb", [128, 1], F32), ("nmu", [128, 2], F32),
        ("spc0", [128, 1], F32), ("spc1", [128, 1], F32),
    ):
        sbuf_on(stack, name, shape, dt)

    A = ExitStack()
    for name, shape, dt in (
        ("xfs", [128, 2, 8, HTOK], BF16),
        ("wring", [128, 2, 8, 1024], BF16),
        ("qsb", [128, 8, SUB], BF16),
        ("ksb", [128, 8, HTOK], BF16),
        ("vsb", [128, 6, NH * 65], BF16),
        ("gqsb", [128, 4, SUB], BF16),
        ("esb", [128, 2, 2, 4, 256], BF16),
        ("ersb", [64, 2, 4, SUB], BF16),
        ("pvs", [65, 8, 2, 256], BF16),
        ("pvfm", [128, 8, SUB], BF16),
        ("denl", [16, SUB], BF16),
        ("rems", [65, LHN, SUB], BF16),
        ("remstk", [128, 4, SUB], BF16),
        ("denr", [8, SUB], BF16),
        ("glr_l", [16, SUB], F32),
        ("glr_r", [16, SUB], F32),
        ("recl", [16, SUB], F32),
        ("clbf", [16, SUB], BF16),
        ("crbf", [16, SUB], BF16),
        ("recrbf", [8, SUB], BF16),
        ("remnorm", [128, 4, SUB], BF16),
        ("crsb", [128, SUB], BF16),
        ("tmpv", [128, SUB], BF16),
        ("mixed", [128, 8, SUB], BF16),
        ("xtmb", [128, 2, D], F32),
        ("x2pre", [128, 2, D], F32),
        ("x2tm", [128, 1, D], F32),
        ("gvtmp", [128, 4, NLAT], BF16),
    ):
        sbuf_on(A, name, shape, dt)

    dma("sync", sb["masknm"][:, :], di["masknm"].ap()[:, :])
    dma("sync", sb["maskb0"][:, :], di["maskb0"].ap()[:, :])
    dma("sync", sb["lmask"][:, :], di["lmask"].ap()[:, :])
    dma("sync", sb["ident"][:, :], di["ident"].ap()[:, :])
    dma("sync", sb["identf"][:, :], di["identf"].ap()[:, :])
    dma("sync", sb["sel16"][:, :], di["sel16"].ap()[:, :])
    dma("sync", sb["sel8"][:, :], di["sel8"].ap()[:, :])
    dma("sync", sb["gbrep"][:, :], di["gbrep"].ap()[:, :])
    dma("gpsimd", sb["gstat"][:, :], di["gstat"].ap().rearrange("p e f -> p (e f)"))
    dma("gpsimd", sb["gwT"][:, :, :], di["gwT"].ap().rearrange("(c p) e -> p c e", p=128))
    dma("gpsimd", sb["ebsb"][:, :], di["eb"].ap()[:, :])

    op("vector", lambda e, S: e.memset(sb["spc0"][:, :], 0.0), inc="v")
    op("vector", lambda e, S: e.memset(sb["spc1"][:, :], 0.0), inc="v")

    def vspace():
        op("vector", lambda e, S: e.tensor_copy(sb["spc1"][:, :], sb["spc0"][:, :]), inc="v")

    for j in range(NSUB):
        if j >= 1:
            wt("gpsimd", "v")
        dma("gpsimd", sb["xfs"][:, 1, :, 0:SUB],
            di["xfm"].ap().rearrange("(c p) t -> p c t", p=128)[:, :, HALO + j * SUB: HALO + (j + 1) * SUB])
        gval = plan.val("g")
        wt("vector", "g", gval)
        for c in range(8):
            op("vector",
               lambda e, S, cc=c, j_=j: e.reduce_sum(
                   sb["pooled"][:, cc, j_ * 2:(j_ + 1) * 2],
                   sb["xfs"][:, 1, cc, 0:SUB].rearrange("p (s q) -> p s q", q=256),
                   AXX),
               inc="v")
    vspace()
    op("vector", lambda e, S: e.tensor_scalar(
        sb["pooled"][:, :, 0:8],
        sb["pooled"][:, :, 0:8],
        1.0 / 256.0, None, AX.mult), inc="v")
    pool_reads_done = plan.val("v")
    wt("sync", "v")
    dma("sync", agin.ap().rearrange("(c p) n -> p c n", p=128), sb["pooled"][:, :, 0:8])
    wt("gpsimd", "y")
    op("gpsimd", lambda e, S: e.collective_compute(
        "AllGather", AX.bypass,
        replica_groups=[list(range(N_CORES))],
        ins=[agin.ap().opt()],
        outs=[agout.ap().opt()]), inc="cc")
    wt("sync", "cc")
    for r_ in range(N_CORES):
        dma("sync", sb["pooled"][:, :, 8 * r_: 8 * r_ + 8],
            agout.ap()[r_, :, :].rearrange("(c p) n -> p c n", p=128))
    agload_done = plan.val("y")

    wring_lastuse = [0, 0]
    wring_next = 0

    def load_wring(name, chunks, width):
        nonlocal wring_next
        slot = wring_next % 2
        wring_next += 1
        if wring_lastuse[slot] > 0:
            wt("gpsimd", "t", wring_lastuse[slot])
        dma("gpsimd", sb["wring"][:, slot, 0:chunks, 0:width],
            di[name].ap().rearrange("(c p) o -> p c o", p=128))
        return slot, plan.val("g")

    xfs_lastuse = [None, None]
    xtmb_lastuse = [None, None]
    x2tm_lastuse = [None, None]
    marks = {}
    dump_marks = {}
    load_marks = {}
    spill_marks = []
    wslots = {}

    def load_xfs(s):
        slot = s % 2
        if s == 0:
            pass
        if slot == 1:
            wt("gpsimd", "v", pool_reads_done)
        if xfs_lastuse[slot] is not None:
            wt("gpsimd", "t", xfs_lastuse[slot])
        dma("gpsimd", sb["xfs"][:, slot, :, :],
            di["xfm"].ap().rearrange("(c p) t -> p c t", p=128)[:, :, s * SUB: s * SUB + HTOK])
        return plan.val("g")

    evac_rr = [0]

    def evac_copy(b, out_ap_fn):
        tv = plan.val("t")
        eng = "vector" if (evac_rr[0] % 2 == 0) else "scalar"
        evac_rr[0] += 1
        wt(eng, "t", tv)
        if eng == "vector":
            op(eng, lambda e, S, b_=b, f=out_ap_fn: e.tensor_copy(*f(b_)), inc="v")
        else:
            op(eng, lambda e, S, b_=b, f=out_ap_fn: e.copy(*f(b_)), inc="s")
        banks.release(b, eng)

    def emit_proj(s, xready, wq_pre=None, wk_pre=None):
        slot = s % 2

        def proj_fm(dstn, wslot, ocs, own_only, nlist):
            for oc in ocs:
                for (a, c) in nlist:
                    b = banks.acquire()
                    for kc in range(8):
                        op("tensor", lambda e, S, b_=b, kc_=kc, oc_=oc, ws=wslot, sl=slot,
                           a_=a, c_=c, oo=own_only: e.matmul(
                               pbh[b_][:, 0:c_ - a_],
                               sb["wring"][:, ws, kc_, oc_ * 128: oc_ * 128 + 128],
                               (sb["xfs"][:, sl, kc_, HALO:HTOK] if oo
                                else sb["xfs"][:, sl, kc_, :])[:, a_:c_],
                               start=(kc_ == 0), stop=(kc_ == 7)),
                           inc=("t" if kc == 7 else None))
                    evac_copy(b, lambda b_, oc_=oc, a_=a, c_=c, d=dstn:
                              (sb[d][:, oc_, a_:c_], pbh[b_][:, 0:c_ - a_]))

        wq_slot, wq_rdy = wq_pre if wq_pre is not None else load_wring("wq", 8, 1024)
        wk_slot, wk_rdy = wk_pre if wk_pre is not None else load_wring("wk", 8, 1024)
        wt("tensor", "g", max(xready, wq_rdy))
        proj_fm("qsb", wq_slot, range(8), True, [(0, 512)])
        wring_lastuse[wq_slot] = plan.val("t")
        wv_slot, wv_rdy = load_wring("wv", 8, 1024)
        wt("tensor", "g", wk_rdy)
        proj_fm("ksb", wk_slot, range(8), False, [(0, 512), (512, 768)])
        wring_lastuse[wk_slot] = plan.val("t")
        wgq_slot, wgq_rdy = load_wring("wgq", 8, 512)
        wt("tensor", "g", wv_rdy)
        for tc in range(6):
            for half in range(2):
                b = banks.acquire()
                for kc in range(8):
                    op("tensor", lambda e, S, b_=b, kc_=kc, tc_=tc, h_=half, sl=slot,
                       ws=wv_slot: e.matmul(
                        pbh[b_][:, 0:512],
                        sb["xfs"][:, sl, kc_, tc_ * 128: tc_ * 128 + 128],
                        sb["wring"][:, ws, kc_, h_ * 512: h_ * 512 + 512],
                        start=(kc_ == 0), stop=(kc_ == 7)),
                       inc=("t" if kc == 7 else None))
                evac_copy(b, lambda b_, tc_=tc, h_=half:
                          (sb["vsb"].ap().rearrange("p t (h c) -> p t h c", c=65)[:, tc_, h_ * 8:h_ * 8 + 8, 0:64],
                           pbh[b_][:, 0:512].rearrange("p (h d) -> p h d", d=64)))
        wring_lastuse[wv_slot] = plan.val("t")
        if s == 0:
            op("vector", lambda e, S: e.memset(
                sb["vsb"].ap().rearrange("p t (h c) -> p t h c", c=65)[:, :, :, 64:65], 1.0), inc="v")
        wt("tensor", "g", wgq_rdy)
        proj_fm("gqsb", wgq_slot, range(4), True, [(0, 512)])
        wring_lastuse[wgq_slot] = plan.val("t")
        xfs_lastuse[slot] = plan.val("t")
        marks[("pev", s)] = (plan.val("v"), plan.val("s"))
        if s == 0:
            wslots["wgk"] = load_wring("wgk", 8, 512)
            wslots["wgv"] = load_wring("wgv", 8, 512)

        if KDBG and s == 0:
            wt("sync", "v")
            dma("sync", dbg["d_q"].ap()[:, :, :], sb["qsb"][:, :, :])
            dma("sync", dbg["d_k"].ap()[:, :, :], sb["ksb"][:, :, :])
            dma("sync", dbg["d_v"].ap()[:, :, :], sb["vsb"][:, :, :])
            dma("sync", dbg["d_gq"].ap()[:, :, :], sb["gqsb"][:, :, :])

    def emit_gkgv():
        ws_gk = wslots["wgk"]
        ws_gv = wslots["wgv"]
        wt("vector", "y", agload_done)
        op("vector", lambda e, S: e.tensor_copy(
            sb["pooledbf"].ap().rearrange("p c n -> p (c n)"),
            sb["pooled"].ap().rearrange("p c n -> p (c n)")), inc="v")
        wt("tensor", "g", ws_gv[1])
        wt("tensor", "v")
        for dstn, wslot in (("gksb", ws_gk[0]), ("gvtmp", ws_gv[0])):
            for oc in range(4):
                b = banks.acquire()
                for kc in range(8):
                    op("tensor", lambda e, S, b_=b, kc_=kc, oc_=oc, ws=wslot: e.matmul(
                        pbh[b_][:, 0:NLAT],
                        sb["wring"][:, ws, kc_, oc_ * 128: oc_ * 128 + 128],
                        sb["pooledbf"][:, kc_, :],
                        start=(kc_ == 0), stop=(kc_ == 7)),
                       inc=("t" if kc == 7 else None))
                tv = plan.val("t")
                wt("vector", "t", tv)
                op("vector", lambda e, S, b_=b, oc_=oc, d=dstn: e.tensor_copy(
                    sb[d][:, oc_, :], pbh[b_][:, 0:NLAT]), inc="v")
                banks.release(b, "vector")
        wring_lastuse[wslots["wgk"][0]] = plan.val("t")
        wring_lastuse[wslots["wgv"][0]] = plan.val("t")
        op("vector", lambda e, S: e.memset(
            sb["gvp"].ap().rearrange("n (h c) -> n h c", c=65)[:, :, 64:65], 1.0), inc="v")
        for pt in range(4):
            b = banks.acquire()
            wt("tensor", "v")
            op("tensor", lambda e, S, b_=b, pt_=pt: e.transpose(
                pbig.bitcast(BF16)[0:64, b_, 0:128], sb["gvtmp"][:, pt_, :], sb["ident"][:, :]), inc="t")
            tv = plan.val("t")
            wt("vector", "t", tv)
            op("vector", lambda e, S, b_=b, pt_=pt: e.tensor_copy(
                sb["gvp"].ap().rearrange("n (h c) -> n h c", c=65)[0:64, 2 * pt_:2 * pt_ + 2, 0:64],
                pbig.bitcast(BF16)[0:64, b_, 0:128].rearrange("n (h d) -> n h d", d=64)), inc="v")
            banks.release(b, "vector")
        if KDBG:
            wt("sync", "v")
            dma("sync", dbg["d_pool"].ap()[:, :, :], sb["pooled"][:, :, :])
            dma("sync", dbg["d_gk"].ap()[:, :, :], sb["gksb"][:, :, :])
            dma("sync", dbg["d_gvp"].ap()[:, :], sb["gvp"][:, :])

    def emit_remote(s):
        if (s - 1, "C") in dump_marks:
            wt("vector", "y", dump_marks[(s - 1, "C")])
        wt("tensor", "v", marks[("pev", s)][0])
        wt("tensor", "s", marks[("pev", s)][1])
        if (s - 1, "C") in dump_marks:
            wt("vector", "y", dump_marks[(s - 1, "C")])
            wt("scalar", "y", dump_marks[(s - 1, "C")])
        mask_g = {}

        def rem_pv(lh):
            bt, i4 = divmod(lh, 4)
            wt("tensor", "v", mask_g[bt])
            b2 = banks.acquire()
            op("tensor", lambda e, S, b_=b2, lh_=lh, bt_=bt, i_=i4: e.matmul(
                pbh[b_][0:65, 0:512],
                sb["gvp"][0:64, lh_ * 65: lh_ * 65 + 65],
                sb["ersb"][0:64, bt_, i_, :],
                start=True, stop=True), inc="t")
            evac_copy(b2, lambda b_, lh_=lh: (sb["rems"][:, lh_, :], pbh[b_][0:65, 0:512]))

        def lsc_batch(bt):
            banks.align4()
            bs = [banks.acquire() for _ in range(4)]
            for i, lh in enumerate(range(4 * bt, 4 * bt + 4)):
                base = 64 * (lh % 2)
                op("tensor", lambda e, S, b_=bs[i], lh_=lh, ba=base: e.matmul(
                    pbh[b_][0:64, 0:512],
                    sb["gksb"][ba:ba + 64, lh_ // 2, :],
                    sb["gqsb"][ba:ba + 64, lh_ // 2, :],
                    start=True, stop=True), inc=("t" if i == 3 else None))
            tv = plan.val("t")
            wt("scalar", "t", tv)
            op("scalar", lambda e, S, b0=bs[0], bt_=bt: e.activation(
                sb["ersb"][0:64, bt_, :, :],
                pbig[0:64, b0:b0 + 4, :],
                AF.Exp, scale=0.125), inc="s")
            for b_ in bs:
                banks.release(b_, "scalar")
            sv = plan.val("s")
            wt("vector", "s", sv)
            for i in range(4):
                op("vector", lambda e, S, bt_=bt, i_=i, si=s: e.tensor_tensor(
                    sb["ersb"][0:64, bt_, i_, :], sb["ersb"][0:64, bt_, i_, :],
                    sb["lmask"][0:64, si * SUB:(si + 1) * SUB], AX.mult), inc="v")
            mask_g[bt] = plan.val("v")

        lsc_batch(0)
        lsc_batch(1)
        for lh in range(4):
            rem_pv(lh)
        for lh in range(4, 8):
            rem_pv(lh)
        if (s - 1,) in load_marks:
            wt("sync", "y", load_marks[(s - 1,)])
        wt("sync", "v", plan.val("v"))
        wt("sync", "s", plan.val("s"))
        dma("sync", scrB.ap()[:, :, :], sb["rems"][:, :, :])
        dump_marks[(s, "B")] = plan.val("y")

    def emit_local(s):
        wt("tensor", "v", marks[("pev", s)][0])
        wt("tensor", "s", marks[("pev", s)][1])
        if (s - 1, "A") in dump_marks:
            wt("vector", "y", dump_marks[(s - 1, "A")])
            wt("scalar", "y", dump_marks[(s - 1, "A")])
        for b8 in range(8):
            fm = banks.freemark[b8]
            if fm is not None:
                for eng_s, val in (fm if isinstance(fm, list) else [fm]):
                    wt("tensor", eng_s, val)
        NU = 16
        exp_mark = {}
        mask_mark = {}
        pvev_mark = {}

        def emit_scores(u):
            hp, blk = divmod(u, 2)
            p = u % 2
            if u - 2 in exp_mark:
                wt("tensor", "s", exp_mark[u - 2])
            tiles = [(hi, kc) for hi in range(2) for kc in range(4)
                     if hi * 2 + kc // 2 != 0] + [(0, 0), (0, 1)]
            n_mm = 0
            for hi, kc in tiles:
                n_mm += 1
                if n_mm == 7 and u - 2 in pvev_mark:
                    wt("tensor", pvev_mark[u - 2][0], pvev_mark[u - 2][1])
                h = hp * 2 + hi
                base = 64 * (h % 2)
                bank = 4 * p + hi * 2 + kc // 2
                c0 = (kc % 2) * 256
                kt = 2 * blk + kc
                op("tensor", lambda e, S, bk=bank, cc=c0, h_=h, ba=base, kt_=kt, bl=blk: e.matmul(
                    pbig[:, bk, cc:cc + 256],
                    sb["ksb"][ba:ba + 64, h_ // 2, kt_ * 128: kt_ * 128 + 128],
                    sb["qsb"][ba:ba + 64, h_ // 2, bl * 256: bl * 256 + 256],
                    start=True, stop=True),
                   inc=("t" if n_mm == 8 else None))
            tv = plan.val("t")
            es = u % 2
            wt("scalar", "t", tv)
            op("scalar", lambda e, S, p_=p, es_=es: e.activation(
                sb["esb"].ap().rearrange("x r h k q -> x r h (k q)")[:, es_, :, :],
                pbig[:, 4 * p_:4 * p_ + 4, :],
                AF.Exp, scale=0.125), inc="s")
            exp_mark[u] = plan.val("s")
            sv = plan.val("s")
            wt("vector", "s", sv)
            mk = "maskb0" if (s == 0 and blk == 0) else "masknm"
            for hi in range(2):
                op("vector", lambda e, S, es_=es, hi_=hi, m=mk: e.tensor_tensor(
                    sb["esb"].ap().rearrange("x r h k q -> x r h (k q)")[:, es_, hi_, :],
                    sb["esb"].ap().rearrange("x r h k q -> x r h (k q)")[:, es_, hi_, :],
                    sb[m][:, :], AX.mult), inc="v")
            mask_mark[u] = plan.val("v")

        def emit_pv(u):
            hp, blk = divmod(u, 2)
            p = u % 2
            es = u % 2
            bank = 4 * p
            wt("tensor", "v", mask_mark[u])
            for hi in range(2):
                h = hp * 2 + hi
                c0 = hi * 256
                for kc in range(4):
                    kt = 2 * blk + kc
                    op("tensor", lambda e, S, bk=bank, cc=c0, h_=h, kt_=kt, es_=es, hi_=hi, kc_=kc: e.matmul(
                        pbig[0:65, bk, cc:cc + 256],
                        sb["vsb"][:, kt_, h_ * 65: h_ * 65 + 65],
                        sb["esb"][:, es_, hi_, kc_, :],
                        start=(kc_ == 0), stop=(kc_ == 3)),
                       inc=("t" if (hi == 1 and kc == 3) else None))
            tv = plan.val("t")
            eng = "vector" if (u % 2 == 0) else "scalar"
            wt(eng, "t", tv)
            if u in (8, 9):
                wt(eng, "y", dump_marks[(s, "A0")])
            if eng == "vector":
                op(eng, lambda e, S, bk=bank, hp_=hp, bl=blk: e.tensor_copy(
                    sb["pvs"][:, (2 * hp_) % 8: (2 * hp_) % 8 + 2, bl, :],
                    pbig[0:65, bk, :].rearrange("d (h q) -> d h q", q=256)), inc="v")
            else:
                op(eng, lambda e, S, bk=bank, hp_=hp, bl=blk: e.copy(
                    sb["pvs"][:, (2 * hp_) % 8: (2 * hp_) % 8 + 2, bl, :],
                    pbig[0:65, bk, :].rearrange("d (h q) -> d h q", q=256)), inc="s")
            pvev_mark[u] = (PSEM[eng], plan.val(PSEM[eng]))

        for u in range(NU):
            emit_scores(u)
            if u >= 1:
                emit_pv(u - 1)
            if u == 8:
                wt("sync", "v", plan.val("v"))
                wt("sync", "s", plan.val("s"))
                dma("sync", scrA.ap()[:, 0:8, :, :], sb["pvs"][:, :, :, :])
                dump_marks[(s, "A0")] = plan.val("y")
        emit_pv(NU - 1)
        vve = plan.val("v")
        sve = plan.val("s")
        for b8 in range(8):
            banks.freemark[b8] = [("v", vve), ("s", sve)]

        wt("sync", "v", vve)
        wt("sync", "s", sve)
        dma("sync", scrA.ap()[:, 8:16, :, :], sb["pvs"][:, :, :, :])
        dump_marks[(s, "A")] = plan.val("y")

    def emit_glog(s):
        wt("vector", "y", dump_marks[(s, "B")])
        for pt in range(8):
            b = banks.acquire()
            op("tensor", lambda e, S, b_=b, pt_=pt: e.matmul(
                pbh[b_][0:4, 0:512],
                sb["gstat"][:, pt_ * 4: pt_ * 4 + 4],
                sb["qsb"][:, pt_, :],
                start=True, stop=True), inc="t")
            tv = plan.val("t")
            wt("vector", "t", tv)
            op("vector", lambda e, S, b_=b, pt_=pt: e.tensor_copy(
                sb["rems"][0:4, pt_, :], pbh[b_][0:4, 0:512]), inc="v")
            banks.release(b, "vector")
        wt("sync", "v", plan.val("v"))
        for k in range(2):
            for hi in range(2):
                dma("sync",
                    scrC.ap()[k:k + 1, :, hi, :],
                    sb["rems"][2 * k + hi: 2 * k + hi + 1, 0:8, :])
        dump_marks[(s, "C")] = plan.val("y")

    def emit_gather_loads(s):
        wt("sync", "y", dump_marks[(s, "C")])
        if ("cmb", s - 1) in marks:
            wt("sync", "v", marks[("cmb", s - 1)])
        for hi in range(2):
            dma("sync",
                sb["pvfm"][64 * hi: 64 * hi + 64, :, :],
                scrA.ap().rearrange("d (pt hi) b q -> d pt hi (b q)", hi=2)[0:64, :, hi, :])
        dma("sync",
            sb["denl"][:, :],
            scrA.ap().rearrange("d h b q -> d h (b q)")[64, :, :])
        for hi in range(2):
            dma("sync",
                sb["remstk"][64 * hi: 64 * hi + 64, :, :],
                scrB.ap().rearrange("d (pt hi) q -> d pt hi q", hi=2)[0:64, :, hi, :])
        dma("sync",
            sb["denr"][:, :],
            scrB.ap()[64, :, :])
        wt("gpsimd", "y", dump_marks[(s, "C")])
        dma("gpsimd",
            sb["glr_l"][:, :],
            scrC.ap()[0, :, :, :].rearrange("pt hi q -> (pt hi) q"))
        dma("gpsimd",
            sb["glr_r"][:, :],
            scrC.ap()[1, :, :, :].rearrange("pt hi q -> (pt hi) q"))
        glgr_rdy = plan.val("g")
        gathered = plan.val("y")
        load_marks[(s,)] = gathered

        if KDBG and s == 0:
            wt("sync", "y", gathered)
            dma("sync", dbg["d_pvfm"].ap()[:, :, :], sb["pvfm"][:, :, :])
            dma("sync", dbg["d_denl"].ap()[:, :], sb["denl"][:, :])
            dma("sync", dbg["d_remstk"].ap()[:, :, :], sb["remstk"][:, :, :])
            dma("sync", dbg["d_denr"].ap()[:, :], sb["denr"][:, :])
        return glgr_rdy, gathered

    def emit_coeffs(s, glgr_rdy, gathered):
        wt("scalar", "g", glgr_rdy)
        wt("scalar", "y", gathered)
        for gn in ("glr_l", "glr_r"):
            op("scalar", lambda e, S, g_=gn: e.activation(sb[g_][:, :], sb[g_][:, :],
                                                          AF.Exp, scale=-1.0), inc="s")
        sv = plan.val("s")
        wt("vector", "s", sv)
        for gn in ("glr_l", "glr_r"):
            op("vector", lambda e, S, g_=gn: e.tensor_scalar(sb[g_][:, :], sb[g_][:, :],
                                                             1.0, None, AX.add), inc="v")
            op("vector", lambda e, S, g_=gn: e.reciprocal(sb[g_][:, :], sb[g_][:, :]), inc="v")
        wt("vector", "y", gathered)
        op("vector", lambda e, S: e.reciprocal(sb["recl"][:, :], sb["denl"][:, :]), inc="v")
        op("vector", lambda e, S: e.tensor_tensor(sb["clbf"][:, :], sb["glr_l"][:, :],
                                                  sb["recl"][:, :], AX.mult), inc="v")
        op("vector", lambda e, S: e.tensor_copy(sb["crbf"][:, :], sb["glr_r"][:, :]), inc="v")
        if KDBG and s == 0:
            wt("sync", "v")
            dma("sync", dbg["d_gll"].ap()[:, :], sb["glr_l"][:, :])
            dma("sync", dbg["d_glr"].ap()[:, :], sb["glr_r"][:, :])
        op("vector", lambda e, S: e.reciprocal(sb["recl"][0:8, :], sb["denr"][:, :]), inc="v")
        op("vector", lambda e, S: e.tensor_copy(sb["recrbf"][:, :], sb["recl"][0:8, :]), inc="v")
        marks[("coef", s)] = plan.val("v")

    def emit_combine(s):
        rout_slot, rout_rdy = wslots[("rout", s)]
        wt("tensor", "v", marks[("coef", s)])
        for pt in range(4):
            b = banks.acquire()
            op("tensor", lambda e, S, b_=b, pt_=pt: e.matmul(
                pbh[b_][:, 0:512],
                sb["sel8"][:, pt_ * 128: pt_ * 128 + 128],
                sb["recrbf"][:, :],
                start=True, stop=True), inc="t")
            tv = plan.val("t")
            wt("vector", "t", tv)
            op("vector", lambda e, S, b_=b, pt_=pt: e.tensor_tensor(
                sb["remnorm"][:, pt_, :], sb["remstk"][:, pt_, :],
                pbh[b_][:, 0:512], AX.mult), inc="v")
            banks.release(b, "vector")
        wt("tensor", "g", rout_rdy)
        wt("tensor", "v", plan.val("v"))
        for pt in range(8):
            bl_ = banks.acquire()
            op("tensor", lambda e, S, b_=bl_, pt_=pt: e.matmul(
                pbh[b_][:, 0:512],
                sb["sel16"][:, pt_ * 128: pt_ * 128 + 128],
                sb["clbf"][:, :],
                start=True, stop=True), inc="t")
            tvl = plan.val("t")
            bc_ = banks.acquire()
            op("tensor", lambda e, S, b_=bc_, pt_=pt: e.matmul(
                pbh[b_][:, 0:512],
                sb["sel16"][:, pt_ * 128: pt_ * 128 + 128],
                sb["crbf"][:, :],
                start=True, stop=True), inc="t")
            tvc = plan.val("t")
            br_ = banks.acquire()
            for kc in range(4):
                op("tensor", lambda e, S, b_=br_, kc_=kc, pt_=pt, rs=rout_slot: e.matmul(
                    pbh[b_][:, 0:512],
                    sb["wring"][:, rs, kc_, pt_ * 128: pt_ * 128 + 128],
                    sb["remnorm"][:, kc_, :],
                    start=(kc_ == 0), stop=(kc_ == 3)),
                   inc=("t" if kc == 3 else None))
            tvr = plan.val("t")
            wt("vector", "t", tvl)
            op("vector", lambda e, S, b_=bl_, pt_=pt: e.tensor_tensor(
                sb["mixed"][:, pt_, :], sb["pvfm"][:, pt_, :], pbh[b_][:, 0:512],
                AX.mult), inc="v")
            banks.release(bl_, "vector")
            wt("vector", "t", tvc)
            op("vector", lambda e, S, b_=bc_: e.tensor_copy(
                sb["crsb"][:, :], pbh[b_][:, 0:512]), inc="v")
            banks.release(bc_, "vector")
            wt("vector", "t", tvr)
            op("vector", lambda e, S, b_=br_: e.tensor_tensor(
                sb["tmpv"][:, :], pbh[b_][:, 0:512], sb["crsb"][:, :], AX.mult), inc="v")
            banks.release(br_, "vector")
            op("vector", lambda e, S, pt_=pt: e.tensor_tensor(
                sb["mixed"][:, pt_, :], sb["mixed"][:, pt_, :], sb["tmpv"][:, :],
                AX.add), inc="v")
        wring_lastuse[rout_slot] = plan.val("t")
        marks[("cmb", s)] = plan.val("v")
        if KDBG and s == 0:
            wt("sync", "v")
            dma("sync", dbg["d_mixed"].ap()[:, :, :], sb["mixed"][:, :, :])

    def emit_woutln1(s):
        wout_slot, wout_rdy = wslots[("wout", s)]
        wt("tensor", "g", wout_rdy)
        wt("tensor", "v", marks[("cmb", s)])
        pend_stats = []

        def stats_start(qt_g, par):
            vv = plan.val("v")
            wt("scalar", "v", vv)
            op("scalar", lambda e, S, p_=par: e.activation(
                sb["junk"][:, :], sb["x2pre"][:, p_, :], AF.Copy,
                accum_out=sb["ssum"][:, p_:p_ + 1]), inc="s")
            op("scalar", lambda e, S, p_=par: e.activation(
                sb["junk"][:, :], sb["x2pre"][:, p_, :], AF.Square,
                accum_out=sb["ssq"][:, p_:p_ + 1]), inc="s")
            op("scalar", lambda e, S, p_=par: e.mul(sb["nmu"][:, p_:p_ + 1], sb["ssum"][:, p_:p_ + 1], -1.0 / D), inc="s")
            op("scalar", lambda e, S, p_=par: e.mul(sb["msq"][:, p_:p_ + 1], sb["ssq"][:, p_:p_ + 1], 1.0 / D), inc="s")
            op("scalar", lambda e, S, p_=par: e.mul(sb["mu"][:, p_:p_ + 1], sb["ssum"][:, p_:p_ + 1], 1.0 / D), inc="s")
            return plan.val("s")

        def stats_finish(qt_g, par, smark):
            wt("vector", "s", smark)
            op("vector", lambda e, S, p_=par: e.scalar_tensor_tensor(
                sb["var"][:, :], sb["mu"][:, p_:p_ + 1], sb["nmu"][:, p_:p_ + 1], sb["msq"][:, p_:p_ + 1],
                AX.mult, AX.add), inc="v")
            vspace()
            op("vector", lambda e, S: e.tensor_scalar(sb["var"][:, :], sb["var"][:, :],
                                                      1e-5, None, AX.add), inc="v")
            vv = plan.val("v")
            wt("scalar", "v", vv)
            op("scalar", lambda e, S: e.activation(sb["lnv"][:, :], sb["var"][:, :],
                                                   AF.Ln), inc="s")
            op("scalar", lambda e, S: e.activation(sb["aa"][:, :], sb["lnv"][:, :],
                                                   AF.Exp, scale=-0.5), inc="s")
            sv = plan.val("s")
            wt("vector", "s", sv)
            op("vector", lambda e, S, p_=par: e.tensor_scalar(sb["bb"][:, :], sb["mu"][:, p_:p_ + 1],
                                                      sb["aa"][:, 0:1], -1.0, AX.mult, AX.mult), inc="v")
            vspace()
            if KDBG and qt_g == 0:
                wt("sync", "s")
                wt("sync", "v")
                for i_, nm_ in enumerate(("ssum", "ssq", "mu", "msq", "mu2", "var", "lnv", "aa")):
                    dma("sync", dbg["d_stats"].ap()[i_, :, :], sb[nm_][:, 0:1])
            if x2tm_lastuse[0] is not None:
                wt("vector", "y", x2tm_lastuse[0])
            op("vector", lambda e, S, p_=par: e.tensor_scalar(
                sb["x2tm"][:, 0, :], sb["x2pre"][:, p_, :],
                sb["aa"][:, 0:1], sb["bb"][:, 0:1], AX.mult, AX.add), inc="v")
            wt("sync", "v")
            dma("sync", x2scr.ap()[qt_g * 128:(qt_g + 1) * 128, :], sb["x2tm"][:, 0, :])
            x2tm_lastuse[0] = plan.val("y")
            spill_marks.append(plan.val("y"))

        for qt in range(QPS):
            qt_g = s * QPS + qt
            par = qt_g % 2
            tslot = qt_g % 2
            if xtmb_lastuse[tslot] is not None:
                wt("sync", "v", xtmb_lastuse[tslot])
            dma("sync", sb["xtmb"][:, tslot, :], di["xtm"].ap()[qt_g * 128:(qt_g + 1) * 128, :])
            xtrdy = plan.val("y")

            bA = banks.acquire()
            bB = banks.acquire()
            for kc in range(8):
                for half, b in ((0, bA), (1, bB)):
                    op("tensor", lambda e, S, b_=b, kc_=kc, qt_=qt, h_=half, ws=wout_slot: e.matmul(
                        pbh[b_][:, 0:512],
                        sb["mixed"][:, kc_, qt_ * 128: qt_ * 128 + 128],
                        sb["wring"][:, ws, kc_, h_ * 512: h_ * 512 + 512],
                        start=(kc_ == 0), stop=(kc_ == 7)),
                       inc=("t" if (kc == 7 and half == 1) else None))
            tv = plan.val("t")
            wt("vector", "t", tv)
            wt("vector", "y", xtrdy)
            op("vector", lambda e, S, b_=bA, ts=tslot, p_=par: e.tensor_tensor(
                sb["x2pre"][:, p_, 0:512], sb["xtmb"][:, ts, 0:512], pbh[b_][:, 0:512], AX.add), inc="v")
            op("vector", lambda e, S, b_=bB, ts=tslot, p_=par: e.tensor_tensor(
                sb["x2pre"][:, p_, 512:1024], sb["xtmb"][:, ts, 512:1024], pbh[b_][:, 0:512], AX.add), inc="v")
            banks.release(bA, "vector")
            banks.release(bB, "vector")
            xtmb_lastuse[tslot] = plan.val("v")
            if KDBG and qt_g == 0:
                wt("sync", "v")
                dma("sync", dbg["d_x2pre"].ap()[:, :], sb["x2pre"][:, 0, :])
            if pend_stats:
                pq, pp, pm = pend_stats.pop(0)
                stats_finish(pq, pp, pm)
            sm = stats_start(qt_g, par)
            pend_stats.append((qt_g, par, sm))
        while pend_stats:
            pq, pp, pm = pend_stats.pop(0)
            stats_finish(pq, pp, pm)
        wring_lastuse[wout_slot] = plan.val("t")

    xrdy0 = load_xfs(0)
    emit_proj(0, xrdy0)
    emit_gkgv()
    for s in range(NSUB):
        emit_remote(s)
        emit_local(s)
        emit_glog(s)
        if s + 1 < NSUB:
            xrdy = load_xfs(s + 1)
            wq_pre = load_wring("wq", 8, 1024)
            wk_pre = load_wring("wk", 8, 1024)
        glgr_rdy, gathered = emit_gather_loads(s)
        emit_coeffs(s, glgr_rdy, gathered)
        if s + 1 < NSUB:
            emit_proj(s + 1, xrdy, wq_pre, wk_pre)
        wslots[("rout", s)] = load_wring("rout", 4, 1024)
        wslots[("wout", s)] = load_wring("wout", 8, 1024)
        emit_combine(s)
        emit_woutln1(s)

    spills_done = plan.val("y")
    if KDBG:
        wt("sync", "y", spills_done)
        dma("sync", dbg["d_x2"].ap()[:, :], x2scr.ap()[:, :])

    marks = {s: plan.val(s) for s in ("t", "v", "s", "g", "y")}
    for eng in ENGS:
        for s_, v_ in marks.items():
            wt(eng, s_, v_)

    A.close()

    Bst = ExitStack()
    for name, shape, dt in (
        ("x2fm", [128, 8, T], BF16),
        ("xbf", [128, 2, D], BF16),
        ("wE", [128, 2, 8, D], BF16),
        ("acc", [128, NQT, D], F32),
        ("gfm", [8, T], F32),
        ("uu", [128, EE], F32),
        ("usum", [128, 1], F32),
        ("t1", [128, EE], F32),
        ("g2", [128, EE], F32),
        ("m1", [128, 1], F32),
        ("m2", [128, 1], F32),
        ("wtm", [128, NQT, EE], F32),
        ("wfm", [8, T], BF16),
        ("xt2b", [128, 2, D], F32),
        ("outb", [128, 2, D], F32),
        ("x3pre", [128, D], F32),
    ):
        sbuf_on(Bst, name, shape, dt)

    xbf_lastuse = [None, None]
    for qt in range(NQT):
        slot = qt % 2
        if xbf_lastuse[slot] is not None:
            wt("gpsimd", "t", xbf_lastuse[slot])
        dma("gpsimd", sb["xbf"][:, slot, :], x2scr.ap()[qt * 128:(qt + 1) * 128, :])
        xrdy = plan.val("g")
        wt("tensor", "g", xrdy)
        for fc in range(8):
            b = banks.acquire()
            op("tensor", lambda e, S, b_=b, sl=slot, fc_=fc: e.transpose(
                pbig.bitcast(BF16)[:, b_, 0:128], sb["xbf"][:, sl, fc_ * 128: fc_ * 128 + 128],
                sb["ident"][:, :]), inc="t")
            tv = plan.val("t")
            wt("vector", "t", tv)
            op("vector", lambda e, S, b_=b, fc_=fc, qt_=qt: e.tensor_copy(
                sb["x2fm"][:, fc_, qt_ * 128:(qt_ + 1) * 128], pbig.bitcast(BF16)[:, b_, 0:128]), inc="v")
            banks.release(b, "vector")
        xbf_lastuse[slot] = plan.val("t")

    wt("tensor", "v")
    for nck in range(4):
        b = banks.acquire()
        for kc in range(8):
            op("tensor", lambda e, S, b_=b, kc_=kc, n_=nck: e.matmul(
                pbh[b_][0:8, 0:512],
                sb["gwT"][:, kc_, :],
                sb["x2fm"][:, kc_, n_ * 512:(n_ + 1) * 512],
                start=(kc_ == 0), stop=(kc_ == 7)), inc=("t" if kc == 7 else None))
        tv = plan.val("t")
        wt("vector", "t", tv)
        op("vector", lambda e, S, b_=b, n_=nck: e.tensor_copy(
            sb["gfm"][:, n_ * 512:(n_ + 1) * 512], pbh[b_][0:8, 0:512]), inc="v")
        banks.release(b, "vector")
    for qt in range(NQT):
        b = banks.acquire()
        wt("tensor", "v")
        op("tensor", lambda e, S, b_=b, qt_=qt: e.transpose(
            pbh[b_][:, 0:8], sb["gfm"][0:8, qt_ * 128:(qt_ + 1) * 128],
            sb["identf"][0:8, 0:8]), inc="t")
        tv = plan.val("t")
        wt("vector", "t", tv)
        op("vector", lambda e, S, b_=b: e.tensor_tensor(
            sb["uu"][:, :], pbh[b_][:, 0:8], sb["gbrep"][:, :], AX.add), inc="v")
        banks.release(b, "vector")
        vv = plan.val("v")
        wt("scalar", "v", vv)
        op("scalar", lambda e, S: e.activation(sb["uu"][:, :], sb["uu"][:, :],
                                               AF.Exp, accum_out=sb["usum"][:, :]), inc="s")
        sv = plan.val("s")
        wt("vector", "s", sv)
        op("vector", lambda e, S: e.reduce_max(sb["m1"][:, :], sb["uu"][:, :], AXX), inc="v")
        vspace()
        op("vector", lambda e, S: e.tensor_scalar(sb["t1"][:, :], sb["uu"][:, :],
                                                  sb["m1"][:, 0:1], None, AX.is_ge), inc="v")
        vspace()
        op("vector", lambda e, S: e.scalar_tensor_tensor(
            sb["g2"][:, :], sb["t1"][:, :], -1e30, sb["uu"][:, :], AX.mult, AX.add), inc="v")
        vspace()
        op("vector", lambda e, S: e.reduce_max(sb["m2"][:, :], sb["g2"][:, :], AXX), inc="v")
        vspace()
        op("vector", lambda e, S: e.tensor_scalar(sb["t1"][:, :], sb["uu"][:, :],
                                                  sb["m2"][:, 0:1], None, AX.is_ge), inc="v")
        vspace()
        op("vector", lambda e, S: e.tensor_tensor(sb["g2"][:, :], sb["uu"][:, :],
                                                  sb["t1"][:, :], AX.mult), inc="v")
        op("vector", lambda e, S: e.reciprocal(sb["m1"][:, :], sb["usum"][:, :]), inc="v")
        vspace()
        op("vector", lambda e, S, qt_=qt: e.tensor_scalar(
            sb["wtm"][:, qt_, :], sb["g2"][:, :], sb["m1"][:, 0:1], None, AX.mult), inc="v")
        b2 = banks.acquire()
        wt("tensor", "v")
        op("tensor", lambda e, S, b_=b2, qt_=qt: e.transpose(
            pbh[b_][0:8, 0:128], sb["wtm"][:, qt_, :], sb["identf"][:, :]), inc="t")
        tv = plan.val("t")
        wt("vector", "t", tv)
        op("vector", lambda e, S, b_=b2, qt_=qt: e.tensor_copy(
            sb["wfm"][:, qt_ * 128:(qt_ + 1) * 128], pbh[b_][0:8, 0:128]), inc="v")
        banks.release(b2, "vector")

    if KDBG:
        wt("sync", "v")
        dma("sync", dbg["d_w"].ap()[:, :, :], sb["wtm"][:, :, :])
        dma("sync", dbg["d_x2fm"].ap()[:, :, :], sb["x2fm"][:, :, :])

    wt("tensor", "v")
    for qt in range(NQT):
        bA = banks.acquire()
        bB = banks.acquire()
        for half, b in ((0, bA), (1, bB)):
            op("tensor", lambda e, S, b_=b, qt_=qt, h_=half: e.matmul(
                pbh[b_][:, 0:512],
                sb["wfm"][:, qt_ * 128:(qt_ + 1) * 128],
                sb["ebsb"][:, h_ * 512:(h_ + 1) * 512],
                start=True, stop=True), inc="t")
        tv = plan.val("t")
        wt("vector", "t", tv)
        for half, b in ((0, bA), (1, bB)):
            op("vector", lambda e, S, b_=b, qt_=qt, h_=half: e.tensor_copy(
                sb["acc"][:, qt_, h_ * 512:(h_ + 1) * 512], pbh[b_][:, 0:512]), inc="v")
        banks.release(bA, "vector")
        banks.release(bB, "vector")

    wE_lastuse = [None, None]
    wE_rdy = [0, 0]
    xt2b_lastuse = [None, None]
    outb_lastuse = [None, None]

    def emit_ln2(qt):
        slot = qt % 2
        if xt2b_lastuse[slot] is not None:
            wt("sync", "v", xt2b_lastuse[slot])
        dma("sync", sb["xt2b"][:, slot, :], x2scr.ap()[qt * 128:(qt + 1) * 128, :])
        xrdy = plan.val("y")
        wt("vector", "y", xrdy)
        op("vector", lambda e, S, qt_=qt, sl=slot: e.tensor_tensor(
            sb["x3pre"][:, :], sb["acc"][:, qt_, :], sb["xt2b"][:, sl, :], AX.add), inc="v")
        xt2b_lastuse[slot] = plan.val("v")
        op("vector", lambda e, S: e.reduce_sum(sb["ssum"][:, 0:1], sb["x3pre"][:, :], AXX), inc="v")
        vv = plan.val("v")
        wt("scalar", "v", vv)
        op("scalar", lambda e, S: e.activation(sb["junk"][:, :], sb["x3pre"][:, :],
                                               AF.Square, accum_out=sb["ssq"][:, 0:1]), inc="s")
        sv = plan.val("s")
        wt("vector", "s", sv)
        op("vector", lambda e, S: e.tensor_scalar(sb["nmu"][:, 0:1], sb["ssum"][:, 0:1],
                                                  -1.0 / D, None, AX.mult), inc="v")
        op("vector", lambda e, S: e.tensor_scalar(sb["msq"][:, 0:1], sb["ssq"][:, 0:1],
                                                  1.0 / D, None, AX.mult), inc="v")
        op("vector", lambda e, S: e.tensor_scalar(sb["mu"][:, 0:1], sb["ssum"][:, 0:1],
                                                  1.0 / D, None, AX.mult), inc="v")
        vspace()
        op("vector", lambda e, S: e.scalar_tensor_tensor(
            sb["var"][:, :], sb["mu"][:, 0:1], sb["nmu"][:, 0:1], sb["msq"][:, 0:1],
            AX.mult, AX.add), inc="v")
        vspace()
        op("vector", lambda e, S: e.tensor_scalar(sb["var"][:, :], sb["var"][:, :],
                                                  1e-5, None, AX.add), inc="v")
        vv = plan.val("v")
        wt("scalar", "v", vv)
        op("scalar", lambda e, S: e.activation(sb["lnv"][:, :], sb["var"][:, :],
                                               AF.Ln), inc="s")
        op("scalar", lambda e, S: e.activation(sb["aa"][:, :], sb["lnv"][:, :],
                                               AF.Exp, scale=-0.5), inc="s")
        sv = plan.val("s")
        wt("vector", "s", sv)
        op("vector", lambda e, S: e.tensor_scalar(sb["bb"][:, :], sb["mu"][:, 0:1],
                                                  sb["aa"][:, 0:1], -1.0, AX.mult, AX.mult), inc="v")
        vspace()
        if outb_lastuse[slot] is not None:
            wt("vector", "y", outb_lastuse[slot])
        op("vector", lambda e, S, sl=slot: e.tensor_scalar(
            sb["outb"][:, sl, :], sb["x3pre"][:, :],
            sb["aa"][:, 0:1], sb["bb"][:, 0:1], AX.mult, AX.add), inc="v")
        wt("sync", "v")
        dma("sync", y_out.ap()[qt * 128:(qt + 1) * 128, :], sb["outb"][:, slot, :])
        outb_lastuse[slot] = plan.val("y")

    for e_i in range(2):
        slot = e_i % 2
        dma("gpsimd", sb["wE"][:, slot, :, :],
            di["ewT"].ap()[e_i:e_i + 1, :, :].rearrange("z (c p) o -> p (z c) o", p=128))
        wE_rdy[slot] = plan.val("g")
    for e_i in range(EE):
        slot = e_i % 2
        wt("tensor", "g", wE_rdy[slot])
        for qt in range(NQT):
            bA = banks.acquire()
            bB = banks.acquire()
            for kc in range(8):
                for half, b in ((0, bA), (1, bB)):
                    op("tensor", lambda e, S, b_=b, kc_=kc, qt_=qt, sl=slot, h_=half: e.matmul(
                        pbh[b_][:, 0:512],
                        sb["x2fm"][:, kc_, qt_ * 128:(qt_ + 1) * 128],
                        sb["wE"][:, sl, kc_, h_ * 512:(h_ + 1) * 512],
                        start=(kc_ == 0), stop=(kc_ == 7)),
                       inc=("t" if (kc == 7 and half == 1) else None))
            tv = plan.val("t")
            wt("vector", "t", tv)
            for half, b in ((0, bA), (1, bB)):
                op("vector", lambda e, S, b_=b, qt_=qt, h_=half, ei=e_i: e.scalar_tensor_tensor(
                    sb["acc"][:, qt_, h_ * 512:(h_ + 1) * 512],
                    pbh[b_][:, 0:512],
                    sb["wtm"][:, qt_, ei:ei + 1],
                    sb["acc"][:, qt_, h_ * 512:(h_ + 1) * 512],
                    AX.mult, AX.add), inc="v")
            banks.release(bA, "vector")
            banks.release(bB, "vector")
            if e_i == EE - 1:
                if KDBG:
                    wt("sync", "v")
                    dma("sync", dbg["d_acc"].ap()[:, qt:qt + 1, :], sb["acc"][:, qt:qt + 1, :])
                if qt >= 1:
                    emit_ln2(qt - 1)
                if qt == NQT - 1:
                    emit_ln2(qt)
        wE_lastuse[slot] = plan.val("t")
        nxt = e_i + 2
        if nxt < EE:
            wt("gpsimd", "t", wE_lastuse[slot])
            dma("gpsimd", sb["wE"][:, slot, :, :],
                di["ewT"].ap()[nxt:nxt + 1, :, :].rearrange("z (c p) o -> p (z c) o", p=128))
            wE_rdy[slot] = plan.val("g")

    wt("sync", "y", plan.val("y"))

    with nc.Block() as block:
        @block.tensor
        def _(e):
            for fn in plan.ops["tensor"]:
                fn(e, sems)

        @block.vector
        def _(e):
            for fn in plan.ops["vector"]:
                fn(e, sems)

        @block.scalar
        def _(e):
            for fn in plan.ops["scalar"]:
                fn(e, sems)

        @block.gpsimd
        def _(e):
            for fn in plan.ops["gpsimd"]:
                fn(e, sems)

        @block.sync
        def _(e):
            for fn in plan.ops["sync"]:
                fn(e, sems)

    Bst.close()
    stack.close()
    return nc


def _host_masks():
    kk = np.arange(512)[:, None]
    qq = np.arange(256)[None, :]
    base = (kk > qq) & (kk <= qq + 256)
    masknm = base.astype(np.float32)
    maskb0 = (base & (kk >= 256)).astype(np.float32)

    def lay(m):
        return np.ascontiguousarray(
            m.reshape(4, 128, 256).transpose(1, 0, 2).reshape(128, 1024)
        ).astype(ml_dtypes.bfloat16)

    return lay(masknm), lay(maskb0)


def kernel(L0_attn_x, L0_attn_wq, L0_attn_wk, L0_attn_wv, L0_attn_wgq, L0_attn_wgk,
           L0_attn_wgv, L0_attn_r_out_w, L0_attn_gate_local_w, L0_attn_gate_remote_w,
           L0_attn_w_out, L0_ffn_gate_w, L0_ffn_gate_b, L0_ffn_expert_w, L0_ffn_expert_b):
    x = np.ascontiguousarray(np.asarray(L0_attn_x, dtype=np.float32)[0])
    xT = np.ascontiguousarray(x.T)

    f32 = lambda a: np.ascontiguousarray(np.asarray(a, dtype=np.float32))
    wq, wk, wv = f32(L0_attn_wq), f32(L0_attn_wk), f32(L0_attn_wv)
    wgq, wgk, wgv = f32(L0_attn_wgq), f32(L0_attn_wgk), f32(L0_attn_wgv)
    rout, wout = f32(L0_attn_r_out_w), f32(L0_attn_w_out)
    glw, grw = f32(L0_attn_gate_local_w), f32(L0_attn_gate_remote_w)
    gw, gb = f32(L0_ffn_gate_w), f32(L0_ffn_gate_b)
    ew, eb = f32(L0_ffn_expert_w), f32(L0_ffn_expert_b)

    ewT = np.ascontiguousarray(ew.transpose(0, 2, 1))
    gwT = np.ascontiguousarray(gw.T)
    gbrep = np.ascontiguousarray(np.broadcast_to(gb[None, :], (128, EE)))

    gstat = np.zeros((128, 8, 4), np.float32)
    for pt in range(8):
        gstat[0:64, pt, 0] = glw[2 * pt]
        gstat[64:128, pt, 1] = glw[2 * pt + 1]
        gstat[0:64, pt, 2] = grw[2 * pt]
        gstat[64:128, pt, 3] = grw[2 * pt + 1]

    masknm, maskb0 = _host_masks()
    ident = np.eye(128, dtype=np.float32).astype(ml_dtypes.bfloat16)
    identf = np.eye(128, dtype=np.float32)
    sel16 = np.zeros((16, EE, 128), np.float32)
    for pt in range(8):
        for p in range(128):
            sel16[2 * pt + (p >= 64), pt, p] = 1.0
    sel16 = np.ascontiguousarray(sel16.reshape(16, EE * 128)).astype(ml_dtypes.bfloat16)
    sel8 = np.zeros((8, 4, 128), np.float32)
    for pt in range(4):
        for p in range(128):
            sel8[2 * pt + (p >= 64), pt, p] = 1.0
    sel8 = np.ascontiguousarray(sel8.reshape(8, 4 * 128)).astype(ml_dtypes.bfloat16)

    nc = build_nc()

    in_maps = []
    for c in range(N_CORES):
        lo = c * T
        if c == 0:
            xfm = np.concatenate([np.zeros((D, HALO), np.float32), xT[:, 0:T]], axis=1)
        else:
            xfm = xT[:, lo - HALO: lo + T]
        xtm = x[lo: lo + T]
        gtok = np.arange(lo, lo + T)
        lm = (np.arange(NLAT)[:, None] <= (gtok[None, :] // CHK)).astype(np.float32)
        in_maps.append({
            "xfm": np.ascontiguousarray(xfm),
            "xtm": np.ascontiguousarray(xtm),
            "wq": wq, "wk": wk, "wv": wv,
            "wgq": wgq, "wgk": wgk, "wgv": wgv,
            "rout": rout, "wout": wout,
            "ewT": ewT, "eb": eb, "gwT": gwT, "gbrep": gbrep,
            "gstat": gstat,
            "masknm": masknm, "maskb0": (maskb0 if c == 0 else masknm),
            "lmask": np.ascontiguousarray(lm.astype(ml_dtypes.bfloat16)),
            "ident": ident, "identf": identf, "sel16": sel16, "sel8": sel8,
        })

    res = run_bass_kernel_spmd(nc, in_maps, core_ids=list(range(N_CORES)))
    kernel.last_results = res.results
    out = np.concatenate([res.results[c]["y"] for c in range(N_CORES)], axis=0)
    return out.reshape(1, SEQ, D)
